# version 07
# baseline (speedup 1.0000x reference)
import contextlib
import numpy as np
import ml_dtypes
import concourse.bass as bass
import concourse.mybir as mybir
from concourse.bass_utils import run_bass_kernel_spmd

F32 = mybir.dt.float32
BF16 = mybir.dt.bfloat16
AF = mybir.ActivationFunctionType
ALU = mybir.AluOpType
AX = mybir.AxisListType

D = 1024
H = 16
NCH = 32
NLEV = 6
HID = 4096
NE = 8
ENGS = ("sync", "gpsimd", "tensor", "vector", "scalar")


class Ch:
    def __init__(self, stack, nc, name, step=1):
        self.sem = stack.enter_context(nc.semaphore(name))
        self.step = step
        self.n = 0


class Bank:
    def __init__(self, stack, nc, name, dtype, view=None):
        if view is None:
            cols = 1024 if dtype == BF16 else 512
            self.t = stack.enter_context(nc.psum_tensor(name, [128, cols], dtype))
        else:
            self.t = view
        self.full = Ch(stack, nc, name + "f")
        self.free = Ch(stack, nc, name + "e")
        self.k = 0
        self.state = "idle"

    def produce_waits(self):
        assert self.state == "idle", self.state
        self.state = "prod"
        return [(self.free, self.k)]

    def produced(self):
        assert self.state == "prod"
        self.state = "full"
        return self.full

    def consume_waits(self):
        assert self.state == "full"
        self.state = "cons"
        return [(self.full, self.k + 1)]

    def released(self):
        assert self.state == "cons"
        self.state = "idle"
        self.k += 1
        return self.free


class Plan:
    def __init__(self, stack, nc):
        self.nc = nc
        self.ops = {e: [] for e in ENGS}
        self.lastinc = {e: None for e in ENGS}
        self.lastsync = {e: None for e in ENGS}
        self.chained = ("vector", "scalar")
        self.chD = Ch(stack, nc, "chD", 16)
        self.chW = Ch(stack, nc, "chW", 16)
        self.chP = Ch(stack, nc, "chP")
        self.chV = Ch(stack, nc, "chV")
        self.chS = Ch(stack, nc, "chS")
        self.chDm = Ch(stack, nc, "chDm")
        self.chWm = Ch(stack, nc, "chWm")
        self.engch = {"sync": self.chDm, "gpsimd": self.chWm, "tensor": self.chP,
                      "vector": self.chV, "scalar": self.chS}
        self.bk = {}
        self.praw = [stack.enter_context(
            nc.psum_tensor(f"pp{i}", [128, 1024], F32)) for i in range(3)]
        for i, nm in enumerate(("A", "B", "C", "D", "E", "F")):
            v = self.praw[i // 2][:, (i % 2) * 512:(i % 2 + 1) * 512]
            self.bk[nm] = Bank(stack, nc, "pb" + nm, F32, view=v)
        for i, nm in enumerate(("AB", "CD", "EF")):
            self.bk[nm] = Bank(stack, nc, "pp" + nm, F32,
                               view=self.praw[i][:, :])
        for nm in ("T0", "T1"):
            self.bk[nm] = Bank(stack, nc, "pb" + nm, BF16)

    def emit(self, eng, build, waits=(), inc=None):
        incs = []
        if inc is not None:
            for ch in (inc if isinstance(inc, (list, tuple)) else [inc]):
                if ch is None:
                    continue
                ch.n += 1
                incs.append(ch)
        assert len(incs) <= 1, "one semaphore update per instruction"
        wl = [(ch, int(n)) for (ch, n) in waits if ch is not None and n > 0]
        if eng in self.chained:
            if not incs:
                ech = self.engch[eng]
                ech.n += 1
                incs.append(ech)
            if self.lastsync[eng] is not None:
                wl.append(self.lastsync[eng])
            self.lastsync[eng] = (incs[0], incs[0].n)

        def fn(e, build=build, wl=wl, incs=incs):
            for ch, n in wl:
                e.wait_ge(ch.sem, ch.step * n)
            ins = build(e)
            for ch in incs:
                ins.then_inc(ch.sem, ch.step)
            return ins
        self.ops[eng].append(fn)
        self.lastinc[eng] = (incs[0], incs[0].n) if incs else None

    def mark(self, eng):
        assert eng in ("tensor", "vector", "scalar")
        ch = self.engch[eng]
        ch.n += 1
        tgt = ch.n
        if self.lastinc.get(eng):
            ch.n -= 1
            return self.lastinc[eng]
        if self.ops[eng]:
            last = self.ops[eng].pop()

            def fn(e, last=last, ch=ch):
                ins = last(e)
                ins.then_inc(ch.sem, ch.step)
                return ins
        else:
            def fn(e, ch=ch):
                ins = e.nop()
                ins.then_inc(ch.sem, ch.step)
                return ins
        self.ops[eng].append(fn)
        self.lastinc[eng] = (ch, tgt)
        return (ch, tgt)

    def barrier(self):
        marks = [self.mark(e) for e in ("tensor", "vector", "scalar")]
        marks += [(self.chD, self.chD.n), (self.chW, self.chW.n)]
        for e in ENGS:
            def fn(en, marks=list(marks)):
                ins = None
                for ch, n in marks:
                    if n > 0:
                        ins = en.wait_ge(ch.sem, ch.step * n)
                return ins
            self.ops[e].append(fn)
            self.lastinc[e] = None

    def barrier3(self):
        es = ("tensor", "vector", "scalar")
        marks = [self.mark(e) for e in es]
        for e in es:
            def fn(en, marks=list(marks)):
                ins = None
                for ch, n in marks:
                    ins = en.wait_ge(ch.sem, ch.step * n)
                return ins
            self.ops[e].append(fn)
            self.lastinc[e] = None

    def realize(self, block):
        for e in ENGS:
            lst = self.ops[e]

            def run(engine, lst=lst):
                for fn in lst:
                    fn(engine)
            getattr(block, e)(run)


class Arena:
    def __init__(self, stack, nc, cols):
        self.base = stack.enter_context(nc.sbuf_tensor("arena", [128, cols], BF16))
        self.cols = cols
        self.off = 0
        self.marks = []

    def bf(self, n):
        self.off = (self.off + 15) // 16 * 16
        assert self.off + n <= self.cols, f"arena bf overflow {self.off}+{n}"
        v = self.base[:, self.off:self.off + n]
        self.off += n
        return v

    def f32(self, n):
        return self.bf(2 * n).bitcast(F32)

    def push(self):
        self.marks.append(self.off)

    def pop(self):
        self.off = self.marks.pop()


def bc(view, parts, hcount, rep=64, h0=0):
    return bass.AP(view.tensor, view.offset + h0,
                   [[view.ap[0][0], parts], [1, hcount], [0, rep]])


def make_consts():
    ident = np.eye(128, dtype=np.float32)
    triu_incl = np.triu(np.ones((128, 128), np.float32), 0)
    stril = np.tril(np.ones((128, 128), np.float32), -1)
    cols = np.zeros((128, 16), np.float32)
    cols[:64, 0] = 1.0
    cols[:, 1] = 1.0
    cvals = np.zeros((128, 16), np.float32)
    cvals[:, 0] = 1e-5
    cvals[:, 1] = 1.0
    cst = np.concatenate([ident, triu_incl, np.tile(stril, (1, 4)),
                          np.tile(triu_incl, (1, 4)), np.tile(ident, (1, 4)),
                          cols, cvals], axis=1)
    return np.ascontiguousarray(cst, np.float32), np.eye(128).astype(ml_dtypes.bfloat16)


def build_program(n_chunks=NCH, stop_after="D", taps=(), tap_chunk=0, blocs=(0, 1), b_sub="full", dbg_spill=False):
    nc = bass.Bass("TRN2", target_bir_lowering=False)
    stack = contextlib.ExitStack()
    taps = set(taps)

    x_d = nc.dram_tensor("x", [2, 2048, D], F32, kind="ExternalInput")
    w_in = {}
    for nm, shp in (("Wq", [D, D]), ("Wk", [D, D]), ("Wv", [D, D]),
                    ("Wb", [D, H]), ("Wg", [D, H]), ("Wo", [D, D]),
                    ("bo", [D]), ("ln1g", [D]), ("ln1b", [D]), ("ln2g", [D]),
                    ("ln2b", [D]), ("w1", [D, HID]), ("b1", [HID]),
                    ("w2", [HID, D]), ("b2", [D]), ("Wr", [NE, D]),
                    ("W1e", [NE, D, HID]), ("W2e", [NE, HID, D])):
        w_in[nm] = nc.dram_tensor(nm, shp, F32, kind="ExternalInput")
    cstf_d = nc.dram_tensor("cstf", [128, 1824], F32, kind="ExternalInput")
    cstb_d = nc.dram_tensor("cstb", [128, 128], BF16, kind="ExternalInput")
    out_d = nc.dram_tensor("out", [2, 2048, D], F32, kind="ExternalOutput")
    spk = "ExternalOutput" if dbg_spill else "Internal"
    spk_in = "ExternalInput" if dbg_spill == "input" else spk
    x2_sp = nc.dram_tensor("x2_sp", [NCH, 128, D], F32, kind=spk_in)
    x1T_sp = nc.dram_tensor("x1T_sp", [NCH, 8, 128, 128], BF16, kind=spk_in)
    y_sp = nc.dram_tensor("y_sp", [NCH, 128, D], F32,
                          kind=("ExternalOutput" if dbg_spill else "Internal"))
    yT_sp = nc.dram_tensor("yT_sp", [NCH, 8, 128, 128], BF16,
                           kind=("ExternalOutput" if dbg_spill else "Internal"))

    tap_d = {}
    TAPS_BF = {"qt": [128, 2048], "tall": [128, 4096], "attnT": [128, 4096],
               "khatT": [128, 4096], "wt": [128, 4096], "u0b": [128, 2048],
               "zp": [128, 2048], "khat": [128, D], "vtm": [128, D],
               "x1T": [128, D], "yT": [128, D], "hh": [128, 4096], "x1gd": [128, 2048], "attnout": [128, D]}
    TAPS_F = {"beta": [128, H], "L": [128, H], "gamma": [128, H],
              "x2": [128, D], "y": [128, D],
              "cwt": [128, 8 * NCH], "ktmf": [128, D], "S0": [128, 2048], "logg": [128, H], "Lt": [128, H], "rows": [1, 48], "gelub": [128, 32], "ypre0": [128, D], "accq": [128, 8192], "rsm": [128, 16], "zrow": [128, 16],
              "ypre": [128, D]}
    for t in taps:
        if t in TAPS_BF:
            tap_d[t] = nc.dram_tensor("tap_" + t, TAPS_BF[t], BF16,
                                      kind="ExternalOutput")
        else:
            tap_d[t] = nc.dram_tensor("tap_" + t, TAPS_F[t], F32,
                                      kind="ExternalOutput")

    ar = Arena(stack, nc, 103424)
    p = Plan(stack, nc)
    BK = p.bk
    chX = [Ch(stack, nc, f"chX{i}", 16) for i in range(2)]
    chSp = Ch(stack, nc, "chSp", 16)
    chCx1 = [Ch(stack, nc, f"chCx1{i}", 16) for i in range(2)]
    chCx2 = [Ch(stack, nc, f"chCx2{i}", 16) for i in range(2)]
    chCxc = [Ch(stack, nc, f"chCxc{i}", 16) for i in range(2)]
    chYT = [Ch(stack, nc, f"chYT{i}", 16) for i in range(4)]
    chYs = [Ch(stack, nc, f"chYs{i}", 16) for i in range(2)]
    chWp = [Ch(stack, nc, f"chWp{i}", 16) for i in range(2)]
    bA, bB, bC, bD_, bE, bF_ = (BK[k] for k in "ABCDEF")

    def tap(name, view, waits):
        if name in tap_d:
            pr, fr = view.shape
            p.emit("sync", lambda e, v=view, n=name, pr=pr, fr=fr: e.dma_start(
                tap_d[n][0:pr, 0:fr], v), waits=waits, inc=p.chD)

    cstf = ar.f32(1824)
    cstb = ar.bf(128)
    IDF = cstf[:, 0:128]
    TRIU = cstf[:, 128:256]
    STRIL4 = cstf[:, 256:768]
    TRIU4 = cstf[:, 768:1280]
    IDENT4 = cstf[:, 1280:1792]
    MASKCOL = cstf[:, 1792:1793]
    ONESCOL = cstf[:, 1793:1794]
    ONESROW = TRIU[0:1, 0:128]
    IDB = cstb[:, 0:128]
    nc.const_aps.register(F32, 1e-5, cstf[:, 1808:1809])
    nc.const_aps.register(F32, 1.0, cstf[:, 1809:1810])
    b2bc = ar.f32(D)
    wrT = ar.bf(64)
    cw_all = ar.f32(8 * NCH)
    g2c = ar.f32(8)
    b2lcb = ar.bf(8)
    ar.push()

    wq_sb = ar.bf(8192)
    wk_sb = ar.bf(8192)
    wv_sb = ar.bf(8192)
    wo_sb = ar.bf(8192)
    wb_sb = ar.bf(128)
    wg_sb = ar.bf(128)
    g1c = ar.f32(8)
    b1lcb = ar.bf(8)
    bqc = ar.f32(8)
    bkbc = ar.f32(D)
    bvbc = ar.f32(D)
    bbbc = ar.f32(H)
    bgbc = ar.f32(H)
    bobc = ar.f32(D)
    rowst = ar.f32(576)
    S0 = ar.f32(2048)
    xt = [ar.f32(D) for _ in range(2)]
    stat = ar.f32(8)
    xnb = ar.bf(D)
    x1T = ar.bf(D)
    qt_b = [ar.bf(2048) for _ in range(2)]
    khat_b = [ar.bf(D) for _ in range(2)]
    vtm_b = [ar.bf(D) for _ in range(2)]
    zbf = ar.f32(H)
    zgf = ar.f32(H)
    beta_s = [ar.f32(H) for _ in range(2)]
    zmin = ar.f32(H)
    zab = ar.f32(H)
    logg_s = ar.f32(H)
    L_s = ar.f32(H)
    Lt_s = ar.f32(H)
    gam = [ar.f32(H) for _ in range(2)]
    vbsc = [ar.f32(H) for _ in range(2)]
    eLm = [ar.f32(H) for _ in range(2)]
    gCh = [ar.f32(H) for _ in range(2)]
    gC = [ar.f32(H) for _ in range(2)]
    knss = ar.f32(H)
    knsc = ar.f32(H)
    nsb = ar.bf(1024)
    sqa = ar.bf(1024)
    fTp = [ar.bf(1024) for _ in range(NLEV + 1)]
    tall = ar.bf(4096)
    attnT = ar.bf(4096)
    khatT = ar.bf(4096)
    wT = ar.bf(4096)
    ar.push()
    vecst = ar.f32(512)
    wr_raw = ar.f32(1024)
    rows2 = ar.f32(2048)
    ar.pop()
    u0b = ar.bf(2048)
    zp = ar.bf(2048)
    stmp = ar.f32(1024)
    stmpw = stmp
    abf = ar.bf(D)
    aT = ar.bf(D)
    x2t = ar.f32(D)
    x1t_sb = x1T
    xsq = zp.bitcast(F32)
    ktm_f = x2t
    import sys
    print("ARENA scan off:", ar.off, "of", ar.cols, file=sys.stderr)
    assert ar.off <= ar.cols, ar.off

    p.emit("sync", lambda e: e.dma_start(cstf, cstf_d[:, :]), inc=p.chD)
    p.emit("sync", lambda e: e.dma_start(cstb, cstb_d[:, :]), inc=p.chD)
    for w_sb, nm, ncol in ((wq_sb, "Wq", D), (wk_sb, "Wk", D), (wv_sb, "Wv", D),
                           (wo_sb, "Wo", D), (wb_sb, "Wb", H), (wg_sb, "Wg", H)):
        if ncol == D:
            src = w_in[nm].rearrange("(kb p) (nb t) -> p kb nb t", p=128, t=128)
            dst = w_sb.rearrange("p (kb nb t) -> p kb nb t", kb=8, t=128)
        else:
            src = w_in[nm].rearrange("(kb p) t -> p kb t", p=128)
            dst = w_sb.rearrange("p (kb t) -> p kb t", kb=8)
        p.emit("gpsimd", lambda e, d=dst, s=src: e.dma_start(d, s), inc=p.chW)
    nW_w = p.chW.n
    for i, nm in enumerate(("ln1g", "ln1b", "ln2g", "ln2b")):
        p.emit("sync", lambda e, i=i, nm=nm: e.dma_start(
            vecst[0:8, i * 128:(i + 1) * 128],
            w_in[nm].rearrange("(r t) -> r t", r=8)), inc=p.chD)
    p.emit("sync", lambda e: e.dma_start(
        rows2[0:1, 0:D], w_in["bo"].rearrange("(r f) -> r f", r=1)), inc=p.chD)
    p.emit("sync", lambda e: e.dma_start(
        rows2[0:1, D:2 * D], w_in["b2"].rearrange("(r f) -> r f", r=1)),
        inc=p.chD)
    p.emit("sync", lambda e: e.dma_start(wr_raw[0:8, 0:512],
                                         w_in["Wr"][:, 0:512]), inc=p.chD)
    p.emit("sync", lambda e: e.dma_start(wr_raw[0:8, 512:1024],
                                         w_in["Wr"][:, 512:1024]), inc=p.chD)
    nD_A = p.chD.n

    w0 = bA.produce_waits() + [(p.chD, nD_A), (p.chW, nW_w)]
    for i in range(4):
        p.emit("tensor", lambda e, i=i: e.matmul(
            bA.t[:, i * 8:(i + 1) * 8], vecst[0:8, i * 128:(i + 1) * 128],
            IDF[0:8, 0:8], start=True, stop=True),
            waits=(w0 if i == 0 else ()), inc=(bA.produced() if i == 3 else None))
    wA = bA.consume_waits()
    p.emit("vector", lambda e: e.tensor_copy(g1c[:, 0:8], bA.t[:, 0:8]), waits=wA)
    p.emit("vector", lambda e: e.tensor_copy(b1lcb[:, 0:8], bA.t[:, 8:16]))
    p.emit("vector", lambda e: e.tensor_copy(g2c[:, 0:8], bA.t[:, 16:24]))
    p.emit("vector", lambda e: e.tensor_copy(b2lcb[:, 0:8], bA.t[:, 24:32]),
           inc=bA.released())
    w0 = bB.produce_waits()
    for kb in range(8):
        rsl = wr_raw[0:8, kb * 128:(kb + 1) * 128]
        p.emit("tensor", lambda e, kb=kb, rsl=rsl: e.matmul(
            bB.t[:, kb * 8:(kb + 1) * 8], rsl, IDF[0:8, 0:8],
            start=True, stop=True),
            waits=(w0 if kb == 0 else ()), inc=(bB.produced() if kb == 7 else None))
    wB = bB.consume_waits()
    p.emit("vector", lambda e: e.tensor_copy(wrT[:, 0:64], bB.t[:, 0:64]),
           waits=wB, inc=bB.released())
    mVA = p.mark("vector")

    w0 = bC.produce_waits() + [mVA]
    for nb in range(8):
        for kb in range(8):
            p.emit("tensor", lambda e, nb=nb, kb=kb: e.matmul(
                bC.t[:, nb:nb + 1],
                wq_sb[:, kb * 1024 + nb * 128:kb * 1024 + (nb + 1) * 128],
                b1lcb[:, kb:kb + 1], start=(kb == 0), stop=(kb == 7)),
                waits=(w0 if nb == 0 and kb == 0 else ()),
                inc=(bC.produced() if (nb == 7 and kb == 7) else None))
    wC = bC.consume_waits()
    p.emit("vector", lambda e: e.tensor_scalar(bqc[:, 0:8], bC.t[:, 0:8], 0.125,
                                               None, ALU.mult), waits=wC,
           inc=bC.released())

    def row_then_bcast(wsb, dstb, width, ncolw):
        nhalf = (width + 511) // 512
        for half in range(nhalf):
            w = min(512, width - half * 512)
            w0 = bD_.produce_waits()
            for kb in range(8):
                p.emit("tensor", lambda e, kb=kb, half=half, w=w, wsb=wsb,
                       ncolw=ncolw: e.matmul(
                    bD_.t[0:1, 0:w], b1lcb[:, kb:kb + 1],
                    wsb[:, kb * ncolw + half * 512:kb * ncolw + half * 512 + w],
                    start=(kb == 0), stop=(kb == 7)),
                    waits=(w0 if kb == 0 else ()),
                    inc=(bD_.produced() if kb == 7 else None))
            wD = bD_.consume_waits()
            p.emit("vector", lambda e, w=w: e.tensor_copy(
                rowst[0:1, 0:w], bD_.t[0:1, 0:w]), waits=wD,
                inc=bD_.released())
            mr = p.mark("vector")
            w0 = bE.produce_waits() + [mr]
            p.emit("tensor", lambda e, w=w: e.matmul(
                bE.t[:, 0:w], ONESROW, rowst[0:1, 0:w],
                start=True, stop=True), waits=w0, inc=bE.produced())
            wE = bE.consume_waits()
            p.emit("vector", lambda e, dstb=dstb, half=half, w=w: e.tensor_copy(
                dstb[:, half * 512:half * 512 + w], bE.t[:, 0:w]), waits=wE,
                inc=bE.released())

    row_then_bcast(wk_sb, bkbc, D, D)
    row_then_bcast(wv_sb, bvbc, D, D)
    row_then_bcast(wb_sb, bbbc, H, H)
    row_then_bcast(wg_sb, bgbc, H, H)
    for (rw, dstb) in ((rows2[0:1, 0:D], bobc), (rows2[0:1, D:2 * D], b2bc)):
        for half in range(2):
            row = rw[0:1, half * 512:(half + 1) * 512]
            w0 = bE.produce_waits()
            p.emit("tensor", lambda e, row=row: e.matmul(
                bE.t[:, 0:512], ONESROW, row, start=True, stop=True),
                waits=w0, inc=bE.produced())
            wE = bE.consume_waits()
            p.emit("vector", lambda e, dstb=dstb, half=half: e.tensor_copy(
                dstb[:, half * 512:(half + 1) * 512], bE.t[:, 0:512]),
                waits=wE, inc=bE.released())
    for wsb, ncol in ((wq_sb, D), (wk_sb, D), (wv_sb, D), (wb_sb, H), (wg_sb, H)):
        for kb in range(8):
            p.emit("vector", lambda e, wsb=wsb, kb=kb, ncol=ncol: e.tensor_scalar(
                wsb[:, kb * ncol:(kb + 1) * ncol],
                wsb[:, kb * ncol:(kb + 1) * ncol],
                g1c[:, kb:kb + 1], None, ALU.mult))
    p.emit("vector", lambda e: e.memset(S0[0:64, :], 0.0))
    p.barrier()

    STAT = 1.0 / D

    def ln_stats_norm(src_f32, dst_bf, wpre):
        p.emit("scalar", lambda e: e.activation(xsq, src_f32, AF.Square),
               waits=wpre)
        mS = p.mark("scalar")
        p.emit("vector", lambda e: e.reduce_sum(
            stat[:, 0:1], src_f32.rearrange("p (a f) -> p a f", a=1), axis=AX.X),
            waits=wpre)
        p.emit("vector", lambda e: e.reduce_sum(
            stat[:, 1:2], xsq.rearrange("p (a f) -> p a f", a=1), axis=AX.X),
            waits=[mS])
        p.emit("vector", lambda e: e.tensor_scalar(stat[:, 2:3], stat[:, 0:1],
                                                   STAT, None, ALU.mult))
        p.emit("vector", lambda e: e.tensor_scalar(stat[:, 3:4], stat[:, 1:2],
                                                   STAT, None, ALU.mult))
        p.emit("vector", lambda e: e.tensor_tensor(stat[:, 4:5], stat[:, 2:3],
                                                   stat[:, 2:3], ALU.mult))
        p.emit("vector", lambda e: e.tensor_tensor(stat[:, 5:6], stat[:, 3:4],
                                                   stat[:, 4:5], ALU.subtract))
        mV = p.mark("vector")
        p.emit("scalar", lambda e: e.activation(stat[:, 6:7], stat[:, 5:6],
                                                AF.Ln, bias=1e-5), waits=[mV])
        p.emit("scalar", lambda e: e.activation(stat[:, 7:8], stat[:, 6:7],
                                                AF.Exp, scale=-0.5))
        mR = p.mark("scalar")
        p.emit("vector", lambda e: e.tensor_scalar(
            dst_bf, src_f32, stat[:, 2:3], stat[:, 7:8], ALU.subtract, ALU.mult),
            waits=[mR])
        return p.mark("vector")

    last_x_free = [None, None]
    x1t_spill_gate = [None]
    prev_b3_P = None
    prev_b1gate = None
    prev_gangP = None
    prev_epiP = None

    def chunk_scan(c):
        nonlocal prev_b3_P, prev_b1gate, prev_gangP, prev_epiP
        pAB, pCD, pEF = BK["AB"], BK["CD"], BK["EF"]
        stm = {}

        def b1_gen(bloc):
            nonlocal prev_b1gate
            gc = bloc * 16 + c
            slot = bloc
            xv = xt[slot]
            wx = [last_x_free[slot]] if last_x_free[slot] else []
            p.emit("sync", lambda e, gc=gc, xv=xv: e.dma_start(
                xv, x_d[gc // 16, (gc % 16) * 128:(gc % 16) * 128 + 128, :]),
                waits=wx, inc=chX[slot])
            nx = (chX[slot], chX[slot].n)
            gates = [nx] + ([prev_b1gate] if prev_b1gate else [])
            mnorm = ln_stats_norm(xv, xnb, gates)
            T0 = BK["T0"]
            w0 = T0.produce_waits() + [mnorm]
            for kb in range(8):
                p.emit("tensor", lambda e, kb=kb: e.transpose(
                    T0.t[:, kb * 128:(kb + 1) * 128],
                    xnb[:, kb * 128:(kb + 1) * 128], IDB),
                    waits=(w0 if kb == 0 else ()),
                    inc=(T0.produced() if kb == 7 else None))
            wT0 = T0.consume_waits()
            p.emit("vector", lambda e: e.tensor_copy(x1T, T0.t[:, :]),
                   waits=wT0 + ([x1t_spill_gate[0]] if x1t_spill_gate[0] else []),
                   inc=T0.released())
            mx1T = p.mark("vector")
            yield
            if c == tap_chunk and bloc == 0:
                tap("x1T", x1T, [mx1T])
            qgate = [mx1T] + ([prev_b3_P] if prev_b3_P else [])
            for nbp in range(4):
                bq_ = [pAB, pCD][nbp % 2]
                w0 = bq_.produce_waits() + (qgate if nbp < 2 else [])
                for half in range(2):
                    nb = 2 * nbp + half
                    for kb in range(8):
                        p.emit("tensor", lambda e, nb=nb, kb=kb, half=half,
                               bq_=bq_: e.matmul(
                            bq_.t[:, half * 512:half * 512 + 128],
                            wq_sb[:, kb * 1024 + nb * 128:kb * 1024 + (nb + 1) * 128],
                            x1T[:, kb * 128:(kb + 1) * 128],
                            start=(kb == 0), stop=(kb == 7)),
                            waits=(w0 if half == 0 and kb == 0 else ()),
                            inc=(bq_.produced() if (half == 1 and kb == 7)
                                 else None))
                wq_ = bq_.consume_waits()
                for half in range(2):
                    nb = 2 * nbp + half
                    p.emit("vector", lambda e, nb=nb, half=half, bq_=bq_,
                           bloc=bloc: e.tensor_scalar(
                        qt_b[bloc][0:64, (2 * nb) * 128:(2 * nb + 1) * 128],
                        bq_.t[0:64, half * 512:half * 512 + 128], 0.125,
                        bqc[0:64, nb:nb + 1], ALU.mult, ALU.add),
                        waits=(wq_ if half == 0 else ()))
                    p.emit("vector", lambda e, nb=nb, half=half, bq_=bq_,
                           bloc=bloc: e.tensor_scalar(
                        qt_b[bloc][0:64, (2 * nb + 1) * 128:(2 * nb + 2) * 128],
                        bq_.t[64:128, half * 512:half * 512 + 128], 0.125,
                        bqc[64:128, nb:nb + 1], ALU.mult, ALU.add),
                        inc=(bq_.released() if half == 1 else None))
            w0 = pAB.produce_waits()
            for half in range(2):
                for kb in range(8):
                    p.emit("tensor", lambda e, kb=kb, half=half: e.matmul(
                        pAB.t[:, half * 512:(half + 1) * 512],
                        x1T[:, kb * 128:(kb + 1) * 128],
                        wk_sb[:, kb * 1024 + half * 512:kb * 1024 + (half + 1) * 512],
                        start=(kb == 0), stop=(kb == 7)),
                        waits=(w0 if half == 0 and kb == 0 else ()),
                        inc=(pAB.produced() if (half == 1 and kb == 7) else None))
            wk_ = pAB.consume_waits()
            p.emit("vector", lambda e: e.tensor_tensor(
                ktm_f[:, 0:1024], pAB.t[:, 0:1024], bkbc[:, 0:1024], ALU.add),
                waits=wk_, inc=pAB.released())
            mkt = p.mark("vector")
            yield
            w0 = pCD.produce_waits()
            for half in range(2):
                for kb in range(8):
                    p.emit("tensor", lambda e, kb=kb, half=half: e.matmul(
                        pCD.t[:, half * 512:(half + 1) * 512],
                        x1T[:, kb * 128:(kb + 1) * 128],
                        wv_sb[:, kb * 1024 + half * 512:kb * 1024 + (half + 1) * 512],
                        start=(kb == 0), stop=(kb == 7)),
                        waits=(w0 if half == 0 and kb == 0 else ()),
                        inc=(pCD.produced() if (half == 1 and kb == 7) else None))
            wv_ = pCD.consume_waits()
            p.emit("vector", lambda e, bloc=bloc: e.tensor_tensor(
                vtm_b[bloc][:, 0:1024], pCD.t[:, 0:1024], bvbc[:, 0:1024],
                ALU.add), waits=wv_, inc=pCD.released())
            w0 = pEF.produce_waits()
            for i, wsb in enumerate((wb_sb, wg_sb)):
                for kb in range(8):
                    p.emit("tensor", lambda e, i=i, kb=kb, wsb=wsb: e.matmul(
                        pEF.t[:, i * 16:(i + 1) * 16],
                        x1T[:, kb * 128:(kb + 1) * 128],
                        wsb[:, kb * 16:(kb + 1) * 16],
                        start=(kb == 0), stop=(kb == 7)),
                        waits=(w0 if i == 0 and kb == 0 else ()),
                        inc=(pEF.produced() if (i == 1 and kb == 7) else None))
            wE = pEF.consume_waits()
            p.emit("vector", lambda e: e.tensor_tensor(zbf[:, 0:16],
                                                       pEF.t[:, 0:16],
                                                       bbbc[:, 0:16], ALU.add),
                   waits=wE)
            p.emit("vector", lambda e: e.tensor_tensor(zgf[:, 0:16],
                                                       pEF.t[:, 16:32],
                                                       bgbc[:, 0:16], ALU.add),
                   inc=pEF.released())
            mz = p.mark("vector")
            yield
            p.emit("scalar", lambda e, bloc=bloc: e.activation(
                beta_s[bloc][:, 0:16], zbf[:, 0:16], AF.Sigmoid), waits=[mz])
            p.emit("vector", lambda e: e.tensor_scalar(zmin[:, 0:16],
                                                       zgf[:, 0:16], 0.0,
                                                       None, ALU.min))
            mzv = p.mark("vector")
            yield
            p.emit("scalar", lambda e: e.activation(zab[:, 0:16], zgf[:, 0:16],
                                                    AF.Abs), waits=[mzv])
            p.emit("scalar", lambda e: e.activation(zab[:, 0:16], zab[:, 0:16],
                                                    AF.Exp, scale=-1.0))
            p.emit("scalar", lambda e: e.activation(zab[:, 0:16], zab[:, 0:16],
                                                    AF.Ln, bias=1.0))
            mzs = p.mark("scalar")
            yield
            p.emit("vector", lambda e: e.tensor_tensor(
                logg_s[:, 0:16], zmin[:, 0:16], zab[:, 0:16], ALU.subtract),
                waits=[mzs])
            mlg = p.mark("vector")
            yield
            p.emit("scalar", lambda e: e.activation(xsq, ktm_f, AF.Square),
                   waits=[mkt])
            msq = p.mark("scalar")
            yield
            p.emit("vector", lambda e: e.reduce_sum(
                knss[:, 0:16], xsq.rearrange("p (h d) -> p h d", d=64),
                axis=AX.X), waits=[msq])
            mks = p.mark("vector")
            yield
            p.emit("scalar", lambda e: e.activation(knsc[:, 0:16],
                                                    knss[:, 0:16], AF.Ln),
                   waits=[mks])
            p.emit("scalar", lambda e: e.activation(knsc[:, 0:16],
                                                    knsc[:, 0:16], AF.Exp,
                                                    scale=0.5))
            mkn = p.mark("scalar")
            yield
            p.emit("vector", lambda e: e.tensor_scalar(knsc[:, 0:16],
                                                       knsc[:, 0:16], 1e-6,
                                                       None, ALU.add),
                   waits=[mkn])
            p.emit("vector", lambda e: e.reciprocal(knsc[:, 0:16],
                                                    knsc[:, 0:16]))
            p.emit("vector", lambda e, bloc=bloc: e.tensor_tensor(
                khat_b[bloc].rearrange("p (h j) -> p h j", j=64),
                ktm_f.rearrange("p (h j) -> p h j", j=64),
                bc(knsc, 128, 16), ALU.mult))
            mkh = p.mark("vector")
            yield
            if c == tap_chunk and bloc == 0:
                tap("ktmf", ktm_f, [mkh])
                tap("khat", khat_b[0], [mkh])
                tap("vtm", vtm_b[0], [mkh])
                tap("beta", beta_s[0][:, 0:16], [mkh])
            w0 = pEF.produce_waits() + [mlg]
            p.emit("tensor", lambda e: e.matmul(pEF.t[:, 0:16], TRIU,
                                                logg_s[:, 0:16], start=True,
                                                stop=True), waits=w0)
            p.emit("tensor", lambda e: e.matmul(pEF.t[0:1, 16:32], MASKCOL,
                                                logg_s[:, 0:16], start=True,
                                                stop=True))
            p.emit("tensor", lambda e: e.matmul(pEF.t[0:1, 32:48], ONESCOL,
                                                logg_s[:, 0:16], start=True,
                                                stop=True), inc=pEF.produced())
            wF = pEF.consume_waits()
            p.emit("vector", lambda e: e.tensor_copy(L_s[:, 0:16],
                                                     pEF.t[:, 0:16]), waits=wF)
            p.emit("vector", lambda e: e.tensor_copy(rowst[0:1, 512:528],
                                                     pEF.t[0:1, 16:32]))
            p.emit("vector", lambda e: e.tensor_copy(rowst[0:1, 528:544],
                                                     pEF.t[0:1, 32:48]),
                   inc=pEF.released())
            p.emit("vector", lambda e: e.tensor_tensor(
                rowst[0:1, 544:560], rowst[0:1, 528:544], rowst[0:1, 512:528],
                ALU.subtract))
            mLr = p.mark("vector")
            yield
            w0 = pEF.produce_waits() + [mLr]
            p.emit("tensor", lambda e: e.matmul(pEF.t[:, 0:16], ONESROW,
                                                rowst[0:1, 512:528],
                                                start=True, stop=True),
                   waits=w0)
            p.emit("tensor", lambda e: e.matmul(pEF.t[0:64, 16:32],
                                                TRIU[0:1, 0:64],
                                                rowst[0:1, 512:528],
                                                start=True, stop=True))
            p.emit("tensor", lambda e: e.matmul(pEF.t[:, 32:48], ONESROW,
                                                rowst[0:1, 544:560],
                                                start=True, stop=True))
            p.emit("tensor", lambda e: e.matmul(pEF.t[0:64, 48:64],
                                                TRIU[0:1, 0:64],
                                                rowst[0:1, 528:544],
                                                start=True, stop=True),
                   inc=pEF.produced())
            wE = pEF.consume_waits()
            p.emit("vector", lambda e: e.tensor_tensor(Lt_s[:, 0:16],
                                                       L_s[:, 0:16],
                                                       pEF.t[:, 0:16],
                                                       ALU.subtract), waits=wE)
            p.emit("scalar", lambda e, bloc=bloc: e.activation(
                eLm[bloc][0:64, 0:16], pEF.t[0:64, 16:32], AF.Exp), waits=wE)
            p.emit("scalar", lambda e, bloc=bloc: e.activation(
                gCh[bloc][:, 0:16], pEF.t[:, 32:48], AF.Exp))
            p.emit("scalar", lambda e, bloc=bloc: e.activation(
                gC[bloc][0:64, 0:16], pEF.t[0:64, 48:64], AF.Exp),
                inc=pEF.released())
            mLt = p.mark("vector")
            yield
            p.emit("scalar", lambda e, bloc=bloc: e.activation(
                gam[bloc][:, 0:16], Lt_s[:, 0:16], AF.Exp), waits=[mLt])
            p.emit("scalar", lambda e: e.activation(knss[:, 0:16],
                                                    Lt_s[:, 0:16], AF.Exp,
                                                    scale=-1.0))
            mex = p.mark("scalar")
            yield
            p.emit("vector", lambda e, bloc=bloc: e.tensor_tensor(
                vbsc[bloc][:, 0:16], knss[:, 0:16], beta_s[bloc][:, 0:16],
                ALU.mult), waits=[mex])
            mvb = p.mark("vector")
            yield
            prev_b1gate = mvb
            if c == tap_chunk and bloc == 0:
                tap("L", L_s[:, 0:16], [mLt])
                tap("gamma", gam[0][:, 0:16], [mvb])

            stm[("mkh", bloc)] = mkh
            stm[("mex", bloc)] = mex
            yield

        def gangs_gen(bloc):
            nonlocal prev_gangP
            for gidx in range(2):
                hs = [8 * gidx + i for i in range(8)]
                g8 = bloc * 2 + gidx
                T0 = BK["T0"]
                w0 = T0.produce_waits() + [stm[("mkh", bloc)]]
                for i, h in enumerate(hs):
                    p.emit("tensor", lambda e, i=i, h=h, bloc=bloc: e.transpose(
                        T0.t[0:64, i * 128:(i + 1) * 128],
                        khat_b[bloc][:, h * 64:(h + 1) * 64], IDB),
                        waits=(w0 if i == 0 else ()),
                        inc=(T0.produced() if i == 7 else None))
                wT0 = T0.consume_waits()
                p.emit("vector", lambda e, g8=g8: e.tensor_copy(
                    khatT[0:64, g8 * 1024:(g8 + 1) * 1024], T0.t[0:64, 0:1024]),
                    waits=wT0 + ([prev_b3_P] if prev_b3_P and g8 == 0 else []),
                    inc=T0.released())
                mkT = p.mark("vector")
                yield
                kTg = khatT[0:64, g8 * 1024:(g8 + 1) * 1024]
                w0 = pAB.produce_waits() + [mkT]
                for i, h in enumerate(hs):
                    p.emit("tensor", lambda e, i=i, kTg=kTg: e.matmul(
                        pAB.t[:, i * 128:(i + 1) * 128],
                        kTg[0:64, i * 128:(i + 1) * 128],
                        kTg[0:64, i * 128:(i + 1) * 128], start=True, stop=True),
                        waits=(w0 if i == 0 else ()),
                        inc=(pAB.produced() if i == 7 else None))
                wA = pAB.consume_waits()
                gateg = [prev_gangP] if prev_gangP else []
                p.emit("vector", lambda e, bloc=bloc, hs=hs: e.tensor_tensor(
                    nsb.rearrange("p (i cc) -> p i cc", cc=128),
                    pAB.t.rearrange("p (i cc) -> p i cc", cc=128),
                    bc(beta_s[bloc], 128, 8, rep=128, h0=hs[0]), ALU.mult),
                    waits=wA + gateg)
                p.emit("vector", lambda e: e.tensor_tensor(
                    nsb[:, 0:512], nsb[:, 0:512], STRIL4[:, 0:512], ALU.mult))
                p.emit("vector", lambda e: e.tensor_tensor(
                    nsb[:, 512:1024], nsb[:, 512:1024], STRIL4[:, 0:512],
                    ALU.mult), inc=pAB.released())
                mN = p.mark("vector")
                yield
                T1 = BK["T1"]
                w0 = T1.produce_waits() + [mN]
                for i in range(8):
                    p.emit("tensor", lambda e, i=i: e.transpose(
                        T1.t[:, i * 128:(i + 1) * 128],
                        nsb[:, i * 128:(i + 1) * 128], IDB),
                        waits=(w0 if i == 0 else ()),
                        inc=(T1.produced() if i == 7 else None))
                wT1 = T1.consume_waits()
                p.emit("scalar", lambda e: e.copy(fTp[0][:, 0:1024],
                                                  T1.t[:, 0:1024]),
                       waits=wT1, inc=T1.released())
                mprev = p.mark("scalar")
                yield
                cur, nxt = nsb, sqa
                for lev in range(1, NLEV + 1):
                    if lev < NLEV:
                        w0 = pCD.produce_waits() + [mprev]
                        for i in range(8):
                            p.emit("tensor", lambda e, i=i, lev=lev,
                                   cur=cur: e.matmul(
                                pCD.t[:, i * 128:(i + 1) * 128],
                                fTp[lev - 1][:, i * 128:(i + 1) * 128],
                                cur[:, i * 128:(i + 1) * 128],
                                start=True, stop=True),
                                waits=(w0 if i == 0 else ()),
                                inc=(pCD.produced() if i == 7 else None))
                    w0 = pEF.produce_waits() + [mprev]
                    for i in range(8):
                        p.emit("tensor", lambda e, i=i, lev=lev,
                               cur=cur: e.matmul(
                            pEF.t[:, i * 128:(i + 1) * 128],
                            cur[:, i * 128:(i + 1) * 128],
                            fTp[lev - 1][:, i * 128:(i + 1) * 128],
                            start=True, stop=True),
                            waits=(w0 if i == 0 else ()),
                            inc=(pEF.produced() if i == 7 else None))
                    if lev < NLEV:
                        wc = pCD.consume_waits()
                        p.emit("scalar", lambda e, nxt=nxt: e.copy(
                            nxt[:, 0:1024], pCD.t[:, 0:1024]), waits=wc,
                            inc=pCD.released())
                    we = pEF.consume_waits()
                    p.emit("scalar", lambda e, lev=lev: e.copy(
                        fTp[lev][:, 0:1024], pEF.t[:, 0:1024]), waits=we,
                        inc=pEF.released())
                    mprev = p.mark("scalar")
                    yield
                    cur, nxt = nxt, cur
                t0, t1 = nsb, sqa
                h0 = hs[0]
                p.emit("vector", lambda e, h0=h0, t0=t0, bloc=bloc: e.tensor_tensor(
                    t0.rearrange("p (i cc) -> p i cc", cc=128)[:, :, 0:64],
                    khat_b[bloc].rearrange("p (h j) -> p h j", j=64)
                    [:, h0:h0 + 8, :],
                    bc(beta_s[bloc], 128, 8, h0=h0), ALU.mult),
                    waits=[mprev])
                p.emit("vector", lambda e, h0=h0, t0=t0, bloc=bloc: e.tensor_tensor(
                    t0.rearrange("p (i cc) -> p i cc", cc=128)[:, :, 64:128],
                    vtm_b[bloc].rearrange("p (h j) -> p h j", j=64)
                    [:, h0:h0 + 8, :],
                    bc(vbsc[bloc], 128, 8, h0=h0), ALU.mult))
                mt = p.mark("vector")
                yield
                for lev in range(NLEV, 0, -1):
                    w0 = pAB.produce_waits() + [mt]
                    for i in range(8):
                        p.emit("tensor", lambda e, i=i, t0=t0: e.matmul(
                            pAB.t[:, i * 128:(i + 1) * 128],
                            IDB, t0[:, i * 128:(i + 1) * 128],
                            start=True, stop=False),
                            waits=(w0 if i == 0 else ()))
                        p.emit("tensor", lambda e, i=i, lev=lev, t0=t0: e.matmul(
                            pAB.t[:, i * 128:(i + 1) * 128],
                            fTp[lev][:, i * 128:(i + 1) * 128],
                            t0[:, i * 128:(i + 1) * 128],
                            start=False, stop=True),
                            inc=(pAB.produced() if i == 7 else None))
                    wa = pAB.consume_waits()
                    p.emit("scalar", lambda e, t1=t1: e.copy(
                        t1[:, 0:1024], pAB.t[:, 0:1024]), waits=wa,
                        inc=pAB.released())
                    mt = p.mark("scalar")
                    yield
                    t0, t1 = t1, t0
                w0 = pCD.produce_waits() + [mt]
                for i in range(8):
                    p.emit("tensor", lambda e, i=i, t0=t0: e.matmul(
                        pCD.t[:, i * 128:(i + 1) * 128],
                        fTp[0][:, i * 128:(i + 1) * 128],
                        t0[:, i * 128:(i + 1) * 128], start=True, stop=True),
                        waits=(w0 if i == 0 else ()),
                        inc=(pCD.produced() if i == 7 else None))
                wc = pCD.consume_waits()
                p.emit("vector", lambda e, g8=g8, t0=t0: e.tensor_tensor(
                    tall[:, g8 * 1024:(g8 + 1) * 1024], t0[:, 0:1024],
                    pCD.t[:, 0:1024], ALU.subtract), waits=wc,
                    inc=pCD.released())
                mtall = p.mark("vector")
                yield
                w0 = pEF.produce_waits() + [mkT]
                for i, h in enumerate(hs):
                    p.emit("tensor", lambda e, i=i, h=h, bloc=bloc,
                           kTg=kTg: e.matmul(
                        pEF.t[:, i * 128:(i + 1) * 128],
                        kTg[0:64, i * 128:(i + 1) * 128],
                        qt_b[bloc][0:64, h * 128:(h + 1) * 128],
                        start=True, stop=True),
                        waits=(w0 if i == 0 else ()),
                        inc=(pEF.produced() if i == 7 else None))
                we = pEF.consume_waits()
                p.emit("vector", lambda e, g8=g8: e.tensor_tensor(
                    attnT[:, g8 * 1024:g8 * 1024 + 512], pEF.t[:, 0:512],
                    TRIU4[:, 0:512], ALU.mult), waits=we)
                p.emit("vector", lambda e, g8=g8: e.tensor_tensor(
                    attnT[:, g8 * 1024 + 512:(g8 + 1) * 1024], pEF.t[:, 512:1024],
                    TRIU4[:, 0:512], ALU.mult), inc=pEF.released())
                T0 = BK["T0"]
                w0 = T0.produce_waits() + [mtall]
                for i in range(8):
                    u = bloc * 16 + 8 * gidx + i
                    p.emit("tensor", lambda e, i=i, u=u: e.transpose(
                        T0.t[0:64, i * 128:(i + 1) * 128],
                        tall[:, u * 128:u * 128 + 64], IDB),
                        waits=(w0 if i == 0 else ()),
                        inc=(T0.produced() if i == 7 else None))
                wT0 = T0.consume_waits()
                p.emit("vector", lambda e, g8=g8: e.tensor_copy(
                    wT[0:64, g8 * 1024:(g8 + 1) * 1024], T0.t[0:64, 0:1024]),
                    waits=wT0, inc=T0.released())
                prev_gangP = p.mark("tensor")
                yield
            yield

        def b34_gen(bloc):
            nonlocal prev_b3_P, prev_epiP, prev_b1gate
            p.emit("vector", lambda e, bloc=bloc: e.tensor_tensor(
                u0b[0:64, bloc * 1024:(bloc + 1) * 1024].rearrange(
                    "p (u j) -> p u j", j=64),
                S0[0:64, bloc * 1024:(bloc + 1) * 1024].rearrange(
                    "p (u j) -> p u j", j=64),
                bc(eLm[bloc], 64, 16), ALU.mult),
                waits=[stm[("mex", bloc)]])
            mu0 = p.mark("vector")
            yield
            w0 = pAB.produce_waits() + [mu0]
            for i in range(16):
                u = bloc * 16 + i
                p.emit("tensor", lambda e, i=i, u=u: e.matmul(
                    pAB.t[:, i * 64:(i + 1) * 64],
                    wT[0:64, u * 128:(u + 1) * 128],
                    u0b[0:64, u * 64:(u + 1) * 64], start=True, stop=True),
                    waits=(w0 if i == 0 else ()),
                    inc=(pAB.produced() if i == 15 else None))
            wg_ = pAB.consume_waits()
            u0_ = bloc * 16
            p.emit("vector", lambda e, u0_=u0_: e.tensor_tensor(
                zp[:, u0_ * 64:(u0_ + 16) * 64],
                tall.rearrange("p (u cc) -> p u cc", cc=128)
                [:, u0_:u0_ + 16, 64:128],
                pAB.t[:, 0:1024], ALU.subtract), waits=wg_, inc=pAB.released())
            mzp = p.mark("vector")
            yield
            w0 = pCD.produce_waits() + [mzp]
            for i in range(16):
                u = bloc * 16 + i
                h = i
                p.emit("tensor", lambda e, i=i, u=u, h=h, bloc=bloc: e.matmul(
                    pCD.t[:, i * 64:(i + 1) * 64],
                    qt_b[bloc][0:64, h * 128:(h + 1) * 128],
                    u0b[0:64, u * 64:(u + 1) * 64], start=True, stop=False),
                    waits=(w0 if i == 0 else ()))
                p.emit("tensor", lambda e, i=i, u=u: e.matmul(
                    pCD.t[:, i * 64:(i + 1) * 64],
                    attnT[:, u * 128:(u + 1) * 128],
                    zp[:, u * 64:(u + 1) * 64], start=False, stop=True),
                    inc=(pCD.produced() if i == 15 else None))
            wo_ = pCD.consume_waits()
            wev = list(wo_)
            if prev_epiP:
                wev.append(prev_epiP)
            p.emit("vector", lambda e, bloc=bloc: e.tensor_tensor(
                abf.rearrange("p (h j) -> p h j", j=64),
                pCD.t.rearrange("p (i j) -> p i j", j=64),
                bc(gam[bloc], 128, 16), ALU.mult),
                waits=wev, inc=pCD.released())
            w0 = pEF.produce_waits() + [mzp]
            for i in range(16):
                u = bloc * 16 + i
                h = i
                p.emit("tensor", lambda e, i=i, u=u, h=h, bloc=bloc: e.matmul(
                    pEF.t[0:64, i * 64:(i + 1) * 64],
                    khat_b[bloc][:, h * 64:(h + 1) * 64],
                    zp[:, u * 64:(u + 1) * 64], start=True, stop=True),
                    waits=(w0 if i == 0 else ()),
                    inc=(pEF.produced() if i == 15 else None))
            ws_ = pEF.consume_waits()
            u0c = bloc * 1024
            p.emit("vector", lambda e, bloc=bloc: e.tensor_tensor(
                stmp[0:64, 0:1024].rearrange("p (i j) -> p i j", j=64)
                if False else
                stmpw[0:64, 0:1024].rearrange("p (i j) -> p i j", j=64),
                pEF.t[0:64, 0:1024].rearrange("p (i j) -> p i j", j=64),
                bc(gCh[bloc], 64, 16), ALU.mult), waits=ws_,
                inc=pEF.released())
            p.emit("vector", lambda e, u0c=u0c, bloc=bloc: e.tensor_tensor(
                S0[0:64, u0c:u0c + 1024].rearrange("p (i j) -> p i j", j=64),
                S0[0:64, u0c:u0c + 1024].rearrange("p (i j) -> p i j", j=64),
                bc(gC[bloc], 64, 16), ALU.mult))
            p.emit("vector", lambda e, u0c=u0c: e.tensor_tensor(
                S0[0:64, u0c:u0c + 1024], S0[0:64, u0c:u0c + 1024],
                stmpw[0:64, 0:1024], ALU.add))
            prev_b3_P = p.mark("tensor")
            yield
            matt = p.mark("vector")
            yield
            if c == tap_chunk and bloc == 0:
                tap("u0b", u0b[0:64, 0:1024], [matt])
                tap("zp", zp[:, 0:1024], [matt])
                tap("attnout", abf, [matt])
                tap("S0", S0[0:64, 0:1024], [matt])

            gc = bloc * 16 + c
            slot = bloc
            xv = xt[slot]
            T1 = BK["T1"]
            w0 = T1.produce_waits() + [matt]
            for kb in range(8):
                p.emit("tensor", lambda e, kb=kb: e.transpose(
                    T1.t[:, kb * 128:(kb + 1) * 128],
                    abf[:, kb * 128:(kb + 1) * 128], IDB),
                    waits=(w0 if kb == 0 else ()),
                    inc=(T1.produced() if kb == 7 else None))
            prev_epiP = p.mark("tensor")
            yield
            wT1 = T1.consume_waits()
            p.emit("vector", lambda e: e.tensor_copy(aT, T1.t[:, :]), waits=wT1,
                   inc=T1.released())
            maT = p.mark("vector")
            yield
            w0 = pAB.produce_waits() + [maT]
            for half in range(2):
                for kb in range(8):
                    p.emit("tensor", lambda e, kb=kb, half=half: e.matmul(
                        pAB.t[:, half * 512:(half + 1) * 512],
                        aT[:, kb * 128:(kb + 1) * 128],
                        wo_sb[:, kb * 1024 + half * 512:kb * 1024 + (half + 1) * 512],
                        start=(kb == 0), stop=(kb == 7)),
                        waits=(w0 if half == 0 and kb == 0 else ()),
                        inc=(pAB.produced() if (half == 1 and kb == 7) else None))
            ww_ = pAB.consume_waits()
            p.emit("vector", lambda e: e.tensor_tensor(
                x2t[:, 0:1024], pAB.t[:, 0:1024], bobc[:, 0:1024], ALU.add),
                waits=ww_)
            p.emit("vector", lambda e, xv=xv: e.tensor_tensor(
                x2t[:, 0:1024], x2t[:, 0:1024], xv[:, 0:1024], ALU.add),
                inc=pAB.released())
            mx2 = p.mark("vector")
            yield
            last_x_free[slot] = mx2
            p.emit("sync", lambda e, gc=gc: e.dma_start(x2_sp[gc], x2t),
                   waits=[mx2], inc=p.chD)
            if c == tap_chunk and bloc == 0:
                tap("x2", x2t, [mx2])
            mn2 = ln_stats_norm(x2t, xnb, [mx2])
            T0 = BK["T0"]
            w0 = T0.produce_waits() + [mn2]
            for kb in range(8):
                p.emit("tensor", lambda e, kb=kb: e.transpose(
                    T0.t[:, kb * 128:(kb + 1) * 128],
                    xnb[:, kb * 128:(kb + 1) * 128], IDB),
                    waits=(w0 if kb == 0 else ()),
                    inc=(T0.produced() if kb == 7 else None))
            wT0 = T0.consume_waits()
            p.emit("vector", lambda e: e.tensor_copy(x1t_sb, T0.t[:, :]),
                   waits=wT0, inc=T0.released())
            mx1m = p.mark("vector")
            yield
            p.emit("sync", lambda e, gc=gc: e.dma_start(
                x1T_sp[gc].rearrange("kb p t -> p kb t"),
                x1t_sb.rearrange("p (kb t) -> p kb t", kb=8)),
                waits=[mx1m], inc=chSp)
            x1t_spill_gate[0] = (chSp, chSp.n)
            prev_b1gate = mx1m
            yield

        def _drain(g):
            for _ in g:
                pass

        def _ilv(a, b):
            act = [iter(a), iter(b)]
            while act:
                for g in list(act):
                    try:
                        next(g)
                    except StopIteration:
                        act.remove(g)

        if len(blocs) == 2:
            _drain(b1_gen(0))
            _ilv(gangs_gen(0), b1_gen(1))
            _ilv(gangs_gen(1), b34_gen(0))
            _drain(b34_gen(1))
        else:
            for bl in blocs:
                _drain(b1_gen(bl))
                _drain(gangs_gen(bl))
                _drain(b34_gen(bl))
        p.barrier3()
    if stop_after != "A" and dbg_spill != "input":
        for c in range(min(n_chunks, 16)):
            chunk_scan(c)

    if stop_after in ("C", "D"):
        phase_C(nc, p, ar, BK, dict(
            x_d=x_d, w_in=w_in, x2_sp=x2_sp, x1T_sp=x1T_sp, y_sp=y_sp,
            yT_sp=yT_sp, IDB=IDB, IDF=IDF, ONESROW=ONESROW, b2bc=b2bc,
            wrT=wrT, cw_all=cw_all, g2c=g2c, b2lcb=b2lcb, tap=tap,
            tap_chunk=tap_chunk, n_chunks=n_chunks, chCx1=chCx1, chCx2=chCx2,
            chCxc=chCxc), stop_after)
    if stop_after == "D":
        phase_D(nc, p, ar, BK, dict(
            w_in=w_in, y_sp=y_sp, yT_sp=yT_sp, out_d=out_d, cw_all=cw_all,
            tap=tap, n_chunks=n_chunks, chYT=chYT, chYs=chYs, chWp=chWp))
    p.barrier()
    with nc.Block() as block:
        p.realize(block)
    stack.close()
    return nc


def phase_C(nc, p, ar, BK, env, stop_after):
    bA, bB, bC, bD_, bE, bF_ = (BK[k] for k in "ABCDEF")
    x_d, w_in = env["x_d"], env["w_in"]
    x2_sp, x1T_sp, y_sp, yT_sp = env["x2_sp"], env["x1T_sp"], env["y_sp"], env["yT_sp"]
    IDB, IDF, ONESROW = env["IDB"], env["IDF"], env["ONESROW"]
    b2bc, wrT, cw_all = env["b2bc"], env["wrT"], env["cw_all"]
    g2c, b2lcb, tap = env["g2c"], env["b2lcb"], env["tap"]
    n_chunks = env["n_chunks"]
    chCx1, chCx2, chCxc = env["chCx1"], env["chCx2"], env["chCxc"]
    p.barrier()
    ar.pop()
    ar.push()
    w1_sb = ar.bf(32768)
    w2_sb = ar.bf(32768)
    x1g = [ar.bf(2048), ar.bf(2048)]
    hTh = ar.bf(8192)
    x2c = [ar.f32(D) for _ in range(2)]
    xc = [ar.f32(D) for _ in range(2)]
    ypre = [ar.f32(D) for _ in range(2)]
    xsq2 = ar.f32(D)
    stat2 = ar.f32(8)
    ybf = ar.bf(D)
    yT_s = ar.bf(D)
    gelub = ar.f32(32)
    b1st = ar.f32(128)
    rsm = ar.f32(16)
    zrow = ar.f32(16)
    assert ar.off <= ar.cols, ar.off

    src = w_in["w1"].rearrange("(kb p) (hb t) -> p kb hb t", p=128, t=128)
    dst = w1_sb.rearrange("p (kb hb t) -> p kb hb t", kb=8, t=128)
    p.emit("gpsimd", lambda e: e.dma_start(dst, src), inc=p.chW)
    src2 = w_in["w2"].rearrange("(hb p) (fh f) -> p hb fh f", p=128, f=512)
    dst2 = w2_sb.rearrange("p (hb fh f) -> p hb fh f", hb=32, f=512)
    p.emit("gpsimd", lambda e: e.dma_start(dst2, src2), inc=p.chW)
    p.emit("sync", lambda e: e.dma_start(
        b1st[0:32, 0:128], w_in["b1"].rearrange("(r t) -> r t", r=32)),
        inc=p.chD)
    nWD = (p.chW, p.chW.n)
    nDD = (p.chD, p.chD.n)
    w0 = bE.produce_waits() + [nWD]
    for hb in range(32):
        for kb in range(8):
            p.emit("tensor", lambda e, hb=hb, kb=kb: e.matmul(
                bE.t[:, hb:hb + 1],
                w1_sb[:, kb * 4096 + hb * 128:kb * 4096 + (hb + 1) * 128],
                b2lcb[:, kb:kb + 1], start=(kb == 0), stop=(kb == 7)),
                waits=(w0 if hb == 0 and kb == 0 else ()),
                inc=(bE.produced() if (hb == 31 and kb == 7) else None))
    w0 = bF_.produce_waits() + [nDD]
    p.emit("tensor", lambda e: e.matmul(bF_.t[:, 0:32], b1st[0:32, 0:128],
                                        IDF[0:32, 0:32], start=True, stop=True),
           waits=w0, inc=bF_.produced())
    wE = bE.consume_waits()
    wF = bF_.consume_waits()
    p.emit("vector", lambda e: e.tensor_copy(b1st[:, 0:32], bF_.t[:, 0:32]),
           waits=wF, inc=bF_.released())
    p.emit("vector", lambda e: e.tensor_tensor(gelub[:, 0:32], bE.t[:, 0:32],
                                               b1st[:, 0:32], ALU.add),
           waits=wE, inc=bE.released())
    tap("gelub", gelub[:, 0:32], [p.mark("vector")])
    for kb in range(8):
        p.emit("vector", lambda e, kb=kb: e.tensor_scalar(
            w1_sb[:, kb * 4096:(kb + 1) * 4096],
            w1_sb[:, kb * 4096:(kb + 1) * 4096],
            g2c[:, kb:kb + 1], None, ALU.mult))
    p.barrier()

    assert n_chunks >= 16
    ngrp = 16
    prev_marks = {}
    for grp in range(ngrp):
        gcs = [2 * grp, 2 * grp + 1]
        buf = grp % 2
        wx = [prev_marks.get(("x1g", buf))] if ("x1g", buf) in prev_marks else []
        for ci, gc in enumerate(gcs):
            p.emit("sync", lambda e, gc=gc, ci=ci, buf=buf: e.dma_start(
                x1g[buf].rearrange("p (kb c t) -> p kb c t", kb=8, c=2)
                [:, :, ci, :],
                x1T_sp[gc].rearrange("kb p t -> p kb t")),
                waits=([w for w in wx if w] if ci == 0 else []), inc=chCx1[buf])
            p.emit("sync", lambda e, gc=gc, ci=ci, buf=buf: e.dma_start(
                x2c[ci], x2_sp[gc]),
                waits=[prev_marks[("x2c", ci)]] if ("x2c", ci) in prev_marks
                else [], inc=chCx2[ci])
            p.emit("sync", lambda e, gc=gc, ci=ci: e.dma_start(
                xc[ci], x_d[gc // 16, (gc % 16) * 128:(gc % 16) * 128 + 128, :]),
                waits=[prev_marks[("xc", ci)]] if ("xc", ci) in prev_marks
                else [], inc=chCxc[ci])
        nload = [(chCx1[buf], chCx1[buf].n), (chCx2[0], chCx2[0].n),
                 (chCx2[1], chCx2[1].n), (chCxc[0], chCxc[0].n),
                 (chCxc[1], chCxc[1].n)]
        for hf in range(2):
            for hb in range(16):
                hbg = hf * 16 + hb
                bm = [bE, bF_][hbg % 2]
                w0 = bm.produce_waits() + nload
                for kb in range(8):
                    p.emit("tensor", lambda e, kb=kb, hbg=hbg, bm=bm, buf=buf: e.matmul(
                        bm.t[:, 0:256],
                        w1_sb[:, kb * 4096 + hbg * 128:kb * 4096 + (hbg + 1) * 128],
                        x1g[buf][:, kb * 256:(kb + 1) * 256],
                        start=(kb == 0), stop=(kb == 7)),
                        waits=(w0 if kb == 0 else ()),
                        inc=(bm.produced() if kb == 7 else None))
                wm = bm.consume_waits()
                p.emit("scalar", lambda e, hbg=hbg, bm=bm: e.activation(
                    hTh[:, hbg * 256:(hbg + 1) * 256], bm.t[:, 0:256], AF.Gelu,
                    bias=gelub[:, hbg:hbg + 1]), waits=wm, inc=bm.released())
        mh = p.mark("scalar")
        ob = {(0, 0): bA, (0, 1): bB, (1, 0): bC, (1, 1): bD_}
        for ci in range(2):
            for fh in range(2):
                bk_ = ob[(ci, fh)]
                w0 = bk_.produce_waits() + [mh]
                for hbg in range(32):
                    p.emit("tensor", lambda e, ci=ci, fh=fh, hbg=hbg,
                           bk_=bk_: e.matmul(
                        bk_.t[:, 0:512],
                        hTh[:, hbg * 256 + ci * 128:hbg * 256 + (ci + 1) * 128],
                        w2_sb[:, hbg * 1024 + fh * 512:hbg * 1024 + (fh + 1) * 512],
                        start=(hbg == 0), stop=(hbg == 31)),
                        waits=(w0 if hbg == 0 else ()),
                        inc=(bk_.produced() if hbg == 31 else None))
        mrow_marks = []
        for ci, gc in enumerate(gcs):
            for fh in range(2):
                bk_ = ob[(ci, fh)]
                wk_ = bk_.consume_waits()
                p.emit("vector", lambda e, ci=ci, fh=fh, bk_=bk_: e.tensor_tensor(
                    ypre[ci][:, fh * 512:(fh + 1) * 512], bk_.t[:, 0:512],
                    b2bc[:, fh * 512:(fh + 1) * 512], ALU.add), waits=wk_,
                    inc=bk_.released())
            if gc == 1:
                tap("ypre0", ypre[1], [p.mark("vector")])
            for fh in range(2):
                p.emit("vector", lambda e, ci=ci, fh=fh: e.tensor_tensor(
                    ypre[ci][:, fh * 512:(fh + 1) * 512],
                    ypre[ci][:, fh * 512:(fh + 1) * 512],
                    x2c[ci][:, fh * 512:(fh + 1) * 512], ALU.add))
                p.emit("vector", lambda e, ci=ci, fh=fh: e.tensor_tensor(
                    ypre[ci][:, fh * 512:(fh + 1) * 512],
                    ypre[ci][:, fh * 512:(fh + 1) * 512],
                    xc[ci][:, fh * 512:(fh + 1) * 512], ALU.add))
            mypre = p.mark("vector")
            if gc == 1:
                tap("ypre", ypre[ci], [mypre])
            prev_marks[("xc", ci)] = mypre
            p.emit("scalar", lambda e, ci=ci: e.activation(xsq2, ypre[ci],
                                                           AF.Square),
                   waits=[mypre])
            mS = p.mark("scalar")
            p.emit("vector", lambda e, ci=ci: e.reduce_sum(
                stat2[:, 0:1], ypre[ci].rearrange("p (a f) -> p a f", a=1),
                axis=AX.X), waits=[mypre])
            p.emit("vector", lambda e: e.reduce_sum(
                stat2[:, 1:2], xsq2.rearrange("p (a f) -> p a f", a=1),
                axis=AX.X), waits=[mS])
            p.emit("vector", lambda e: e.tensor_scalar(stat2[:, 2:3],
                                                       stat2[:, 0:1], 1.0 / D,
                                                       None, ALU.mult))
            p.emit("vector", lambda e: e.tensor_scalar(stat2[:, 3:4],
                                                       stat2[:, 1:2], 1.0 / D,
                                                       None, ALU.mult))
            p.emit("vector", lambda e: e.tensor_tensor(stat2[:, 4:5],
                                                       stat2[:, 2:3],
                                                       stat2[:, 2:3], ALU.mult))
            p.emit("vector", lambda e: e.tensor_tensor(stat2[:, 5:6],
                                                       stat2[:, 3:4],
                                                       stat2[:, 4:5],
                                                       ALU.subtract))
            mV = p.mark("vector")
            p.emit("scalar", lambda e: e.activation(stat2[:, 6:7], stat2[:, 5:6],
                                                    AF.Ln, bias=1e-5),
                   waits=[mV])
            p.emit("scalar", lambda e: e.activation(stat2[:, 7:8], stat2[:, 6:7],
                                                    AF.Exp, scale=-0.5))
            mR = p.mark("scalar")
            p.emit("vector", lambda e, ci=ci: e.tensor_scalar(
                x2c[ci], ypre[ci], stat2[:, 2:3], stat2[:, 7:8],
                ALU.subtract, ALU.mult), waits=[mR])
            my = p.mark("vector")
            p.emit("sync", lambda e, ci=ci, gc=gc: e.dma_start(
                y_sp[gc], x2c[ci]), waits=[my], inc=p.chD)
            p.emit("scalar", lambda e, ci=ci: e.copy(ybf, x2c[ci]), waits=[my])
            myb = p.mark("scalar")
            prev_marks[("x2c", ci)] = myb
            T0 = BK["T0"]
            w0 = T0.produce_waits() + [myb]
            for kb in range(8):
                p.emit("tensor", lambda e, kb=kb: e.transpose(
                    T0.t[:, kb * 128:(kb + 1) * 128],
                    ybf[:, kb * 128:(kb + 1) * 128], IDB),
                    waits=(w0 if kb == 0 else ()),
                    inc=(T0.produced() if kb == 7 else None))
            wT0 = T0.consume_waits()
            p.emit("vector", lambda e: e.tensor_copy(yT_s, T0.t[:, :]),
                   waits=wT0, inc=T0.released())
            myT = p.mark("vector")
            p.emit("sync", lambda e, gc=gc: e.dma_start(
                yT_sp[gc].rearrange("kb p t -> p kb t"),
                yT_s.rearrange("p (kb t) -> p kb t", kb=8)),
                waits=[myT], inc=p.chD)
            w0 = bE.produce_waits() + [myT]
            for kb in range(8):
                p.emit("tensor", lambda e, kb=kb: e.matmul(
                    bE.t[:, 0:8], yT_s[:, kb * 128:(kb + 1) * 128],
                    wrT[:, kb * 8:(kb + 1) * 8], start=(kb == 0),
                    stop=(kb == 7)),
                    waits=(w0 if kb == 0 else ()),
                    inc=(bE.produced() if kb == 7 else None))
            wE = bE.consume_waits()
            p.emit("vector", lambda e: e.reduce_max(rsm[:, 0:1], bE.t[:, 0:8],
                                                    axis=AX.X), waits=wE)
            p.emit("vector", lambda e: e.tensor_scalar(rsm[:, 1:2],
                                                       rsm[:, 0:1], -1.0, None,
                                                       ALU.mult))
            mneg = p.mark("vector")
            p.emit("scalar", lambda e: e.activation(rsm[:, 8:16], bE.t[:, 0:8],
                                                    AF.Exp, bias=rsm[:, 1:2]),
                   waits=[mneg], inc=bE.released())
            msc = p.mark("scalar")
            p.emit("vector", lambda e: e.reduce_max(rsm[:, 2:3], rsm[:, 8:16],
                                                    axis=AX.X), waits=[msc])
            p.emit("vector", lambda e: e.tensor_single_scalar(
                zrow[:, 0:8], rsm[:, 8:16], rsm[:, 2:3], ALU.is_equal))
            p.emit("vector", lambda e: e.tensor_tensor(
                zrow[:, 8:16], zrow[:, 0:8], rsm[:, 8:16], ALU.mult))
            p.emit("vector", lambda e: e.tensor_tensor(
                zrow[:, 8:16], rsm[:, 8:16], zrow[:, 8:16], ALU.subtract))
            p.emit("vector", lambda e: e.reduce_max(rsm[:, 4:5], zrow[:, 8:16],
                                                    axis=AX.X))
            p.emit("vector", lambda e: e.tensor_single_scalar(
                zrow[:, 0:8], rsm[:, 8:16], rsm[:, 4:5], ALU.is_ge))
            p.emit("vector", lambda e: e.tensor_tensor(zrow[:, 0:8],
                                                       zrow[:, 0:8],
                                                       rsm[:, 8:16], ALU.mult))
            p.emit("vector", lambda e: e.reduce_sum(
                rsm[:, 5:6], zrow[:, 0:8].rearrange("p (a f) -> p a f", a=1),
                axis=AX.X))
            p.emit("vector", lambda e: e.reciprocal(rsm[:, 6:7], rsm[:, 5:6]))
            p.emit("vector", lambda e, gc=gc: e.tensor_scalar(
                cw_all[:, gc * 8:(gc + 1) * 8], zrow[:, 0:8], rsm[:, 6:7],
                None, ALU.mult))
            if gc == NCH - 1:
                tap("cwt", cw_all[:, 0:8 * NCH], [p.mark("vector")])
            if gc == 0:
                mrt = p.mark("vector")
                tap("rsm", rsm[:, 0:16], [mrt])
                tap("zrow", zrow[:, 0:16], [mrt])
        prev_marks[("x1g", buf)] = p.mark("tensor")
        p.barrier3()


def phase_D(nc, p, ar, BK, env):
    bA, bB, bC, bD_, bE, bF_ = (BK[k] for k in "ABCDEF")
    pAB, pCD, pEF = BK["AB"], BK["CD"], BK["EF"]
    w_in, y_sp, yT_sp, out_d = env["w_in"], env["y_sp"], env["yT_sp"], env["out_d"]
    cw_all, tap = env["cw_all"], env["tap"]
    chYT, chYs, chWp = env["chYT"], env["chYs"], env["chWp"]
    p.barrier()
    ar.pop()
    ar.push()
    w1p = [ar.bf(8192), ar.bf(8192)]
    w2p = [ar.bf(8192), ar.bf(8192)]
    yTg = [ar.bf(2048) for _ in range(4)]
    hTg = [ar.bf(2048), ar.bf(2048)]
    accq = ar.f32(8192)
    yst = [ar.f32(D), ar.f32(D)]
    outt = ar.f32(D)
    xsq3 = ar.f32(D)
    stat3 = ar.f32(8)
    assert ar.off <= ar.cols, ar.off
    p.barrier()

    wprev = {}
    for q in range(4):
        p.emit("vector", lambda e: e.memset(accq[:, 0:4096], 0.0),
               waits=[wprev["acc"]] if "acc" in wprev else [])
        p.emit("vector", lambda e: e.memset(accq[:, 4096:8192], 0.0))
        macc0 = p.mark("vector")
        for ee in range(NE):
            for pan in range(4):
                pb = (ee * 4 + pan) % 2
                wwait = [wprev[("w", pb)]] if ("w", pb) in wprev else []
                src = w_in["W1e"][ee].rearrange(
                    "(kb p) (pan hh t) -> pan p kb hh t", p=128, pan=4, t=128)[pan]
                dst = w1p[pb].rearrange("p (kb hh t) -> p kb hh t", kb=8, t=128)
                p.emit("gpsimd", lambda e, src=src, dst=dst: e.dma_start(dst, src),
                       waits=wwait, inc=chWp[pb])
                src2 = w_in["W2e"][ee].rearrange(
                    "(pan hh p) f -> pan p hh f", p=128, pan=4)[pan]
                dst2 = w2p[pb].rearrange("p (hh f) -> p hh f", hh=8)
                p.emit("gpsimd", lambda e, src2=src2, dst2=dst2: e.dma_start(
                    dst2, src2), inc=chWp[pb])
                nw = (chWp[pb], chWp[pb].n)
                for grp in range(4):
                    gc = q * 8 + grp * 2
                    gb = grp
                    if ee == 0 and pan == 0:
                        ywait = [wprev[("y", gb)]] if ("y", gb) in wprev else []
                        for ci in range(2):
                            p.emit("sync", lambda e, gc=gc, ci=ci, gb=gb: e.dma_start(
                                yTg[gb].rearrange("p (kb c t) -> p kb c t",
                                                  kb=8, c=2)[:, :, ci, :],
                                yT_sp[gc + ci].rearrange("kb p t -> p kb t")),
                                waits=(ywait if ci == 0 else []), inc=chYT[gb])
                    ny = (chYT[gb], chYT[gb].n)
                    hb_ = hTg[grp % 2]
                    hwait = [wprev[("h", grp % 2)]] if ("h", grp % 2) in wprev else []
                    for hh4 in range(2):
                        bm = [pEF, pAB][hh4]
                        w0 = bm.produce_waits() + [nw, ny] + (hwait if hh4 == 0 else [])
                        for sub in range(4):
                            hh = hh4 * 4 + sub
                            for kb in range(8):
                                p.emit("tensor", lambda e, kb=kb, hh=hh, sub=sub,
                                       bm=bm, pb=pb, gb=grp: e.matmul(
                                    bm.t[:, sub * 256:(sub + 1) * 256],
                                    w1p[pb][:, kb * 1024 + hh * 128:kb * 1024 + (hh + 1) * 128],
                                    yTg[gb][:, kb * 256:(kb + 1) * 256],
                                    start=(kb == 0), stop=(kb == 7)),
                                    waits=(w0 if sub == 0 and kb == 0 else ()),
                                    inc=(bm.produced() if (sub == 3 and kb == 7)
                                         else None))
                        wm = bm.consume_waits()
                        p.emit("scalar", lambda e, hh4=hh4, bm=bm, hb_=hb_: e.activation(
                            hb_[:, hh4 * 1024:(hh4 + 1) * 1024],
                            bm.t[:, 0:1024], AF.Gelu), waits=wm,
                            inc=bm.released())
                    mh = p.mark("scalar")
                    for ci in range(2):
                        gcc = gc + ci
                        bk_ = [pCD, pEF][ci]
                        w0 = bk_.produce_waits() + [mh]
                        for fh in range(2):
                            for hh in range(8):
                                p.emit("tensor", lambda e, ci=ci, fh=fh, hh=hh,
                                       bk_=bk_, pb=pb, hb_=hb_: e.matmul(
                                    bk_.t[:, fh * 512:(fh + 1) * 512],
                                    hb_[:, hh * 256 + ci * 128:hh * 256 + (ci + 1) * 128],
                                    w2p[pb][:, hh * 1024 + fh * 512:hh * 1024 + (fh + 1) * 512],
                                    start=(hh == 0), stop=(hh == 7)),
                                    waits=(w0 if fh == 0 and hh == 0 else ()),
                                    inc=(bk_.produced() if (fh == 1 and hh == 7)
                                         else None))
                        wk_ = bk_.consume_waits()
                        p.emit("vector", lambda e, ci=ci, gcc=gcc, ee=ee, grp=grp,
                               bk_=bk_: e.scalar_tensor_tensor(
                            accq[:, (grp * 2 + ci) * 1024:(grp * 2 + ci + 1) * 1024],
                            bk_.t[:, 0:1024],
                            cw_all[:, gcc * 8 + ee:gcc * 8 + ee + 1],
                            accq[:, (grp * 2 + ci) * 1024:(grp * 2 + ci + 1) * 1024],
                            ALU.mult, ALU.add),
                            waits=wk_ + ([macc0] if ee == 0 and pan == 0
                                         else []), inc=bk_.released())
                    wprev[("h", grp % 2)] = p.mark("tensor")
                wprev[("w", pb)] = p.mark("tensor")
        mq = p.mark("vector")
        if q == 0:
            tap("accq", accq[:, 0:8192], [mq])
        mqP = p.mark("tensor")
        for gb in range(4):
            wprev[("y", gb)] = mqP
        for ci8 in range(8):
            gc = q * 8 + ci8
            sb = ci8 % 2
            ywait = [wprev[("ys", sb)]] if ("ys", sb) in wprev else []
            p.emit("sync", lambda e, gc=gc, sb=sb: e.dma_start(yst[sb], y_sp[gc]),
                   waits=ywait + ([mq] if ci8 < 2 else []), inc=chYs[sb])
            nyl = (chYs[sb], chYs[sb].n)
            p.emit("vector", lambda e, ci8=ci8, sb=sb: e.tensor_tensor(
                outt, yst[sb], accq[:, ci8 * 1024:(ci8 + 1) * 1024], ALU.add),
                waits=[nyl, mq])
            mo = p.mark("vector")
            p.emit("scalar", lambda e: e.activation(xsq3, outt, AF.Square),
                   waits=[mo])
            mS = p.mark("scalar")
            p.emit("vector", lambda e: e.reduce_sum(
                stat3[:, 0:1], outt.rearrange("p (a f) -> p a f", a=1),
                axis=AX.X))
            p.emit("vector", lambda e: e.reduce_sum(
                stat3[:, 1:2], xsq3.rearrange("p (a f) -> p a f", a=1),
                axis=AX.X), waits=[mS])
            p.emit("vector", lambda e: e.tensor_scalar(
                stat3[:, 2:3], stat3[:, 0:1], 1.0 / D, None, ALU.mult))
            p.emit("vector", lambda e: e.tensor_scalar(
                stat3[:, 3:4], stat3[:, 1:2], 1.0 / D, None, ALU.mult))
            p.emit("vector", lambda e: e.tensor_tensor(
                stat3[:, 4:5], stat3[:, 2:3], stat3[:, 2:3], ALU.mult))
            p.emit("vector", lambda e: e.tensor_tensor(
                stat3[:, 5:6], stat3[:, 3:4], stat3[:, 4:5], ALU.subtract))
            mV = p.mark("vector")
            p.emit("scalar", lambda e: e.activation(stat3[:, 6:7], stat3[:, 5:6],
                                                    AF.Ln, bias=1e-5),
                   waits=[mV])
            p.emit("scalar", lambda e: e.activation(stat3[:, 7:8], stat3[:, 6:7],
                                                    AF.Exp, scale=-0.5))
            mR = p.mark("scalar")
            p.emit("vector", lambda e, sb=sb: e.tensor_scalar(
                yst[sb], outt, stat3[:, 2:3], stat3[:, 7:8], ALU.subtract,
                ALU.mult), waits=[mR])
            mfin = p.mark("vector")
            wprev[("ys", sb)] = mfin
            p.emit("sync", lambda e, gc=gc, sb=sb: e.dma_start(
                out_d[gc // 16, (gc % 16) * 128:(gc % 16) * 128 + 128, :],
                yst[sb]), waits=[mfin], inc=p.chD)
        wprev["acc"] = p.mark("vector")


_PROG_CACHE = {}


def kernel(**inputs):
    cstf_np, cstb_np = make_consts()
    x = np.ascontiguousarray(np.asarray(inputs["L0_attn_x"], np.float32))
    names = dict(
        Wq="L0_attn_Wq", Wk="L0_attn_Wk", Wv="L0_attn_Wv", Wb="L0_attn_Wb",
        Wg="L0_attn_Wg", Wo="L0_attn_Wo", bo="L0_attn_bo", ln1g="L0_attn_ln1_g",
        ln1b="L0_attn_ln1_b", ln2g="L0_attn_ln2_g", ln2b="L0_attn_ln2_b",
        w1="L0_attn_w1", b1="L0_attn_b1", w2="L0_attn_w2", b2="L0_attn_b2",
        Wr="L0_ffn_Wr", W1e="L0_ffn_W1", W2e="L0_ffn_W2")
    wmap = {k: np.ascontiguousarray(np.asarray(inputs[v], np.float32))
            for k, v in names.items()}
    key = "full"
    if key not in _PROG_CACHE:
        _PROG_CACHE[key] = build_program()
    nc = _PROG_CACHE[key]
    in_maps = []
    for core in range(8):
        m = {"x": x[2 * core:2 * core + 2], "cstf": cstf_np, "cstb": cstb_np}
        m.update(wmap)
        in_maps.append(m)
    res = run_bass_kernel_spmd(nc, in_maps, core_ids=list(range(8)))
    out = np.concatenate([r["out"] for r in res.results], axis=0)
    return out.reshape(16, 2048, D).astype(np.float32)


# version 08
# speedup vs baseline: 1.1174x; 1.1174x over previous
import contextlib
import numpy as np
import ml_dtypes
import concourse.bass as bass
import concourse.mybir as mybir
from concourse.bass_utils import run_bass_kernel_spmd

F32 = mybir.dt.float32
BF16 = mybir.dt.bfloat16
AF = mybir.ActivationFunctionType
ALU = mybir.AluOpType
AX = mybir.AxisListType

D = 1024
H = 16
NCH = 32
NLEV = 6
HID = 4096
NE = 8
ENGS = ("sync", "gpsimd", "tensor", "vector", "scalar")


class Ch:
    def __init__(self, stack, nc, name, step=1):
        self.sem = stack.enter_context(nc.semaphore(name))
        self.step = step
        self.n = 0


class Bank:
    def __init__(self, stack, nc, name, dtype, view=None):
        if view is None:
            cols = 1024 if dtype == BF16 else 512
            self.t = stack.enter_context(nc.psum_tensor(name, [128, cols], dtype))
        else:
            self.t = view
        self.full = Ch(stack, nc, name + "f")
        self.free = Ch(stack, nc, name + "e")
        self.k = 0
        self.state = "idle"

    def produce_waits(self):
        assert self.state == "idle", self.state
        self.state = "prod"
        return [(self.free, self.k)]

    def produced(self):
        assert self.state == "prod"
        self.state = "full"
        return self.full

    def consume_waits(self):
        assert self.state == "full"
        self.state = "cons"
        return [(self.full, self.k + 1)]

    def released(self):
        assert self.state == "cons"
        self.state = "idle"
        self.k += 1
        return self.free


class Plan:
    def __init__(self, stack, nc):
        self.nc = nc
        self.ops = {e: [] for e in ENGS}
        self.lastinc = {e: None for e in ENGS}
        self.lastsync = {e: None for e in ENGS}
        self.chained = ("vector", "scalar")
        self.chD = Ch(stack, nc, "chD", 16)
        self.chW = Ch(stack, nc, "chW", 16)
        self.chP = Ch(stack, nc, "chP")
        self.chV = Ch(stack, nc, "chV")
        self.chS = Ch(stack, nc, "chS")
        self.chDm = Ch(stack, nc, "chDm")
        self.chWm = Ch(stack, nc, "chWm")
        self.engch = {"sync": self.chDm, "gpsimd": self.chWm, "tensor": self.chP,
                      "vector": self.chV, "scalar": self.chS}
        self.bk = {}
        self.praw = [stack.enter_context(
            nc.psum_tensor(f"pp{i}", [128, 1024], F32)) for i in range(3)]
        for i, nm in enumerate(("A", "B", "C", "D", "E", "F")):
            v = self.praw[i // 2][:, (i % 2) * 512:(i % 2 + 1) * 512]
            self.bk[nm] = Bank(stack, nc, "pb" + nm, F32, view=v)
        for i, nm in enumerate(("AB", "CD", "EF")):
            self.bk[nm] = Bank(stack, nc, "pp" + nm, F32,
                               view=self.praw[i][:, :])
        for nm in ("T0", "T1"):
            self.bk[nm] = Bank(stack, nc, "pb" + nm, BF16)

    def emit(self, eng, build, waits=(), inc=None):
        incs = []
        if inc is not None:
            for ch in (inc if isinstance(inc, (list, tuple)) else [inc]):
                if ch is None:
                    continue
                ch.n += 1
                incs.append(ch)
        assert len(incs) <= 1, "one semaphore update per instruction"
        wl = [(ch, int(n)) for (ch, n) in waits if ch is not None and n > 0]
        if eng in self.chained:
            if not incs:
                ech = self.engch[eng]
                ech.n += 1
                incs.append(ech)
            if self.lastsync[eng] is not None:
                wl.append(self.lastsync[eng])
            self.lastsync[eng] = (incs[0], incs[0].n)

        def fn(e, build=build, wl=wl, incs=incs):
            for ch, n in wl:
                e.wait_ge(ch.sem, ch.step * n)
            ins = build(e)
            for ch in incs:
                ins.then_inc(ch.sem, ch.step)
            return ins
        self.ops[eng].append(fn)
        self.lastinc[eng] = (incs[0], incs[0].n) if incs else None

    def mark(self, eng):
        assert eng in ("tensor", "vector", "scalar")
        ch = self.engch[eng]
        ch.n += 1
        tgt = ch.n
        if self.lastinc.get(eng):
            ch.n -= 1
            return self.lastinc[eng]
        if self.ops[eng]:
            last = self.ops[eng].pop()

            def fn(e, last=last, ch=ch):
                ins = last(e)
                ins.then_inc(ch.sem, ch.step)
                return ins
        else:
            def fn(e, ch=ch):
                ins = e.nop()
                ins.then_inc(ch.sem, ch.step)
                return ins
        self.ops[eng].append(fn)
        self.lastinc[eng] = (ch, tgt)
        return (ch, tgt)

    def barrier(self):
        marks = [self.mark(e) for e in ("tensor", "vector", "scalar")]
        marks += [(self.chD, self.chD.n), (self.chW, self.chW.n)]
        for e in ENGS:
            def fn(en, marks=list(marks)):
                ins = None
                for ch, n in marks:
                    if n > 0:
                        ins = en.wait_ge(ch.sem, ch.step * n)
                return ins
            self.ops[e].append(fn)
            self.lastinc[e] = None

    def barrier3(self):
        es = ("tensor", "vector", "scalar")
        marks = [self.mark(e) for e in es]
        for e in es:
            def fn(en, marks=list(marks)):
                ins = None
                for ch, n in marks:
                    ins = en.wait_ge(ch.sem, ch.step * n)
                return ins
            self.ops[e].append(fn)
            self.lastinc[e] = None

    def realize(self, block):
        for e in ENGS:
            lst = self.ops[e]

            def run(engine, lst=lst):
                for fn in lst:
                    fn(engine)
            getattr(block, e)(run)


class Arena:
    def __init__(self, stack, nc, cols):
        self.base = stack.enter_context(nc.sbuf_tensor("arena", [128, cols], BF16))
        self.cols = cols
        self.off = 0
        self.marks = []

    def bf(self, n):
        self.off = (self.off + 15) // 16 * 16
        assert self.off + n <= self.cols, f"arena bf overflow {self.off}+{n}"
        v = self.base[:, self.off:self.off + n]
        self.off += n
        return v

    def f32(self, n):
        return self.bf(2 * n).bitcast(F32)

    def push(self):
        self.marks.append(self.off)

    def pop(self):
        self.off = self.marks.pop()


def bc(view, parts, hcount, rep=64, h0=0):
    return bass.AP(view.tensor, view.offset + h0,
                   [[view.ap[0][0], parts], [1, hcount], [0, rep]])


def make_consts():
    ident = np.eye(128, dtype=np.float32)
    triu_incl = np.triu(np.ones((128, 128), np.float32), 0)
    stril = np.tril(np.ones((128, 128), np.float32), -1)
    cols = np.zeros((128, 16), np.float32)
    cols[:64, 0] = 1.0
    cols[:, 1] = 1.0
    cvals = np.zeros((128, 16), np.float32)
    cvals[:, 0] = 1e-5
    cvals[:, 1] = 1.0
    cst = np.concatenate([ident, triu_incl, np.tile(stril, (1, 4)),
                          np.tile(triu_incl, (1, 4)), np.tile(ident, (1, 4)),
                          cols, cvals], axis=1)
    return np.ascontiguousarray(cst, np.float32), np.eye(128).astype(ml_dtypes.bfloat16)


def build_program(n_chunks=NCH, stop_after="D", taps=(), tap_chunk=0, blocs=(0, 1), b_sub="full", dbg_spill=False):
    nc = bass.Bass("TRN2", target_bir_lowering=False)
    stack = contextlib.ExitStack()
    taps = set(taps)

    x_d = nc.dram_tensor("x", [2, 2048, D], F32, kind="ExternalInput")
    w_in = {}
    for nm, shp in (("Wq", [D, D]), ("Wk", [D, D]), ("Wv", [D, D]),
                    ("Wb", [D, H]), ("Wg", [D, H]), ("Wo", [D, D]),
                    ("bo", [D]), ("ln1g", [D]), ("ln1b", [D]), ("ln2g", [D]),
                    ("ln2b", [D]), ("w1", [D, HID]), ("b1", [HID]),
                    ("w2", [HID, D]), ("b2", [D]), ("Wr", [NE, D]),
                    ("W1e", [NE, D, HID]), ("W2e", [NE, HID, D])):
        w_in[nm] = nc.dram_tensor(nm, shp, F32, kind="ExternalInput")
    cstf_d = nc.dram_tensor("cstf", [128, 1824], F32, kind="ExternalInput")
    cstb_d = nc.dram_tensor("cstb", [128, 128], BF16, kind="ExternalInput")
    out_d = nc.dram_tensor("out", [2, 2048, D], F32, kind="ExternalOutput")
    spk = "ExternalOutput" if dbg_spill else "Internal"
    spk_in = "ExternalInput" if dbg_spill == "input" else spk
    x2_sp = nc.dram_tensor("x2_sp", [NCH, 128, D], F32, kind=spk_in)
    x1T_sp = nc.dram_tensor("x1T_sp", [NCH, 8, 128, 128], BF16, kind=spk_in)
    y_sp = nc.dram_tensor("y_sp", [NCH, 128, D], F32,
                          kind=("ExternalOutput" if dbg_spill else "Internal"))
    yT_sp = nc.dram_tensor("yT_sp", [NCH, 8, 128, 128], BF16,
                           kind=("ExternalOutput" if dbg_spill else "Internal"))

    tap_d = {}
    TAPS_BF = {"qt": [128, 2048], "tall": [128, 4096], "attnT": [128, 4096],
               "khatT": [128, 4096], "wt": [128, 4096], "u0b": [128, 2048],
               "zp": [128, 2048], "khat": [128, D], "vtm": [128, D],
               "x1T": [128, D], "yT": [128, D], "hh": [128, 4096], "x1gd": [128, 2048], "attnout": [128, D]}
    TAPS_F = {"beta": [128, H], "L": [128, H], "gamma": [128, H],
              "x2": [128, D], "y": [128, D],
              "cwt": [128, 8 * NCH], "ktmf": [128, D], "S0": [128, 2048], "logg": [128, H], "Lt": [128, H], "rows": [1, 48], "gelub": [128, 32], "ypre0": [128, D], "accq": [128, 8192], "rsm": [128, 16], "zrow": [128, 16],
              "ypre": [128, D]}
    for t in taps:
        if t in TAPS_BF:
            tap_d[t] = nc.dram_tensor("tap_" + t, TAPS_BF[t], BF16,
                                      kind="ExternalOutput")
        else:
            tap_d[t] = nc.dram_tensor("tap_" + t, TAPS_F[t], F32,
                                      kind="ExternalOutput")

    ar = Arena(stack, nc, 103424)
    p = Plan(stack, nc)
    BK = p.bk
    chX = [Ch(stack, nc, f"chX{i}", 16) for i in range(2)]
    chSp = Ch(stack, nc, "chSp", 16)
    chCx1 = [Ch(stack, nc, f"chCx1{i}", 16) for i in range(2)]
    chCx2 = [Ch(stack, nc, f"chCx2{i}", 16) for i in range(2)]
    chCxc = [Ch(stack, nc, f"chCxc{i}", 16) for i in range(2)]
    chYT = [Ch(stack, nc, f"chYT{i}", 16) for i in range(4)]
    chYs = [Ch(stack, nc, f"chYs{i}", 16) for i in range(2)]
    chWp = [Ch(stack, nc, f"chWp{i}", 16) for i in range(2)]
    bA, bB, bC, bD_, bE, bF_ = (BK[k] for k in "ABCDEF")

    def tap(name, view, waits):
        if name in tap_d:
            pr, fr = view.shape
            p.emit("sync", lambda e, v=view, n=name, pr=pr, fr=fr: e.dma_start(
                tap_d[n][0:pr, 0:fr], v), waits=waits, inc=p.chD)

    cstf = ar.f32(1824)
    cstb = ar.bf(128)
    IDF = cstf[:, 0:128]
    TRIU = cstf[:, 128:256]
    STRIL4 = cstf[:, 256:768]
    TRIU4 = cstf[:, 768:1280]
    IDENT4 = cstf[:, 1280:1792]
    MASKCOL = cstf[:, 1792:1793]
    ONESCOL = cstf[:, 1793:1794]
    ONESROW = TRIU[0:1, 0:128]
    IDB = cstb[:, 0:128]
    nc.const_aps.register(F32, 1e-5, cstf[:, 1808:1809])
    nc.const_aps.register(F32, 1.0, cstf[:, 1809:1810])
    b2bc = ar.f32(D)
    wrT = ar.bf(64)
    cw_all = ar.f32(8 * NCH)
    g2c = ar.f32(8)
    b2lcb = ar.bf(8)
    ar.push()

    wq_sb = ar.bf(8192)
    wk_sb = ar.bf(8192)
    wv_sb = ar.bf(8192)
    wo_sb = ar.bf(8192)
    wb_sb = ar.bf(128)
    wg_sb = ar.bf(128)
    g1c = ar.f32(8)
    b1lcb = ar.bf(8)
    bqc = ar.f32(8)
    bkbc = ar.f32(D)
    bvbc = ar.f32(D)
    bbbc = ar.f32(H)
    bgbc = ar.f32(H)
    bobc = ar.f32(D)
    rowst = ar.f32(576)
    S0 = ar.f32(2048)
    xt = [ar.f32(D) for _ in range(2)]
    stat = ar.f32(8)
    xnb = ar.bf(D)
    x1T = ar.bf(D)
    qt_b = [ar.bf(2048) for _ in range(2)]
    khat_b = [ar.bf(D) for _ in range(2)]
    vtm_b = [ar.bf(D) for _ in range(2)]
    zbf = ar.f32(H)
    zgf = ar.f32(H)
    beta_s = [ar.f32(H) for _ in range(2)]
    zmin = ar.f32(H)
    zab = ar.f32(H)
    logg_s = ar.f32(H)
    L_s = ar.f32(H)
    Lt_s = ar.f32(H)
    gam = [ar.f32(H) for _ in range(2)]
    vbsc = [ar.f32(H) for _ in range(2)]
    eLm = [ar.f32(H) for _ in range(2)]
    gCh = [ar.f32(H) for _ in range(2)]
    gC = [ar.f32(H) for _ in range(2)]
    knss = ar.f32(H)
    knsc = ar.f32(H)
    nsb = ar.bf(1024)
    sqa = ar.bf(1024)
    fTp = [ar.bf(1024) for _ in range(NLEV + 1)]
    tall = ar.bf(4096)
    attnT = ar.bf(4096)
    khatT = ar.bf(4096)
    wT = ar.bf(4096)
    ar.push()
    vecst = ar.f32(512)
    wr_raw = ar.f32(1024)
    rows2 = ar.f32(2048)
    ar.pop()
    u0b = ar.bf(2048)
    zp = ar.bf(2048)
    stmp = ar.f32(1024)
    stmpw = stmp
    abf = ar.bf(D)
    aT = ar.bf(D)
    x2t = ar.f32(D)
    x1t_sb = x1T
    xsq = zp.bitcast(F32)
    ktm_f = x2t
    import sys
    print("ARENA scan off:", ar.off, "of", ar.cols, file=sys.stderr)
    assert ar.off <= ar.cols, ar.off

    p.emit("sync", lambda e: e.dma_start(cstf, cstf_d[:, :]), inc=p.chD)
    p.emit("sync", lambda e: e.dma_start(cstb, cstb_d[:, :]), inc=p.chD)
    for w_sb, nm, ncol in ((wq_sb, "Wq", D), (wk_sb, "Wk", D), (wv_sb, "Wv", D),
                           (wo_sb, "Wo", D), (wb_sb, "Wb", H), (wg_sb, "Wg", H)):
        if ncol == D:
            src = w_in[nm].rearrange("(kb p) (nb t) -> p kb nb t", p=128, t=128)
            dst = w_sb.rearrange("p (kb nb t) -> p kb nb t", kb=8, t=128)
        else:
            src = w_in[nm].rearrange("(kb p) t -> p kb t", p=128)
            dst = w_sb.rearrange("p (kb t) -> p kb t", kb=8)
        p.emit("gpsimd", lambda e, d=dst, s=src: e.dma_start(d, s), inc=p.chW)
    nW_w = p.chW.n
    for i, nm in enumerate(("ln1g", "ln1b", "ln2g", "ln2b")):
        p.emit("sync", lambda e, i=i, nm=nm: e.dma_start(
            vecst[0:8, i * 128:(i + 1) * 128],
            w_in[nm].rearrange("(r t) -> r t", r=8)), inc=p.chD)
    p.emit("sync", lambda e: e.dma_start(
        rows2[0:1, 0:D], w_in["bo"].rearrange("(r f) -> r f", r=1)), inc=p.chD)
    p.emit("sync", lambda e: e.dma_start(
        rows2[0:1, D:2 * D], w_in["b2"].rearrange("(r f) -> r f", r=1)),
        inc=p.chD)
    p.emit("sync", lambda e: e.dma_start(wr_raw[0:8, 0:512],
                                         w_in["Wr"][:, 0:512]), inc=p.chD)
    p.emit("sync", lambda e: e.dma_start(wr_raw[0:8, 512:1024],
                                         w_in["Wr"][:, 512:1024]), inc=p.chD)
    nD_A = p.chD.n

    w0 = bA.produce_waits() + [(p.chD, nD_A), (p.chW, nW_w)]
    for i in range(4):
        p.emit("tensor", lambda e, i=i: e.matmul(
            bA.t[:, i * 8:(i + 1) * 8], vecst[0:8, i * 128:(i + 1) * 128],
            IDF[0:8, 0:8], start=True, stop=True),
            waits=(w0 if i == 0 else ()), inc=(bA.produced() if i == 3 else None))
    wA = bA.consume_waits()
    p.emit("vector", lambda e: e.tensor_copy(g1c[:, 0:8], bA.t[:, 0:8]), waits=wA)
    p.emit("vector", lambda e: e.tensor_copy(b1lcb[:, 0:8], bA.t[:, 8:16]))
    p.emit("vector", lambda e: e.tensor_copy(g2c[:, 0:8], bA.t[:, 16:24]))
    p.emit("vector", lambda e: e.tensor_copy(b2lcb[:, 0:8], bA.t[:, 24:32]),
           inc=bA.released())
    w0 = bB.produce_waits()
    for kb in range(8):
        rsl = wr_raw[0:8, kb * 128:(kb + 1) * 128]
        p.emit("tensor", lambda e, kb=kb, rsl=rsl: e.matmul(
            bB.t[:, kb * 8:(kb + 1) * 8], rsl, IDF[0:8, 0:8],
            start=True, stop=True),
            waits=(w0 if kb == 0 else ()), inc=(bB.produced() if kb == 7 else None))
    wB = bB.consume_waits()
    p.emit("vector", lambda e: e.tensor_copy(wrT[:, 0:64], bB.t[:, 0:64]),
           waits=wB, inc=bB.released())
    mVA = p.mark("vector")

    w0 = bC.produce_waits() + [mVA]
    for nb in range(8):
        for kb in range(8):
            p.emit("tensor", lambda e, nb=nb, kb=kb: e.matmul(
                bC.t[:, nb:nb + 1],
                wq_sb[:, kb * 1024 + nb * 128:kb * 1024 + (nb + 1) * 128],
                b1lcb[:, kb:kb + 1], start=(kb == 0), stop=(kb == 7)),
                waits=(w0 if nb == 0 and kb == 0 else ()),
                inc=(bC.produced() if (nb == 7 and kb == 7) else None))
    wC = bC.consume_waits()
    p.emit("vector", lambda e: e.tensor_scalar(bqc[:, 0:8], bC.t[:, 0:8], 0.125,
                                               None, ALU.mult), waits=wC,
           inc=bC.released())

    def row_then_bcast(wsb, dstb, width, ncolw):
        nhalf = (width + 511) // 512
        for half in range(nhalf):
            w = min(512, width - half * 512)
            w0 = bD_.produce_waits()
            for kb in range(8):
                p.emit("tensor", lambda e, kb=kb, half=half, w=w, wsb=wsb,
                       ncolw=ncolw: e.matmul(
                    bD_.t[0:1, 0:w], b1lcb[:, kb:kb + 1],
                    wsb[:, kb * ncolw + half * 512:kb * ncolw + half * 512 + w],
                    start=(kb == 0), stop=(kb == 7)),
                    waits=(w0 if kb == 0 else ()),
                    inc=(bD_.produced() if kb == 7 else None))
            wD = bD_.consume_waits()
            p.emit("vector", lambda e, w=w: e.tensor_copy(
                rowst[0:1, 0:w], bD_.t[0:1, 0:w]), waits=wD,
                inc=bD_.released())
            mr = p.mark("vector")
            w0 = bE.produce_waits() + [mr]
            p.emit("tensor", lambda e, w=w: e.matmul(
                bE.t[:, 0:w], ONESROW, rowst[0:1, 0:w],
                start=True, stop=True), waits=w0, inc=bE.produced())
            wE = bE.consume_waits()
            p.emit("vector", lambda e, dstb=dstb, half=half, w=w: e.tensor_copy(
                dstb[:, half * 512:half * 512 + w], bE.t[:, 0:w]), waits=wE,
                inc=bE.released())

    row_then_bcast(wk_sb, bkbc, D, D)
    row_then_bcast(wv_sb, bvbc, D, D)
    row_then_bcast(wb_sb, bbbc, H, H)
    row_then_bcast(wg_sb, bgbc, H, H)
    for (rw, dstb) in ((rows2[0:1, 0:D], bobc), (rows2[0:1, D:2 * D], b2bc)):
        for half in range(2):
            row = rw[0:1, half * 512:(half + 1) * 512]
            w0 = bE.produce_waits()
            p.emit("tensor", lambda e, row=row: e.matmul(
                bE.t[:, 0:512], ONESROW, row, start=True, stop=True),
                waits=w0, inc=bE.produced())
            wE = bE.consume_waits()
            p.emit("vector", lambda e, dstb=dstb, half=half: e.tensor_copy(
                dstb[:, half * 512:(half + 1) * 512], bE.t[:, 0:512]),
                waits=wE, inc=bE.released())
    for wsb, ncol in ((wq_sb, D), (wk_sb, D), (wv_sb, D), (wb_sb, H), (wg_sb, H)):
        for kb in range(8):
            p.emit("vector", lambda e, wsb=wsb, kb=kb, ncol=ncol: e.tensor_scalar(
                wsb[:, kb * ncol:(kb + 1) * ncol],
                wsb[:, kb * ncol:(kb + 1) * ncol],
                g1c[:, kb:kb + 1], None, ALU.mult))
    p.emit("vector", lambda e: e.memset(S0[0:64, :], 0.0))
    p.barrier()

    STAT = 1.0 / D

    def ln_stats_norm(src_f32, dst_bf, wpre):
        p.emit("scalar", lambda e: e.activation(xsq, src_f32, AF.Square),
               waits=wpre)
        mS = p.mark("scalar")
        p.emit("vector", lambda e: e.reduce_sum(
            stat[:, 0:1], src_f32.rearrange("p (a f) -> p a f", a=1), axis=AX.X),
            waits=wpre)
        p.emit("vector", lambda e: e.reduce_sum(
            stat[:, 1:2], xsq.rearrange("p (a f) -> p a f", a=1), axis=AX.X),
            waits=[mS])
        p.emit("vector", lambda e: e.tensor_scalar(stat[:, 2:3], stat[:, 0:1],
                                                   STAT, None, ALU.mult))
        p.emit("vector", lambda e: e.tensor_scalar(stat[:, 3:4], stat[:, 1:2],
                                                   STAT, None, ALU.mult))
        p.emit("vector", lambda e: e.tensor_tensor(stat[:, 4:5], stat[:, 2:3],
                                                   stat[:, 2:3], ALU.mult))
        p.emit("vector", lambda e: e.tensor_tensor(stat[:, 5:6], stat[:, 3:4],
                                                   stat[:, 4:5], ALU.subtract))
        mV = p.mark("vector")
        p.emit("scalar", lambda e: e.activation(stat[:, 6:7], stat[:, 5:6],
                                                AF.Ln, bias=1e-5), waits=[mV])
        p.emit("scalar", lambda e: e.activation(stat[:, 7:8], stat[:, 6:7],
                                                AF.Exp, scale=-0.5))
        mR = p.mark("scalar")
        p.emit("vector", lambda e: e.tensor_scalar(
            dst_bf, src_f32, stat[:, 2:3], stat[:, 7:8], ALU.subtract, ALU.mult),
            waits=[mR])
        return p.mark("vector")

    last_x_free = [None, None]
    x1t_spill_gate = [None]
    prev_b3_P = None
    prev_b1gate = None
    prev_gangP = None
    prev_epiP = None

    def chunk_scan(c):
        nonlocal prev_b3_P, prev_b1gate, prev_gangP, prev_epiP
        pAB, pCD, pEF = BK["AB"], BK["CD"], BK["EF"]
        stm = {}

        def b1_gen(bloc):
            nonlocal prev_b1gate
            gc = bloc * 16 + c
            slot = bloc
            xv = xt[slot]
            wx = [last_x_free[slot]] if last_x_free[slot] else []
            p.emit("sync", lambda e, gc=gc, xv=xv: e.dma_start(
                xv, x_d[gc // 16, (gc % 16) * 128:(gc % 16) * 128 + 128, :]),
                waits=wx, inc=chX[slot])
            nx = (chX[slot], chX[slot].n)
            gates = [nx] + ([prev_b1gate] if prev_b1gate else [])
            mnorm = ln_stats_norm(xv, xnb, gates)
            T0 = BK["T0"]
            w0 = T0.produce_waits() + [mnorm]
            for kb in range(8):
                p.emit("tensor", lambda e, kb=kb: e.transpose(
                    T0.t[:, kb * 128:(kb + 1) * 128],
                    xnb[:, kb * 128:(kb + 1) * 128], IDB),
                    waits=(w0 if kb == 0 else ()),
                    inc=(T0.produced() if kb == 7 else None))
            wT0 = T0.consume_waits()
            p.emit("vector", lambda e: e.tensor_copy(x1T, T0.t[:, :]),
                   waits=wT0 + ([x1t_spill_gate[0]] if x1t_spill_gate[0] else []),
                   inc=T0.released())
            mx1T = p.mark("vector")
            yield
            if c == tap_chunk and bloc == 0:
                tap("x1T", x1T, [mx1T])
            qgate = [mx1T] + ([prev_b3_P] if prev_b3_P else [])
            for nbp in range(4):
                bq_ = [pAB, pCD][nbp % 2]
                w0 = bq_.produce_waits() + (qgate if nbp < 2 else [])
                for half in range(2):
                    nb = 2 * nbp + half
                    for kb in range(8):
                        p.emit("tensor", lambda e, nb=nb, kb=kb, half=half,
                               bq_=bq_: e.matmul(
                            bq_.t[:, half * 512:half * 512 + 128],
                            wq_sb[:, kb * 1024 + nb * 128:kb * 1024 + (nb + 1) * 128],
                            x1T[:, kb * 128:(kb + 1) * 128],
                            start=(kb == 0), stop=(kb == 7)),
                            waits=(w0 if half == 0 and kb == 0 else ()),
                            inc=(bq_.produced() if (half == 1 and kb == 7)
                                 else None))
                wq_ = bq_.consume_waits()
                for half in range(2):
                    nb = 2 * nbp + half
                    p.emit("vector", lambda e, nb=nb, half=half, bq_=bq_,
                           bloc=bloc: e.tensor_scalar(
                        qt_b[bloc][0:64, (2 * nb) * 128:(2 * nb + 1) * 128],
                        bq_.t[0:64, half * 512:half * 512 + 128], 0.125,
                        bqc[0:64, nb:nb + 1], ALU.mult, ALU.add),
                        waits=(wq_ if half == 0 else ()))
                    p.emit("vector", lambda e, nb=nb, half=half, bq_=bq_,
                           bloc=bloc: e.tensor_scalar(
                        qt_b[bloc][0:64, (2 * nb + 1) * 128:(2 * nb + 2) * 128],
                        bq_.t[64:128, half * 512:half * 512 + 128], 0.125,
                        bqc[64:128, nb:nb + 1], ALU.mult, ALU.add),
                        inc=(bq_.released() if half == 1 else None))
            w0 = pAB.produce_waits()
            for half in range(2):
                for kb in range(8):
                    p.emit("tensor", lambda e, kb=kb, half=half: e.matmul(
                        pAB.t[:, half * 512:(half + 1) * 512],
                        x1T[:, kb * 128:(kb + 1) * 128],
                        wk_sb[:, kb * 1024 + half * 512:kb * 1024 + (half + 1) * 512],
                        start=(kb == 0), stop=(kb == 7)),
                        waits=(w0 if half == 0 and kb == 0 else ()),
                        inc=(pAB.produced() if (half == 1 and kb == 7) else None))
            wk_ = pAB.consume_waits()
            p.emit("vector", lambda e: e.tensor_tensor(
                ktm_f[:, 0:1024], pAB.t[:, 0:1024], bkbc[:, 0:1024], ALU.add),
                waits=wk_, inc=pAB.released())
            mkt = p.mark("vector")
            yield
            w0 = pCD.produce_waits()
            for half in range(2):
                for kb in range(8):
                    p.emit("tensor", lambda e, kb=kb, half=half: e.matmul(
                        pCD.t[:, half * 512:(half + 1) * 512],
                        x1T[:, kb * 128:(kb + 1) * 128],
                        wv_sb[:, kb * 1024 + half * 512:kb * 1024 + (half + 1) * 512],
                        start=(kb == 0), stop=(kb == 7)),
                        waits=(w0 if half == 0 and kb == 0 else ()),
                        inc=(pCD.produced() if (half == 1 and kb == 7) else None))
            wv_ = pCD.consume_waits()
            p.emit("vector", lambda e, bloc=bloc: e.tensor_tensor(
                vtm_b[bloc][:, 0:1024], pCD.t[:, 0:1024], bvbc[:, 0:1024],
                ALU.add), waits=wv_, inc=pCD.released())
            w0 = pEF.produce_waits()
            for i, wsb in enumerate((wb_sb, wg_sb)):
                for kb in range(8):
                    p.emit("tensor", lambda e, i=i, kb=kb, wsb=wsb: e.matmul(
                        pEF.t[:, i * 16:(i + 1) * 16],
                        x1T[:, kb * 128:(kb + 1) * 128],
                        wsb[:, kb * 16:(kb + 1) * 16],
                        start=(kb == 0), stop=(kb == 7)),
                        waits=(w0 if i == 0 and kb == 0 else ()),
                        inc=(pEF.produced() if (i == 1 and kb == 7) else None))
            wE = pEF.consume_waits()
            p.emit("vector", lambda e: e.tensor_tensor(zbf[:, 0:16],
                                                       pEF.t[:, 0:16],
                                                       bbbc[:, 0:16], ALU.add),
                   waits=wE)
            p.emit("vector", lambda e: e.tensor_tensor(zgf[:, 0:16],
                                                       pEF.t[:, 16:32],
                                                       bgbc[:, 0:16], ALU.add),
                   inc=pEF.released())
            mz = p.mark("vector")
            yield
            p.emit("scalar", lambda e, bloc=bloc: e.activation(
                beta_s[bloc][:, 0:16], zbf[:, 0:16], AF.Sigmoid), waits=[mz])
            p.emit("vector", lambda e: e.tensor_scalar(zmin[:, 0:16],
                                                       zgf[:, 0:16], 0.0,
                                                       None, ALU.min))
            mzv = p.mark("vector")
            yield
            p.emit("scalar", lambda e: e.activation(zab[:, 0:16], zgf[:, 0:16],
                                                    AF.Abs), waits=[mzv])
            p.emit("scalar", lambda e: e.activation(zab[:, 0:16], zab[:, 0:16],
                                                    AF.Exp, scale=-1.0))
            p.emit("scalar", lambda e: e.activation(zab[:, 0:16], zab[:, 0:16],
                                                    AF.Ln, bias=1.0))
            mzs = p.mark("scalar")
            yield
            p.emit("vector", lambda e: e.tensor_tensor(
                logg_s[:, 0:16], zmin[:, 0:16], zab[:, 0:16], ALU.subtract),
                waits=[mzs])
            mlg = p.mark("vector")
            yield
            p.emit("scalar", lambda e: e.activation(xsq, ktm_f, AF.Square),
                   waits=[mkt])
            msq = p.mark("scalar")
            yield
            p.emit("vector", lambda e: e.reduce_sum(
                knss[:, 0:16], xsq.rearrange("p (h d) -> p h d", d=64),
                axis=AX.X), waits=[msq])
            mks = p.mark("vector")
            yield
            p.emit("scalar", lambda e: e.activation(knsc[:, 0:16],
                                                    knss[:, 0:16], AF.Ln),
                   waits=[mks])
            p.emit("scalar", lambda e: e.activation(knsc[:, 0:16],
                                                    knsc[:, 0:16], AF.Exp,
                                                    scale=0.5))
            mkn = p.mark("scalar")
            yield
            p.emit("vector", lambda e: e.tensor_scalar(knsc[:, 0:16],
                                                       knsc[:, 0:16], 1e-6,
                                                       None, ALU.add),
                   waits=[mkn])
            p.emit("vector", lambda e: e.reciprocal(knsc[:, 0:16],
                                                    knsc[:, 0:16]))
            p.emit("vector", lambda e, bloc=bloc: e.tensor_tensor(
                khat_b[bloc].rearrange("p (h j) -> p h j", j=64),
                ktm_f.rearrange("p (h j) -> p h j", j=64),
                bc(knsc, 128, 16), ALU.mult))
            mkh = p.mark("vector")
            yield
            if c == tap_chunk and bloc == 0:
                tap("ktmf", ktm_f, [mkh])
                tap("khat", khat_b[0], [mkh])
                tap("vtm", vtm_b[0], [mkh])
                tap("beta", beta_s[0][:, 0:16], [mkh])
            w0 = pEF.produce_waits() + [mlg]
            p.emit("tensor", lambda e: e.matmul(pEF.t[:, 0:16], TRIU,
                                                logg_s[:, 0:16], start=True,
                                                stop=True), waits=w0)
            p.emit("tensor", lambda e: e.matmul(pEF.t[0:1, 16:32], MASKCOL,
                                                logg_s[:, 0:16], start=True,
                                                stop=True))
            p.emit("tensor", lambda e: e.matmul(pEF.t[0:1, 32:48], ONESCOL,
                                                logg_s[:, 0:16], start=True,
                                                stop=True), inc=pEF.produced())
            wF = pEF.consume_waits()
            p.emit("vector", lambda e: e.tensor_copy(L_s[:, 0:16],
                                                     pEF.t[:, 0:16]), waits=wF)
            p.emit("vector", lambda e: e.tensor_copy(rowst[0:1, 512:528],
                                                     pEF.t[0:1, 16:32]))
            p.emit("vector", lambda e: e.tensor_copy(rowst[0:1, 528:544],
                                                     pEF.t[0:1, 32:48]),
                   inc=pEF.released())
            p.emit("vector", lambda e: e.tensor_tensor(
                rowst[0:1, 544:560], rowst[0:1, 528:544], rowst[0:1, 512:528],
                ALU.subtract))
            mLr = p.mark("vector")
            yield
            w0 = pEF.produce_waits() + [mLr]
            p.emit("tensor", lambda e: e.matmul(pEF.t[:, 0:16], ONESROW,
                                                rowst[0:1, 512:528],
                                                start=True, stop=True),
                   waits=w0)
            p.emit("tensor", lambda e: e.matmul(pEF.t[0:64, 16:32],
                                                TRIU[0:1, 0:64],
                                                rowst[0:1, 512:528],
                                                start=True, stop=True))
            p.emit("tensor", lambda e: e.matmul(pEF.t[:, 32:48], ONESROW,
                                                rowst[0:1, 544:560],
                                                start=True, stop=True))
            p.emit("tensor", lambda e: e.matmul(pEF.t[0:64, 48:64],
                                                TRIU[0:1, 0:64],
                                                rowst[0:1, 528:544],
                                                start=True, stop=True),
                   inc=pEF.produced())
            wE = pEF.consume_waits()
            p.emit("vector", lambda e: e.tensor_tensor(Lt_s[:, 0:16],
                                                       L_s[:, 0:16],
                                                       pEF.t[:, 0:16],
                                                       ALU.subtract), waits=wE)
            p.emit("scalar", lambda e, bloc=bloc: e.activation(
                eLm[bloc][0:64, 0:16], pEF.t[0:64, 16:32], AF.Exp), waits=wE)
            p.emit("scalar", lambda e, bloc=bloc: e.activation(
                gCh[bloc][:, 0:16], pEF.t[:, 32:48], AF.Exp))
            p.emit("scalar", lambda e, bloc=bloc: e.activation(
                gC[bloc][0:64, 0:16], pEF.t[0:64, 48:64], AF.Exp),
                inc=pEF.released())
            mLt = p.mark("vector")
            yield
            p.emit("scalar", lambda e, bloc=bloc: e.activation(
                gam[bloc][:, 0:16], Lt_s[:, 0:16], AF.Exp), waits=[mLt])
            p.emit("scalar", lambda e: e.activation(knss[:, 0:16],
                                                    Lt_s[:, 0:16], AF.Exp,
                                                    scale=-1.0))
            mex = p.mark("scalar")
            yield
            p.emit("vector", lambda e, bloc=bloc: e.tensor_tensor(
                vbsc[bloc][:, 0:16], knss[:, 0:16], beta_s[bloc][:, 0:16],
                ALU.mult), waits=[mex])
            mvb = p.mark("vector")
            yield
            prev_b1gate = mvb
            if c == tap_chunk and bloc == 0:
                tap("L", L_s[:, 0:16], [mLt])
                tap("gamma", gam[0][:, 0:16], [mvb])

            stm[("mkh", bloc)] = mkh
            stm[("mex", bloc)] = mex
            yield

        def gangs_gen(bloc):
            nonlocal prev_gangP
            for gidx in range(2):
                hs = [8 * gidx + i for i in range(8)]
                g8 = bloc * 2 + gidx
                T0 = BK["T0"]
                w0 = T0.produce_waits() + [stm[("mkh", bloc)]]
                for i, h in enumerate(hs):
                    p.emit("tensor", lambda e, i=i, h=h, bloc=bloc: e.transpose(
                        T0.t[0:64, i * 128:(i + 1) * 128],
                        khat_b[bloc][:, h * 64:(h + 1) * 64], IDB),
                        waits=(w0 if i == 0 else ()),
                        inc=(T0.produced() if i == 7 else None))
                wT0 = T0.consume_waits()
                p.emit("vector", lambda e, g8=g8: e.tensor_copy(
                    khatT[0:64, g8 * 1024:(g8 + 1) * 1024], T0.t[0:64, 0:1024]),
                    waits=wT0 + ([prev_b3_P] if prev_b3_P and g8 == 0 else []),
                    inc=T0.released())
                mkT = p.mark("vector")
                yield
                kTg = khatT[0:64, g8 * 1024:(g8 + 1) * 1024]
                w0 = pAB.produce_waits() + [mkT]
                for i, h in enumerate(hs):
                    p.emit("tensor", lambda e, i=i, kTg=kTg: e.matmul(
                        pAB.t[:, i * 128:(i + 1) * 128],
                        kTg[0:64, i * 128:(i + 1) * 128],
                        kTg[0:64, i * 128:(i + 1) * 128], start=True, stop=True),
                        waits=(w0 if i == 0 else ()),
                        inc=(pAB.produced() if i == 7 else None))
                wA = pAB.consume_waits()
                gateg = [prev_gangP] if prev_gangP else []
                p.emit("vector", lambda e, bloc=bloc, hs=hs: e.tensor_tensor(
                    nsb.rearrange("p (i cc) -> p i cc", cc=128),
                    pAB.t.rearrange("p (i cc) -> p i cc", cc=128),
                    bc(beta_s[bloc], 128, 8, rep=128, h0=hs[0]), ALU.mult),
                    waits=wA + gateg)
                p.emit("vector", lambda e: e.tensor_tensor(
                    nsb[:, 0:512], nsb[:, 0:512], STRIL4[:, 0:512], ALU.mult))
                p.emit("vector", lambda e: e.tensor_tensor(
                    nsb[:, 512:1024], nsb[:, 512:1024], STRIL4[:, 0:512],
                    ALU.mult), inc=pAB.released())
                mN = p.mark("vector")
                yield
                T1 = BK["T1"]
                w0 = T1.produce_waits() + [mN]
                for i in range(8):
                    p.emit("tensor", lambda e, i=i: e.transpose(
                        T1.t[:, i * 128:(i + 1) * 128],
                        nsb[:, i * 128:(i + 1) * 128], IDB),
                        waits=(w0 if i == 0 else ()),
                        inc=(T1.produced() if i == 7 else None))
                wT1 = T1.consume_waits()
                p.emit("scalar", lambda e: e.copy(fTp[0][:, 0:1024],
                                                  T1.t[:, 0:1024]),
                       waits=wT1, inc=T1.released())
                mprev = p.mark("scalar")
                yield
                cur, nxt = nsb, sqa
                for lev in range(1, NLEV + 1):
                    if lev < NLEV:
                        w0 = pCD.produce_waits() + [mprev]
                        for i in range(8):
                            p.emit("tensor", lambda e, i=i, lev=lev,
                                   cur=cur: e.matmul(
                                pCD.t[:, i * 128:(i + 1) * 128],
                                fTp[lev - 1][:, i * 128:(i + 1) * 128],
                                cur[:, i * 128:(i + 1) * 128],
                                start=True, stop=True),
                                waits=(w0 if i == 0 else ()),
                                inc=(pCD.produced() if i == 7 else None))
                    w0 = pEF.produce_waits() + [mprev]
                    for i in range(8):
                        p.emit("tensor", lambda e, i=i, lev=lev,
                               cur=cur: e.matmul(
                            pEF.t[:, i * 128:(i + 1) * 128],
                            cur[:, i * 128:(i + 1) * 128],
                            fTp[lev - 1][:, i * 128:(i + 1) * 128],
                            start=True, stop=True),
                            waits=(w0 if i == 0 else ()),
                            inc=(pEF.produced() if i == 7 else None))
                    if lev < NLEV:
                        wc = pCD.consume_waits()
                        p.emit("scalar", lambda e, nxt=nxt: e.copy(
                            nxt[:, 0:1024], pCD.t[:, 0:1024]), waits=wc,
                            inc=pCD.released())
                    we = pEF.consume_waits()
                    p.emit("scalar", lambda e, lev=lev: e.copy(
                        fTp[lev][:, 0:1024], pEF.t[:, 0:1024]), waits=we,
                        inc=pEF.released())
                    mprev = p.mark("scalar")
                    yield
                    cur, nxt = nxt, cur
                t0, t1 = nsb, sqa
                h0 = hs[0]
                p.emit("vector", lambda e, h0=h0, t0=t0, bloc=bloc: e.tensor_tensor(
                    t0.rearrange("p (i cc) -> p i cc", cc=128)[:, :, 0:64],
                    khat_b[bloc].rearrange("p (h j) -> p h j", j=64)
                    [:, h0:h0 + 8, :],
                    bc(beta_s[bloc], 128, 8, h0=h0), ALU.mult),
                    waits=[mprev])
                p.emit("vector", lambda e, h0=h0, t0=t0, bloc=bloc: e.tensor_tensor(
                    t0.rearrange("p (i cc) -> p i cc", cc=128)[:, :, 64:128],
                    vtm_b[bloc].rearrange("p (h j) -> p h j", j=64)
                    [:, h0:h0 + 8, :],
                    bc(vbsc[bloc], 128, 8, h0=h0), ALU.mult))
                mt = p.mark("vector")
                yield
                for lev in range(NLEV, 0, -1):
                    w0 = pAB.produce_waits() + [mt]
                    for i in range(8):
                        p.emit("tensor", lambda e, i=i, t0=t0: e.matmul(
                            pAB.t[:, i * 128:(i + 1) * 128],
                            IDB, t0[:, i * 128:(i + 1) * 128],
                            start=True, stop=False),
                            waits=(w0 if i == 0 else ()))
                        p.emit("tensor", lambda e, i=i, lev=lev, t0=t0: e.matmul(
                            pAB.t[:, i * 128:(i + 1) * 128],
                            fTp[lev][:, i * 128:(i + 1) * 128],
                            t0[:, i * 128:(i + 1) * 128],
                            start=False, stop=True),
                            inc=(pAB.produced() if i == 7 else None))
                    wa = pAB.consume_waits()
                    p.emit("scalar", lambda e, t1=t1: e.copy(
                        t1[:, 0:1024], pAB.t[:, 0:1024]), waits=wa,
                        inc=pAB.released())
                    mt = p.mark("scalar")
                    yield
                    t0, t1 = t1, t0
                w0 = pCD.produce_waits() + [mt]
                for i in range(8):
                    p.emit("tensor", lambda e, i=i, t0=t0: e.matmul(
                        pCD.t[:, i * 128:(i + 1) * 128],
                        fTp[0][:, i * 128:(i + 1) * 128],
                        t0[:, i * 128:(i + 1) * 128], start=True, stop=True),
                        waits=(w0 if i == 0 else ()),
                        inc=(pCD.produced() if i == 7 else None))
                wc = pCD.consume_waits()
                p.emit("vector", lambda e, g8=g8, t0=t0: e.tensor_tensor(
                    tall[:, g8 * 1024:(g8 + 1) * 1024], t0[:, 0:1024],
                    pCD.t[:, 0:1024], ALU.subtract), waits=wc,
                    inc=pCD.released())
                mtall = p.mark("vector")
                yield
                w0 = pEF.produce_waits() + [mkT]
                for i, h in enumerate(hs):
                    p.emit("tensor", lambda e, i=i, h=h, bloc=bloc,
                           kTg=kTg: e.matmul(
                        pEF.t[:, i * 128:(i + 1) * 128],
                        kTg[0:64, i * 128:(i + 1) * 128],
                        qt_b[bloc][0:64, h * 128:(h + 1) * 128],
                        start=True, stop=True),
                        waits=(w0 if i == 0 else ()),
                        inc=(pEF.produced() if i == 7 else None))
                we = pEF.consume_waits()
                p.emit("vector", lambda e, g8=g8: e.tensor_tensor(
                    attnT[:, g8 * 1024:g8 * 1024 + 512], pEF.t[:, 0:512],
                    TRIU4[:, 0:512], ALU.mult), waits=we)
                p.emit("vector", lambda e, g8=g8: e.tensor_tensor(
                    attnT[:, g8 * 1024 + 512:(g8 + 1) * 1024], pEF.t[:, 512:1024],
                    TRIU4[:, 0:512], ALU.mult), inc=pEF.released())
                T0 = BK["T0"]
                w0 = T0.produce_waits() + [mtall]
                for i in range(8):
                    u = bloc * 16 + 8 * gidx + i
                    p.emit("tensor", lambda e, i=i, u=u: e.transpose(
                        T0.t[0:64, i * 128:(i + 1) * 128],
                        tall[:, u * 128:u * 128 + 64], IDB),
                        waits=(w0 if i == 0 else ()),
                        inc=(T0.produced() if i == 7 else None))
                wT0 = T0.consume_waits()
                p.emit("vector", lambda e, g8=g8: e.tensor_copy(
                    wT[0:64, g8 * 1024:(g8 + 1) * 1024], T0.t[0:64, 0:1024]),
                    waits=wT0, inc=T0.released())
                prev_gangP = p.mark("tensor")
                yield
            yield

        def b34_gen(bloc):
            nonlocal prev_b3_P, prev_epiP, prev_b1gate
            p.emit("vector", lambda e, bloc=bloc: e.tensor_tensor(
                u0b[0:64, bloc * 1024:(bloc + 1) * 1024].rearrange(
                    "p (u j) -> p u j", j=64),
                S0[0:64, bloc * 1024:(bloc + 1) * 1024].rearrange(
                    "p (u j) -> p u j", j=64),
                bc(eLm[bloc], 64, 16), ALU.mult),
                waits=[stm[("mex", bloc)]])
            mu0 = p.mark("vector")
            yield
            w0 = pAB.produce_waits() + [mu0]
            for i in range(16):
                u = bloc * 16 + i
                p.emit("tensor", lambda e, i=i, u=u: e.matmul(
                    pAB.t[:, i * 64:(i + 1) * 64],
                    wT[0:64, u * 128:(u + 1) * 128],
                    u0b[0:64, u * 64:(u + 1) * 64], start=True, stop=True),
                    waits=(w0 if i == 0 else ()),
                    inc=(pAB.produced() if i == 15 else None))
            wg_ = pAB.consume_waits()
            u0_ = bloc * 16
            p.emit("vector", lambda e, u0_=u0_: e.tensor_tensor(
                zp[:, u0_ * 64:(u0_ + 16) * 64],
                tall.rearrange("p (u cc) -> p u cc", cc=128)
                [:, u0_:u0_ + 16, 64:128],
                pAB.t[:, 0:1024], ALU.subtract), waits=wg_, inc=pAB.released())
            mzp = p.mark("vector")
            yield
            w0 = pCD.produce_waits() + [mzp]
            for i in range(16):
                u = bloc * 16 + i
                h = i
                p.emit("tensor", lambda e, i=i, u=u, h=h, bloc=bloc: e.matmul(
                    pCD.t[:, i * 64:(i + 1) * 64],
                    qt_b[bloc][0:64, h * 128:(h + 1) * 128],
                    u0b[0:64, u * 64:(u + 1) * 64], start=True, stop=False),
                    waits=(w0 if i == 0 else ()))
                p.emit("tensor", lambda e, i=i, u=u: e.matmul(
                    pCD.t[:, i * 64:(i + 1) * 64],
                    attnT[:, u * 128:(u + 1) * 128],
                    zp[:, u * 64:(u + 1) * 64], start=False, stop=True),
                    inc=(pCD.produced() if i == 15 else None))
            wo_ = pCD.consume_waits()
            wev = list(wo_)
            if prev_epiP:
                wev.append(prev_epiP)
            p.emit("vector", lambda e, bloc=bloc: e.tensor_tensor(
                abf.rearrange("p (h j) -> p h j", j=64),
                pCD.t.rearrange("p (i j) -> p i j", j=64),
                bc(gam[bloc], 128, 16), ALU.mult),
                waits=wev, inc=pCD.released())
            w0 = pEF.produce_waits() + [mzp]
            for i in range(16):
                u = bloc * 16 + i
                h = i
                p.emit("tensor", lambda e, i=i, u=u, h=h, bloc=bloc: e.matmul(
                    pEF.t[0:64, i * 64:(i + 1) * 64],
                    khat_b[bloc][:, h * 64:(h + 1) * 64],
                    zp[:, u * 64:(u + 1) * 64], start=True, stop=True),
                    waits=(w0 if i == 0 else ()),
                    inc=(pEF.produced() if i == 15 else None))
            ws_ = pEF.consume_waits()
            u0c = bloc * 1024
            p.emit("vector", lambda e, bloc=bloc: e.tensor_tensor(
                stmp[0:64, 0:1024].rearrange("p (i j) -> p i j", j=64)
                if False else
                stmpw[0:64, 0:1024].rearrange("p (i j) -> p i j", j=64),
                pEF.t[0:64, 0:1024].rearrange("p (i j) -> p i j", j=64),
                bc(gCh[bloc], 64, 16), ALU.mult), waits=ws_,
                inc=pEF.released())
            p.emit("vector", lambda e, u0c=u0c, bloc=bloc: e.tensor_tensor(
                S0[0:64, u0c:u0c + 1024].rearrange("p (i j) -> p i j", j=64),
                S0[0:64, u0c:u0c + 1024].rearrange("p (i j) -> p i j", j=64),
                bc(gC[bloc], 64, 16), ALU.mult))
            p.emit("vector", lambda e, u0c=u0c: e.tensor_tensor(
                S0[0:64, u0c:u0c + 1024], S0[0:64, u0c:u0c + 1024],
                stmpw[0:64, 0:1024], ALU.add))
            prev_b3_P = p.mark("tensor")
            yield
            matt = p.mark("vector")
            yield
            if c == tap_chunk and bloc == 0:
                tap("u0b", u0b[0:64, 0:1024], [matt])
                tap("zp", zp[:, 0:1024], [matt])
                tap("attnout", abf, [matt])
                tap("S0", S0[0:64, 0:1024], [matt])

            gc = bloc * 16 + c
            slot = bloc
            xv = xt[slot]
            T1 = BK["T1"]
            w0 = T1.produce_waits() + [matt]
            for kb in range(8):
                p.emit("tensor", lambda e, kb=kb: e.transpose(
                    T1.t[:, kb * 128:(kb + 1) * 128],
                    abf[:, kb * 128:(kb + 1) * 128], IDB),
                    waits=(w0 if kb == 0 else ()),
                    inc=(T1.produced() if kb == 7 else None))
            prev_epiP = p.mark("tensor")
            yield
            wT1 = T1.consume_waits()
            p.emit("vector", lambda e: e.tensor_copy(aT, T1.t[:, :]), waits=wT1,
                   inc=T1.released())
            maT = p.mark("vector")
            yield
            w0 = pAB.produce_waits() + [maT]
            for half in range(2):
                for kb in range(8):
                    p.emit("tensor", lambda e, kb=kb, half=half: e.matmul(
                        pAB.t[:, half * 512:(half + 1) * 512],
                        aT[:, kb * 128:(kb + 1) * 128],
                        wo_sb[:, kb * 1024 + half * 512:kb * 1024 + (half + 1) * 512],
                        start=(kb == 0), stop=(kb == 7)),
                        waits=(w0 if half == 0 and kb == 0 else ()),
                        inc=(pAB.produced() if (half == 1 and kb == 7) else None))
            ww_ = pAB.consume_waits()
            p.emit("vector", lambda e: e.tensor_tensor(
                x2t[:, 0:1024], pAB.t[:, 0:1024], bobc[:, 0:1024], ALU.add),
                waits=ww_)
            p.emit("vector", lambda e, xv=xv: e.tensor_tensor(
                x2t[:, 0:1024], x2t[:, 0:1024], xv[:, 0:1024], ALU.add),
                inc=pAB.released())
            mx2 = p.mark("vector")
            yield
            last_x_free[slot] = mx2
            p.emit("sync", lambda e, gc=gc: e.dma_start(x2_sp[gc], x2t),
                   waits=[mx2], inc=p.chD)
            if c == tap_chunk and bloc == 0:
                tap("x2", x2t, [mx2])
            mn2 = ln_stats_norm(x2t, xnb, [mx2])
            T0 = BK["T0"]
            w0 = T0.produce_waits() + [mn2]
            for kb in range(8):
                p.emit("tensor", lambda e, kb=kb: e.transpose(
                    T0.t[:, kb * 128:(kb + 1) * 128],
                    xnb[:, kb * 128:(kb + 1) * 128], IDB),
                    waits=(w0 if kb == 0 else ()),
                    inc=(T0.produced() if kb == 7 else None))
            wT0 = T0.consume_waits()
            p.emit("vector", lambda e: e.tensor_copy(x1t_sb, T0.t[:, :]),
                   waits=wT0, inc=T0.released())
            mx1m = p.mark("vector")
            yield
            p.emit("sync", lambda e, gc=gc: e.dma_start(
                x1T_sp[gc].rearrange("kb p t -> p kb t"),
                x1t_sb.rearrange("p (kb t) -> p kb t", kb=8)),
                waits=[mx1m], inc=chSp)
            x1t_spill_gate[0] = (chSp, chSp.n)
            prev_b1gate = mx1m
            yield

        def _drain(g):
            for _ in g:
                pass

        def _ilv(a, b):
            act = [iter(a), iter(b)]
            while act:
                for g in list(act):
                    try:
                        next(g)
                    except StopIteration:
                        act.remove(g)

        if len(blocs) == 2:
            _drain(b1_gen(0))
            _ilv(gangs_gen(0), b1_gen(1))
            _ilv(gangs_gen(1), b34_gen(0))
            _drain(b34_gen(1))
        else:
            for bl in blocs:
                _drain(b1_gen(bl))
                _drain(gangs_gen(bl))
                _drain(b34_gen(bl))
        p.barrier3()
    if stop_after != "A" and dbg_spill != "input":
        for c in range(min(n_chunks, 16)):
            chunk_scan(c)

    if stop_after in ("C", "D"):
        phase_C(nc, p, ar, BK, dict(
            x_d=x_d, w_in=w_in, x2_sp=x2_sp, x1T_sp=x1T_sp, y_sp=y_sp,
            yT_sp=yT_sp, IDB=IDB, IDF=IDF, ONESROW=ONESROW, b2bc=b2bc,
            wrT=wrT, cw_all=cw_all, g2c=g2c, b2lcb=b2lcb, tap=tap,
            tap_chunk=tap_chunk, n_chunks=n_chunks, chCx1=chCx1, chCx2=chCx2,
            chCxc=chCxc), stop_after)
    if stop_after == "D":
        phase_D(nc, p, ar, BK, dict(
            w_in=w_in, y_sp=y_sp, yT_sp=yT_sp, out_d=out_d, cw_all=cw_all,
            tap=tap, n_chunks=n_chunks, chYT=chYT, chYs=chYs, chWp=chWp))
    p.barrier()
    with nc.Block() as block:
        p.realize(block)
    stack.close()
    return nc


def phase_C(nc, p, ar, BK, env, stop_after):
    bA, bB, bC, bD_, bE, bF_ = (BK[k] for k in "ABCDEF")
    x_d, w_in = env["x_d"], env["w_in"]
    x2_sp, x1T_sp, y_sp, yT_sp = env["x2_sp"], env["x1T_sp"], env["y_sp"], env["yT_sp"]
    IDB, IDF, ONESROW = env["IDB"], env["IDF"], env["ONESROW"]
    b2bc, wrT, cw_all = env["b2bc"], env["wrT"], env["cw_all"]
    g2c, b2lcb, tap = env["g2c"], env["b2lcb"], env["tap"]
    n_chunks = env["n_chunks"]
    chCx1, chCx2, chCxc = env["chCx1"], env["chCx2"], env["chCxc"]
    p.barrier()
    ar.pop()
    ar.push()
    w1_sb = ar.bf(32768)
    w2_sb = ar.bf(32768)
    x1g = [ar.bf(2048), ar.bf(2048)]
    hTh = ar.bf(8192)
    x2c = [ar.f32(D) for _ in range(2)]
    xc = [ar.f32(D) for _ in range(2)]
    ypre = [ar.f32(D) for _ in range(2)]
    xsq2 = ar.f32(D)
    stat2 = ar.f32(8)
    ybf = ar.bf(D)
    yT_s = ar.bf(D)
    gelub = ar.f32(32)
    b1st = ar.f32(128)
    rsm = ar.f32(16)
    zrow = ar.f32(16)
    assert ar.off <= ar.cols, ar.off

    src = w_in["w1"].rearrange("(kb p) (hb t) -> p kb hb t", p=128, t=128)
    dst = w1_sb.rearrange("p (kb hb t) -> p kb hb t", kb=8, t=128)
    p.emit("gpsimd", lambda e: e.dma_start(dst, src), inc=p.chW)
    src2 = w_in["w2"].rearrange("(hb p) (fh f) -> p hb fh f", p=128, f=512)
    dst2 = w2_sb.rearrange("p (hb fh f) -> p hb fh f", hb=32, f=512)
    p.emit("gpsimd", lambda e: e.dma_start(dst2, src2), inc=p.chW)
    p.emit("sync", lambda e: e.dma_start(
        b1st[0:32, 0:128], w_in["b1"].rearrange("(r t) -> r t", r=32)),
        inc=p.chD)
    nWD = (p.chW, p.chW.n)
    nDD = (p.chD, p.chD.n)
    w0 = bE.produce_waits() + [nWD]
    for hb in range(32):
        for kb in range(8):
            p.emit("tensor", lambda e, hb=hb, kb=kb: e.matmul(
                bE.t[:, hb:hb + 1],
                w1_sb[:, kb * 4096 + hb * 128:kb * 4096 + (hb + 1) * 128],
                b2lcb[:, kb:kb + 1], start=(kb == 0), stop=(kb == 7)),
                waits=(w0 if hb == 0 and kb == 0 else ()),
                inc=(bE.produced() if (hb == 31 and kb == 7) else None))
    w0 = bF_.produce_waits() + [nDD]
    p.emit("tensor", lambda e: e.matmul(bF_.t[:, 0:32], b1st[0:32, 0:128],
                                        IDF[0:32, 0:32], start=True, stop=True),
           waits=w0, inc=bF_.produced())
    wE = bE.consume_waits()
    wF = bF_.consume_waits()
    p.emit("vector", lambda e: e.tensor_copy(b1st[:, 0:32], bF_.t[:, 0:32]),
           waits=wF, inc=bF_.released())
    p.emit("vector", lambda e: e.tensor_tensor(gelub[:, 0:32], bE.t[:, 0:32],
                                               b1st[:, 0:32], ALU.add),
           waits=wE, inc=bE.released())
    tap("gelub", gelub[:, 0:32], [p.mark("vector")])
    for kb in range(8):
        p.emit("vector", lambda e, kb=kb: e.tensor_scalar(
            w1_sb[:, kb * 4096:(kb + 1) * 4096],
            w1_sb[:, kb * 4096:(kb + 1) * 4096],
            g2c[:, kb:kb + 1], None, ALU.mult))
    p.barrier()

    assert n_chunks >= 16
    ngrp = 16
    prev_marks = {}
    for grp in range(ngrp):
        gcs = [2 * grp, 2 * grp + 1]
        buf = grp % 2
        wx = [prev_marks.get(("x1g", buf))] if ("x1g", buf) in prev_marks else []
        for ci, gc in enumerate(gcs):
            p.emit("sync", lambda e, gc=gc, ci=ci, buf=buf: e.dma_start(
                x1g[buf].rearrange("p (kb c t) -> p kb c t", kb=8, c=2)
                [:, :, ci, :],
                x1T_sp[gc].rearrange("kb p t -> p kb t")),
                waits=([w for w in wx if w] if ci == 0 else []), inc=chCx1[buf])
            p.emit("sync", lambda e, gc=gc, ci=ci, buf=buf: e.dma_start(
                x2c[ci], x2_sp[gc]),
                waits=[prev_marks[("x2c", ci)]] if ("x2c", ci) in prev_marks
                else [], inc=chCx2[ci])
            p.emit("sync", lambda e, gc=gc, ci=ci: e.dma_start(
                xc[ci], x_d[gc // 16, (gc % 16) * 128:(gc % 16) * 128 + 128, :]),
                waits=[prev_marks[("xc", ci)]] if ("xc", ci) in prev_marks
                else [], inc=chCxc[ci])
        nload = [(chCx1[buf], chCx1[buf].n), (chCx2[0], chCx2[0].n),
                 (chCx2[1], chCx2[1].n), (chCxc[0], chCxc[0].n),
                 (chCxc[1], chCxc[1].n)]
        for hf in range(2):
            for hb in range(16):
                hbg = hf * 16 + hb
                bm = [bE, bF_][hbg % 2]
                w0 = bm.produce_waits() + nload
                for kb in range(8):
                    p.emit("tensor", lambda e, kb=kb, hbg=hbg, bm=bm, buf=buf: e.matmul(
                        bm.t[:, 0:256],
                        w1_sb[:, kb * 4096 + hbg * 128:kb * 4096 + (hbg + 1) * 128],
                        x1g[buf][:, kb * 256:(kb + 1) * 256],
                        start=(kb == 0), stop=(kb == 7)),
                        waits=(w0 if kb == 0 else ()),
                        inc=(bm.produced() if kb == 7 else None))
                wm = bm.consume_waits()
                p.emit("scalar", lambda e, hbg=hbg, bm=bm: e.activation(
                    hTh[:, hbg * 256:(hbg + 1) * 256], bm.t[:, 0:256], AF.Gelu,
                    bias=gelub[:, hbg:hbg + 1]), waits=wm, inc=bm.released())
        mh = p.mark("scalar")
        ob = {(0, 0): bA, (0, 1): bB, (1, 0): bC, (1, 1): bD_}
        for ci in range(2):
            for fh in range(2):
                bk_ = ob[(ci, fh)]
                w0 = bk_.produce_waits() + [mh]
                for hbg in range(32):
                    p.emit("tensor", lambda e, ci=ci, fh=fh, hbg=hbg,
                           bk_=bk_: e.matmul(
                        bk_.t[:, 0:512],
                        hTh[:, hbg * 256 + ci * 128:hbg * 256 + (ci + 1) * 128],
                        w2_sb[:, hbg * 1024 + fh * 512:hbg * 1024 + (fh + 1) * 512],
                        start=(hbg == 0), stop=(hbg == 31)),
                        waits=(w0 if hbg == 0 else ()),
                        inc=(bk_.produced() if hbg == 31 else None))
        mrow_marks = []
        for ci, gc in enumerate(gcs):
            for fh in range(2):
                bk_ = ob[(ci, fh)]
                wk_ = bk_.consume_waits()
                p.emit("vector", lambda e, ci=ci, fh=fh, bk_=bk_: e.tensor_tensor(
                    ypre[ci][:, fh * 512:(fh + 1) * 512], bk_.t[:, 0:512],
                    b2bc[:, fh * 512:(fh + 1) * 512], ALU.add), waits=wk_,
                    inc=bk_.released())
            if gc == 1:
                tap("ypre0", ypre[1], [p.mark("vector")])
            for fh in range(2):
                p.emit("vector", lambda e, ci=ci, fh=fh: e.tensor_tensor(
                    ypre[ci][:, fh * 512:(fh + 1) * 512],
                    ypre[ci][:, fh * 512:(fh + 1) * 512],
                    x2c[ci][:, fh * 512:(fh + 1) * 512], ALU.add))
                p.emit("vector", lambda e, ci=ci, fh=fh: e.tensor_tensor(
                    ypre[ci][:, fh * 512:(fh + 1) * 512],
                    ypre[ci][:, fh * 512:(fh + 1) * 512],
                    xc[ci][:, fh * 512:(fh + 1) * 512], ALU.add))
            mypre = p.mark("vector")
            if gc == 1:
                tap("ypre", ypre[ci], [mypre])
            prev_marks[("xc", ci)] = mypre
            p.emit("scalar", lambda e, ci=ci: e.activation(xsq2, ypre[ci],
                                                           AF.Square),
                   waits=[mypre])
            mS = p.mark("scalar")
            p.emit("vector", lambda e, ci=ci: e.reduce_sum(
                stat2[:, 0:1], ypre[ci].rearrange("p (a f) -> p a f", a=1),
                axis=AX.X), waits=[mypre])
            p.emit("vector", lambda e: e.reduce_sum(
                stat2[:, 1:2], xsq2.rearrange("p (a f) -> p a f", a=1),
                axis=AX.X), waits=[mS])
            p.emit("vector", lambda e: e.tensor_scalar(stat2[:, 2:3],
                                                       stat2[:, 0:1], 1.0 / D,
                                                       None, ALU.mult))
            p.emit("vector", lambda e: e.tensor_scalar(stat2[:, 3:4],
                                                       stat2[:, 1:2], 1.0 / D,
                                                       None, ALU.mult))
            p.emit("vector", lambda e: e.tensor_tensor(stat2[:, 4:5],
                                                       stat2[:, 2:3],
                                                       stat2[:, 2:3], ALU.mult))
            p.emit("vector", lambda e: e.tensor_tensor(stat2[:, 5:6],
                                                       stat2[:, 3:4],
                                                       stat2[:, 4:5],
                                                       ALU.subtract))
            mV = p.mark("vector")
            p.emit("scalar", lambda e: e.activation(stat2[:, 6:7], stat2[:, 5:6],
                                                    AF.Ln, bias=1e-5),
                   waits=[mV])
            p.emit("scalar", lambda e: e.activation(stat2[:, 7:8], stat2[:, 6:7],
                                                    AF.Exp, scale=-0.5))
            mR = p.mark("scalar")
            p.emit("vector", lambda e, ci=ci: e.tensor_scalar(
                x2c[ci], ypre[ci], stat2[:, 2:3], stat2[:, 7:8],
                ALU.subtract, ALU.mult), waits=[mR])
            my = p.mark("vector")
            p.emit("sync", lambda e, ci=ci, gc=gc: e.dma_start(
                y_sp[gc], x2c[ci]), waits=[my], inc=p.chD)
            p.emit("scalar", lambda e, ci=ci: e.copy(ybf, x2c[ci]), waits=[my])
            myb = p.mark("scalar")
            prev_marks[("x2c", ci)] = myb
            T0 = BK["T0"]
            w0 = T0.produce_waits() + [myb]
            for kb in range(8):
                p.emit("tensor", lambda e, kb=kb: e.transpose(
                    T0.t[:, kb * 128:(kb + 1) * 128],
                    ybf[:, kb * 128:(kb + 1) * 128], IDB),
                    waits=(w0 if kb == 0 else ()),
                    inc=(T0.produced() if kb == 7 else None))
            wT0 = T0.consume_waits()
            p.emit("vector", lambda e: e.tensor_copy(yT_s, T0.t[:, :]),
                   waits=wT0, inc=T0.released())
            myT = p.mark("vector")
            p.emit("sync", lambda e, gc=gc: e.dma_start(
                yT_sp[gc].rearrange("kb p t -> p kb t"),
                yT_s.rearrange("p (kb t) -> p kb t", kb=8)),
                waits=[myT], inc=p.chD)
            w0 = bE.produce_waits() + [myT]
            for kb in range(8):
                p.emit("tensor", lambda e, kb=kb: e.matmul(
                    bE.t[:, 0:8], yT_s[:, kb * 128:(kb + 1) * 128],
                    wrT[:, kb * 8:(kb + 1) * 8], start=(kb == 0),
                    stop=(kb == 7)),
                    waits=(w0 if kb == 0 else ()),
                    inc=(bE.produced() if kb == 7 else None))
            wE = bE.consume_waits()
            p.emit("vector", lambda e: e.reduce_max(rsm[:, 0:1], bE.t[:, 0:8],
                                                    axis=AX.X), waits=wE)
            p.emit("vector", lambda e: e.tensor_scalar(rsm[:, 1:2],
                                                       rsm[:, 0:1], -1.0, None,
                                                       ALU.mult))
            mneg = p.mark("vector")
            p.emit("scalar", lambda e: e.activation(rsm[:, 8:16], bE.t[:, 0:8],
                                                    AF.Exp, bias=rsm[:, 1:2]),
                   waits=[mneg], inc=bE.released())
            msc = p.mark("scalar")
            p.emit("vector", lambda e: e.reduce_max(rsm[:, 2:3], rsm[:, 8:16],
                                                    axis=AX.X), waits=[msc])
            p.emit("vector", lambda e: e.tensor_single_scalar(
                zrow[:, 0:8], rsm[:, 8:16], rsm[:, 2:3], ALU.is_equal))
            p.emit("vector", lambda e: e.tensor_tensor(
                zrow[:, 8:16], zrow[:, 0:8], rsm[:, 8:16], ALU.mult))
            p.emit("vector", lambda e: e.tensor_tensor(
                zrow[:, 8:16], rsm[:, 8:16], zrow[:, 8:16], ALU.subtract))
            p.emit("vector", lambda e: e.reduce_max(rsm[:, 4:5], zrow[:, 8:16],
                                                    axis=AX.X))
            p.emit("vector", lambda e: e.tensor_single_scalar(
                zrow[:, 0:8], rsm[:, 8:16], rsm[:, 4:5], ALU.is_ge))
            p.emit("vector", lambda e: e.tensor_tensor(zrow[:, 0:8],
                                                       zrow[:, 0:8],
                                                       rsm[:, 8:16], ALU.mult))
            p.emit("vector", lambda e: e.reduce_sum(
                rsm[:, 5:6], zrow[:, 0:8].rearrange("p (a f) -> p a f", a=1),
                axis=AX.X))
            p.emit("vector", lambda e: e.reciprocal(rsm[:, 6:7], rsm[:, 5:6]))
            p.emit("vector", lambda e, gc=gc: e.tensor_scalar(
                cw_all[:, gc * 8:(gc + 1) * 8], zrow[:, 0:8], rsm[:, 6:7],
                None, ALU.mult))
            if gc == NCH - 1:
                tap("cwt", cw_all[:, 0:8 * NCH], [p.mark("vector")])
            if gc == 0:
                mrt = p.mark("vector")
                tap("rsm", rsm[:, 0:16], [mrt])
                tap("zrow", zrow[:, 0:16], [mrt])
        prev_marks[("x1g", buf)] = p.mark("tensor")
        p.barrier3()


def phase_D(nc, p, ar, BK, env):
    bA, bB, bC, bD_, bE, bF_ = (BK[k] for k in "ABCDEF")
    pAB, pCD, pEF = BK["AB"], BK["CD"], BK["EF"]
    w_in, y_sp, yT_sp, out_d = env["w_in"], env["y_sp"], env["yT_sp"], env["out_d"]
    cw_all, tap = env["cw_all"], env["tap"]
    chYT, chYs, chWp = env["chYT"], env["chYs"], env["chWp"]
    p.barrier()
    ar.pop()
    ar.push()
    w1p = [ar.bf(8192), ar.bf(8192)]
    w2p = [ar.bf(8192), ar.bf(8192)]
    yTg = [ar.bf(2048) for _ in range(4)]
    hTg = [ar.bf(2048), ar.bf(2048)]
    accq = ar.f32(8192)
    yst = [ar.f32(D), ar.f32(D)]
    outt = ar.f32(D)
    xsq3 = ar.f32(D)
    stat3 = ar.f32(8)
    assert ar.off <= ar.cols, ar.off
    p.barrier()

    wprev = {}
    for q in range(4):
        p.emit("vector", lambda e: e.memset(accq[:, 0:4096], 0.0),
               waits=[wprev["acc"]] if "acc" in wprev else [])
        p.emit("vector", lambda e: e.memset(accq[:, 4096:8192], 0.0))
        macc0 = p.mark("vector")
        for ee in range(NE):
            for pan in range(4):
                pb = (ee * 4 + pan) % 2
                wwait = [wprev[("w", pb)]] if ("w", pb) in wprev else []
                src = w_in["W1e"][ee].rearrange(
                    "(kb p) (pan hh t) -> pan p kb hh t", p=128, pan=4, t=128)[pan]
                dst = w1p[pb].rearrange("p (kb hh t) -> p kb hh t", kb=8, t=128)
                p.emit("gpsimd", lambda e, src=src, dst=dst: e.dma_start(dst, src),
                       waits=wwait, inc=chWp[pb])
                src2 = w_in["W2e"][ee].rearrange(
                    "(pan hh p) f -> pan p hh f", p=128, pan=4)[pan]
                dst2 = w2p[pb].rearrange("p (hh f) -> p hh f", hh=8)
                p.emit("gpsimd", lambda e, src2=src2, dst2=dst2: e.dma_start(
                    dst2, src2), inc=chWp[pb])
                nw = (chWp[pb], chWp[pb].n)
                for grp in range(4):
                    gc = q * 8 + grp * 2
                    gb = grp
                    if ee == 0 and pan == 0:
                        ywait = [wprev[("y", gb)]] if ("y", gb) in wprev else []
                        for ci in range(2):
                            p.emit("sync", lambda e, gc=gc, ci=ci, gb=gb: e.dma_start(
                                yTg[gb].rearrange("p (kb c t) -> p kb c t",
                                                  kb=8, c=2)[:, :, ci, :],
                                yT_sp[gc + ci].rearrange("kb p t -> p kb t")),
                                waits=(ywait if ci == 0 else []), inc=chYT[gb])
                    ny = (chYT[gb], chYT[gb].n)
                    hb_ = hTg[grp % 2]
                    hwait = [wprev[("h", grp % 2)]] if ("h", grp % 2) in wprev else []
                    it = ((q * NE + ee) * 4 + pan) * 4 + grp
                    PR = [pEF, pAB, pCD]
                    ra, rb, rc = PR[it % 3], PR[(it + 1) % 3], PR[(it + 2) % 3]
                    mg = [None, None]
                    for hh4 in range(2):
                        bm = [ra, rb][hh4]
                        w0 = bm.produce_waits() + [nw, ny] + (hwait if hh4 == 0 else [])
                        for sub in range(4):
                            hh = hh4 * 4 + sub
                            for kb in range(8):
                                p.emit("tensor", lambda e, kb=kb, hh=hh, sub=sub,
                                       bm=bm, pb=pb, gb=grp: e.matmul(
                                    bm.t[:, sub * 256:(sub + 1) * 256],
                                    w1p[pb][:, kb * 1024 + hh * 128:kb * 1024 + (hh + 1) * 128],
                                    yTg[gb][:, kb * 256:(kb + 1) * 256],
                                    start=(kb == 0), stop=(kb == 7)),
                                    waits=(w0 if sub == 0 and kb == 0 else ()),
                                    inc=(bm.produced() if (sub == 3 and kb == 7)
                                         else None))
                        wm = bm.consume_waits()
                        p.emit("scalar", lambda e, hh4=hh4, bm=bm, hb_=hb_: e.activation(
                            hb_[:, hh4 * 1024:(hh4 + 1) * 1024],
                            bm.t[:, 0:1024], AF.Gelu), waits=wm,
                            inc=bm.released())
                        mg[hh4] = p.mark("scalar")
                    for ci in range(2):
                        gcc = gc + ci
                        bk_ = [rc, ra][ci]
                        w0 = bk_.produce_waits() + [mg[0]]
                        for fh in range(2):
                            for hh in range(8):
                                p.emit("tensor", lambda e, ci=ci, fh=fh, hh=hh,
                                       bk_=bk_, pb=pb, hb_=hb_: e.matmul(
                                    bk_.t[:, fh * 512:(fh + 1) * 512],
                                    hb_[:, hh * 256 + ci * 128:hh * 256 + (ci + 1) * 128],
                                    w2p[pb][:, hh * 1024 + fh * 512:hh * 1024 + (fh + 1) * 512],
                                    start=(hh == 0), stop=(hh == 7)),
                                    waits=(w0 if fh == 0 and hh == 0 else
                                           ([mg[1]] if fh == 0 and hh == 4
                                            else ())),
                                    inc=(bk_.produced() if (fh == 1 and hh == 7)
                                         else None))
                        wk_ = bk_.consume_waits()
                        p.emit("vector", lambda e, ci=ci, gcc=gcc, ee=ee, grp=grp,
                               bk_=bk_: e.scalar_tensor_tensor(
                            accq[:, (grp * 2 + ci) * 1024:(grp * 2 + ci + 1) * 1024],
                            bk_.t[:, 0:1024],
                            cw_all[:, gcc * 8 + ee:gcc * 8 + ee + 1],
                            accq[:, (grp * 2 + ci) * 1024:(grp * 2 + ci + 1) * 1024],
                            ALU.mult, ALU.add),
                            waits=wk_ + ([macc0] if ee == 0 and pan == 0
                                         else []), inc=bk_.released())
                    wprev[("h", grp % 2)] = p.mark("tensor")
                wprev[("w", pb)] = p.mark("tensor")
        mq = p.mark("vector")
        if q == 0:
            tap("accq", accq[:, 0:8192], [mq])
        mqP = p.mark("tensor")
        for gb in range(4):
            wprev[("y", gb)] = mqP
        for ci8 in range(8):
            gc = q * 8 + ci8
            sb = ci8 % 2
            ywait = [wprev[("ys", sb)]] if ("ys", sb) in wprev else []
            p.emit("sync", lambda e, gc=gc, sb=sb: e.dma_start(yst[sb], y_sp[gc]),
                   waits=ywait + ([mq] if ci8 < 2 else []), inc=chYs[sb])
            nyl = (chYs[sb], chYs[sb].n)
            p.emit("vector", lambda e, ci8=ci8, sb=sb: e.tensor_tensor(
                outt, yst[sb], accq[:, ci8 * 1024:(ci8 + 1) * 1024], ALU.add),
                waits=[nyl, mq])
            mo = p.mark("vector")
            p.emit("scalar", lambda e: e.activation(xsq3, outt, AF.Square),
                   waits=[mo])
            mS = p.mark("scalar")
            p.emit("vector", lambda e: e.reduce_sum(
                stat3[:, 0:1], outt.rearrange("p (a f) -> p a f", a=1),
                axis=AX.X))
            p.emit("vector", lambda e: e.reduce_sum(
                stat3[:, 1:2], xsq3.rearrange("p (a f) -> p a f", a=1),
                axis=AX.X), waits=[mS])
            p.emit("vector", lambda e: e.tensor_scalar(
                stat3[:, 2:3], stat3[:, 0:1], 1.0 / D, None, ALU.mult))
            p.emit("vector", lambda e: e.tensor_scalar(
                stat3[:, 3:4], stat3[:, 1:2], 1.0 / D, None, ALU.mult))
            p.emit("vector", lambda e: e.tensor_tensor(
                stat3[:, 4:5], stat3[:, 2:3], stat3[:, 2:3], ALU.mult))
            p.emit("vector", lambda e: e.tensor_tensor(
                stat3[:, 5:6], stat3[:, 3:4], stat3[:, 4:5], ALU.subtract))
            mV = p.mark("vector")
            p.emit("scalar", lambda e: e.activation(stat3[:, 6:7], stat3[:, 5:6],
                                                    AF.Ln, bias=1e-5),
                   waits=[mV])
            p.emit("scalar", lambda e: e.activation(stat3[:, 7:8], stat3[:, 6:7],
                                                    AF.Exp, scale=-0.5))
            mR = p.mark("scalar")
            p.emit("vector", lambda e, sb=sb: e.tensor_scalar(
                yst[sb], outt, stat3[:, 2:3], stat3[:, 7:8], ALU.subtract,
                ALU.mult), waits=[mR])
            mfin = p.mark("vector")
            wprev[("ys", sb)] = mfin
            p.emit("sync", lambda e, gc=gc, sb=sb: e.dma_start(
                out_d[gc // 16, (gc % 16) * 128:(gc % 16) * 128 + 128, :],
                yst[sb]), waits=[mfin], inc=p.chD)
        wprev["acc"] = p.mark("vector")


_PROG_CACHE = {}


def kernel(**inputs):
    cstf_np, cstb_np = make_consts()
    x = np.ascontiguousarray(np.asarray(inputs["L0_attn_x"], np.float32))
    names = dict(
        Wq="L0_attn_Wq", Wk="L0_attn_Wk", Wv="L0_attn_Wv", Wb="L0_attn_Wb",
        Wg="L0_attn_Wg", Wo="L0_attn_Wo", bo="L0_attn_bo", ln1g="L0_attn_ln1_g",
        ln1b="L0_attn_ln1_b", ln2g="L0_attn_ln2_g", ln2b="L0_attn_ln2_b",
        w1="L0_attn_w1", b1="L0_attn_b1", w2="L0_attn_w2", b2="L0_attn_b2",
        Wr="L0_ffn_Wr", W1e="L0_ffn_W1", W2e="L0_ffn_W2")
    wmap = {k: np.ascontiguousarray(np.asarray(inputs[v], np.float32))
            for k, v in names.items()}
    key = "full"
    if key not in _PROG_CACHE:
        _PROG_CACHE[key] = build_program()
    nc = _PROG_CACHE[key]
    in_maps = []
    for core in range(8):
        m = {"x": x[2 * core:2 * core + 2], "cstf": cstf_np, "cstb": cstb_np}
        m.update(wmap)
        in_maps.append(m)
    res = run_bass_kernel_spmd(nc, in_maps, core_ids=list(range(8)))
    out = np.concatenate([r["out"] for r in res.results], axis=0)
    return out.reshape(16, 2048, D).astype(np.float32)


# version 09
# speedup vs baseline: 1.1842x; 1.0598x over previous
import contextlib
import numpy as np
import ml_dtypes
import concourse.bass as bass
import concourse.mybir as mybir
from concourse.bass_utils import run_bass_kernel_spmd

F32 = mybir.dt.float32
BF16 = mybir.dt.bfloat16
AF = mybir.ActivationFunctionType
ALU = mybir.AluOpType
AX = mybir.AxisListType

D = 1024
H = 16
NCH = 32
NLEV = 6
HID = 4096
NE = 8
ENGS = ("sync", "gpsimd", "tensor", "vector", "scalar")


class Ch:
    def __init__(self, stack, nc, name, step=1):
        self.sem = stack.enter_context(nc.semaphore(name))
        self.step = step
        self.n = 0


class Bank:
    def __init__(self, stack, nc, name, dtype, view=None):
        if view is None:
            cols = 1024 if dtype == BF16 else 512
            self.t = stack.enter_context(nc.psum_tensor(name, [128, cols], dtype))
        else:
            self.t = view
        self.full = Ch(stack, nc, name + "f")
        self.free = Ch(stack, nc, name + "e")
        self.k = 0
        self.state = "idle"

    def produce_waits(self):
        assert self.state == "idle", self.state
        self.state = "prod"
        return [(self.free, self.k)]

    def produced(self):
        assert self.state == "prod"
        self.state = "full"
        return self.full

    def consume_waits(self):
        assert self.state == "full"
        self.state = "cons"
        return [(self.full, self.k + 1)]

    def released(self):
        assert self.state == "cons"
        self.state = "idle"
        self.k += 1
        return self.free


class Plan:
    def __init__(self, stack, nc):
        self.nc = nc
        self.ops = {e: [] for e in ENGS}
        self.lastinc = {e: None for e in ENGS}
        self.lastsync = {e: None for e in ENGS}
        self.chained = ("vector", "scalar")
        self.chD = Ch(stack, nc, "chD", 16)
        self.chW = Ch(stack, nc, "chW", 16)
        self.chP = Ch(stack, nc, "chP")
        self.chV = Ch(stack, nc, "chV")
        self.chS = Ch(stack, nc, "chS")
        self.chDm = Ch(stack, nc, "chDm")
        self.chWm = Ch(stack, nc, "chWm")
        self.engch = {"sync": self.chDm, "gpsimd": self.chWm, "tensor": self.chP,
                      "vector": self.chV, "scalar": self.chS}
        self.bk = {}
        self.praw = [stack.enter_context(
            nc.psum_tensor(f"pp{i}", [128, 1024], F32)) for i in range(3)]
        for i, nm in enumerate(("A", "B", "C", "D", "E", "F")):
            v = self.praw[i // 2][:, (i % 2) * 512:(i % 2 + 1) * 512]
            self.bk[nm] = Bank(stack, nc, "pb" + nm, F32, view=v)
        for i, nm in enumerate(("AB", "CD", "EF")):
            self.bk[nm] = Bank(stack, nc, "pp" + nm, F32,
                               view=self.praw[i][:, :])
        for nm in ("T0", "T1"):
            self.bk[nm] = Bank(stack, nc, "pb" + nm, BF16)

    def emit(self, eng, build, waits=(), inc=None):
        incs = []
        if inc is not None:
            for ch in (inc if isinstance(inc, (list, tuple)) else [inc]):
                if ch is None:
                    continue
                ch.n += 1
                incs.append(ch)
        assert len(incs) <= 1, "one semaphore update per instruction"
        wl = [(ch, int(n)) for (ch, n) in waits if ch is not None and n > 0]
        if eng in self.chained:
            if not incs:
                ech = self.engch[eng]
                ech.n += 1
                incs.append(ech)
            if self.lastsync[eng] is not None:
                wl.append(self.lastsync[eng])
            self.lastsync[eng] = (incs[0], incs[0].n)

        def fn(e, build=build, wl=wl, incs=incs):
            for ch, n in wl:
                e.wait_ge(ch.sem, ch.step * n)
            ins = build(e)
            for ch in incs:
                ins.then_inc(ch.sem, ch.step)
            return ins
        self.ops[eng].append(fn)
        self.lastinc[eng] = (incs[0], incs[0].n) if incs else None

    def mark(self, eng):
        assert eng in ("tensor", "vector", "scalar")
        ch = self.engch[eng]
        ch.n += 1
        tgt = ch.n
        if self.lastinc.get(eng):
            ch.n -= 1
            return self.lastinc[eng]
        if self.ops[eng]:
            last = self.ops[eng].pop()

            def fn(e, last=last, ch=ch):
                ins = last(e)
                ins.then_inc(ch.sem, ch.step)
                return ins
        else:
            def fn(e, ch=ch):
                ins = e.nop()
                ins.then_inc(ch.sem, ch.step)
                return ins
        self.ops[eng].append(fn)
        self.lastinc[eng] = (ch, tgt)
        return (ch, tgt)

    def barrier(self):
        marks = [self.mark(e) for e in ("tensor", "vector", "scalar")]
        marks += [(self.chD, self.chD.n), (self.chW, self.chW.n)]
        for e in ENGS:
            def fn(en, marks=list(marks)):
                ins = None
                for ch, n in marks:
                    if n > 0:
                        ins = en.wait_ge(ch.sem, ch.step * n)
                return ins
            self.ops[e].append(fn)
            self.lastinc[e] = None

    def barrier3(self):
        es = ("tensor", "vector", "scalar")
        marks = [self.mark(e) for e in es]
        for e in es:
            def fn(en, marks=list(marks)):
                ins = None
                for ch, n in marks:
                    ins = en.wait_ge(ch.sem, ch.step * n)
                return ins
            self.ops[e].append(fn)
            self.lastinc[e] = None

    def realize(self, block):
        for e in ENGS:
            lst = self.ops[e]

            def run(engine, lst=lst):
                for fn in lst:
                    fn(engine)
            getattr(block, e)(run)


class Arena:
    def __init__(self, stack, nc, cols):
        self.base = stack.enter_context(nc.sbuf_tensor("arena", [128, cols], BF16))
        self.cols = cols
        self.off = 0
        self.marks = []

    def bf(self, n):
        self.off = (self.off + 15) // 16 * 16
        assert self.off + n <= self.cols, f"arena bf overflow {self.off}+{n}"
        v = self.base[:, self.off:self.off + n]
        self.off += n
        return v

    def f32(self, n):
        return self.bf(2 * n).bitcast(F32)

    def push(self):
        self.marks.append(self.off)

    def pop(self):
        self.off = self.marks.pop()


def bc(view, parts, hcount, rep=64, h0=0):
    return bass.AP(view.tensor, view.offset + h0,
                   [[view.ap[0][0], parts], [1, hcount], [0, rep]])


def make_consts():
    ident = np.eye(128, dtype=np.float32)
    triu_incl = np.triu(np.ones((128, 128), np.float32), 0)
    stril = np.tril(np.ones((128, 128), np.float32), -1)
    cols = np.zeros((128, 16), np.float32)
    cols[:64, 0] = 1.0
    cols[:, 1] = 1.0
    cvals = np.zeros((128, 16), np.float32)
    cvals[:, 0] = 1e-5
    cvals[:, 1] = 1.0
    cst = np.concatenate([ident, triu_incl, np.tile(stril, (1, 4)),
                          np.tile(triu_incl, (1, 4)), np.tile(ident, (1, 4)),
                          cols, cvals], axis=1)
    return np.ascontiguousarray(cst, np.float32), np.eye(128).astype(ml_dtypes.bfloat16)


def build_program(n_chunks=NCH, stop_after="D", taps=(), tap_chunk=0, blocs=(0, 1), b_sub="full", dbg_spill=False):
    nc = bass.Bass("TRN2", target_bir_lowering=False)
    stack = contextlib.ExitStack()
    taps = set(taps)

    x_d = nc.dram_tensor("x", [2, 2048, D], F32, kind="ExternalInput")
    w_in = {}
    for nm, shp in (("Wq", [D, D]), ("Wk", [D, D]), ("Wv", [D, D]),
                    ("Wb", [D, H]), ("Wg", [D, H]), ("Wo", [D, D]),
                    ("bo", [D]), ("ln1g", [D]), ("ln1b", [D]), ("ln2g", [D]),
                    ("ln2b", [D]), ("w1", [D, HID]), ("b1", [HID]),
                    ("w2", [HID, D]), ("b2", [D]), ("Wr", [NE, D]),
                    ("W1e", [NE, D, HID]), ("W2e", [NE, HID, D])):
        w_in[nm] = nc.dram_tensor(nm, shp, F32, kind="ExternalInput")
    cstf_d = nc.dram_tensor("cstf", [128, 1824], F32, kind="ExternalInput")
    cstb_d = nc.dram_tensor("cstb", [128, 128], BF16, kind="ExternalInput")
    out_d = nc.dram_tensor("out", [2, 2048, D], F32, kind="ExternalOutput")
    spk = "ExternalOutput" if dbg_spill else "Internal"
    spk_in = "ExternalInput" if dbg_spill == "input" else spk
    x2_sp = nc.dram_tensor("x2_sp", [NCH, 128, D], F32, kind=spk_in)
    x1T_sp = nc.dram_tensor("x1T_sp", [NCH, 8, 128, 128], BF16, kind=spk_in)
    y_sp = nc.dram_tensor("y_sp", [NCH, 128, D], F32,
                          kind=("ExternalOutput" if dbg_spill else "Internal"))
    yT_sp = nc.dram_tensor("yT_sp", [NCH, 8, 128, 128], BF16,
                           kind=("ExternalOutput" if dbg_spill else "Internal"))

    tap_d = {}
    TAPS_BF = {"qt": [128, 2048], "tall": [128, 4096], "attnT": [128, 4096],
               "khatT": [128, 4096], "wt": [128, 4096], "u0b": [128, 2048],
               "zp": [128, 2048], "khat": [128, D], "vtm": [128, D],
               "x1T": [128, D], "yT": [128, D], "hh": [128, 4096], "x1gd": [128, 2048], "attnout": [128, D]}
    TAPS_F = {"beta": [128, H], "L": [128, H], "gamma": [128, H],
              "x2": [128, D], "y": [128, D],
              "cwt": [128, 8 * NCH], "ktmf": [128, D], "S0": [128, 2048], "logg": [128, H], "Lt": [128, H], "rows": [1, 48], "gelub": [128, 32], "ypre0": [128, D], "accq": [128, 8192], "rsm": [128, 16], "zrow": [128, 16],
              "ypre": [128, D]}
    for t in taps:
        if t in TAPS_BF:
            tap_d[t] = nc.dram_tensor("tap_" + t, TAPS_BF[t], BF16,
                                      kind="ExternalOutput")
        else:
            tap_d[t] = nc.dram_tensor("tap_" + t, TAPS_F[t], F32,
                                      kind="ExternalOutput")

    ar = Arena(stack, nc, 103424)
    p = Plan(stack, nc)
    BK = p.bk
    chX = [Ch(stack, nc, f"chX{i}", 16) for i in range(2)]
    chSp = Ch(stack, nc, "chSp", 16)
    chCx1 = [Ch(stack, nc, f"chCx1{i}", 16) for i in range(2)]
    chCx2 = [Ch(stack, nc, f"chCx2{i}", 16) for i in range(2)]
    chCxc = [Ch(stack, nc, f"chCxc{i}", 16) for i in range(2)]
    chYT = [Ch(stack, nc, f"chYT{i}", 16) for i in range(4)]
    chYs = [Ch(stack, nc, f"chYs{i}", 16) for i in range(2)]
    chWp = [Ch(stack, nc, f"chWp{i}", 16) for i in range(2)]
    bA, bB, bC, bD_, bE, bF_ = (BK[k] for k in "ABCDEF")

    def tap(name, view, waits):
        if name in tap_d:
            pr, fr = view.shape
            p.emit("sync", lambda e, v=view, n=name, pr=pr, fr=fr: e.dma_start(
                tap_d[n][0:pr, 0:fr], v), waits=waits, inc=p.chD)

    cstf = ar.f32(1824)
    cstb = ar.bf(128)
    IDF = cstf[:, 0:128]
    TRIU = cstf[:, 128:256]
    STRIL4 = cstf[:, 256:768]
    TRIU4 = cstf[:, 768:1280]
    IDENT4 = cstf[:, 1280:1792]
    MASKCOL = cstf[:, 1792:1793]
    ONESCOL = cstf[:, 1793:1794]
    ONESROW = TRIU[0:1, 0:128]
    IDB = cstb[:, 0:128]
    nc.const_aps.register(F32, 1e-5, cstf[:, 1808:1809])
    nc.const_aps.register(F32, 1.0, cstf[:, 1809:1810])
    b2bc = ar.f32(D)
    wrT = ar.bf(64)
    cw_all = ar.f32(8 * NCH)
    g2c = ar.f32(8)
    b2lcb = ar.bf(8)
    ar.push()

    wq_sb = ar.bf(8192)
    wk_sb = ar.bf(8192)
    wv_sb = ar.bf(8192)
    wo_sb = ar.bf(8192)
    wb_sb = ar.bf(128)
    wg_sb = ar.bf(128)
    g1c = ar.f32(8)
    b1lcb = ar.bf(8)
    bqc = ar.f32(8)
    bkbc = ar.f32(D)
    bvbc = ar.f32(D)
    bbbc = ar.f32(H)
    bgbc = ar.f32(H)
    bobc = ar.f32(D)
    rowst = ar.f32(576)
    S0 = ar.f32(2048)
    xt = [ar.f32(D) for _ in range(2)]
    stat = ar.f32(8)
    xnb = ar.bf(D)
    x1T = ar.bf(D)
    qt_b = [ar.bf(2048) for _ in range(2)]
    khat_b = [ar.bf(D) for _ in range(2)]
    vtm_b = [ar.bf(D) for _ in range(2)]
    zbf = ar.f32(H)
    zgf = ar.f32(H)
    beta_s = [ar.f32(H) for _ in range(2)]
    zmin = ar.f32(H)
    zab = ar.f32(H)
    logg_s = ar.f32(H)
    L_s = ar.f32(H)
    Lt_s = ar.f32(H)
    gam = [ar.f32(H) for _ in range(2)]
    vbsc = [ar.f32(H) for _ in range(2)]
    eLm = [ar.f32(H) for _ in range(2)]
    gCh = [ar.f32(H) for _ in range(2)]
    gC = [ar.f32(H) for _ in range(2)]
    knss = ar.f32(H)
    knsc = ar.f32(H)
    nsb = ar.bf(1024)
    sqa = ar.bf(1024)
    fTp = [ar.bf(1024) for _ in range(NLEV + 1)]
    tall = ar.bf(4096)
    attnT = ar.bf(4096)
    khatT = ar.bf(4096)
    wT = ar.bf(4096)
    ar.push()
    vecst = ar.f32(512)
    wr_raw = ar.f32(1024)
    rows2 = ar.f32(2048)
    ar.pop()
    u0b = ar.bf(2048)
    zp = ar.bf(2048)
    stmp = ar.f32(1024)
    stmpw = stmp
    abf = ar.bf(D)
    aT = ar.bf(D)
    x2t = ar.f32(D)
    x1t_sb = x1T
    xsq = zp.bitcast(F32)
    ktm_f = x2t
    import sys
    print("ARENA scan off:", ar.off, "of", ar.cols, file=sys.stderr)
    assert ar.off <= ar.cols, ar.off

    p.emit("sync", lambda e: e.dma_start(cstf, cstf_d[:, :]), inc=p.chD)
    p.emit("sync", lambda e: e.dma_start(cstb, cstb_d[:, :]), inc=p.chD)
    for w_sb, nm, ncol in ((wq_sb, "Wq", D), (wk_sb, "Wk", D), (wv_sb, "Wv", D),
                           (wo_sb, "Wo", D), (wb_sb, "Wb", H), (wg_sb, "Wg", H)):
        if ncol == D:
            src = w_in[nm].rearrange("(kb p) (nb t) -> p kb nb t", p=128, t=128)
            dst = w_sb.rearrange("p (kb nb t) -> p kb nb t", kb=8, t=128)
        else:
            src = w_in[nm].rearrange("(kb p) t -> p kb t", p=128)
            dst = w_sb.rearrange("p (kb t) -> p kb t", kb=8)
        p.emit("gpsimd", lambda e, d=dst, s=src: e.dma_start(d, s), inc=p.chW)
    nW_w = p.chW.n
    for i, nm in enumerate(("ln1g", "ln1b", "ln2g", "ln2b")):
        p.emit("sync", lambda e, i=i, nm=nm: e.dma_start(
            vecst[0:8, i * 128:(i + 1) * 128],
            w_in[nm].rearrange("(r t) -> r t", r=8)), inc=p.chD)
    p.emit("sync", lambda e: e.dma_start(
        rows2[0:1, 0:D], w_in["bo"].rearrange("(r f) -> r f", r=1)), inc=p.chD)
    p.emit("sync", lambda e: e.dma_start(
        rows2[0:1, D:2 * D], w_in["b2"].rearrange("(r f) -> r f", r=1)),
        inc=p.chD)
    p.emit("sync", lambda e: e.dma_start(wr_raw[0:8, 0:512],
                                         w_in["Wr"][:, 0:512]), inc=p.chD)
    p.emit("sync", lambda e: e.dma_start(wr_raw[0:8, 512:1024],
                                         w_in["Wr"][:, 512:1024]), inc=p.chD)
    nD_A = p.chD.n

    w0 = bA.produce_waits() + [(p.chD, nD_A), (p.chW, nW_w)]
    for i in range(4):
        p.emit("tensor", lambda e, i=i: e.matmul(
            bA.t[:, i * 8:(i + 1) * 8], vecst[0:8, i * 128:(i + 1) * 128],
            IDF[0:8, 0:8], start=True, stop=True),
            waits=(w0 if i == 0 else ()), inc=(bA.produced() if i == 3 else None))
    wA = bA.consume_waits()
    p.emit("vector", lambda e: e.tensor_copy(g1c[:, 0:8], bA.t[:, 0:8]), waits=wA)
    p.emit("vector", lambda e: e.tensor_copy(b1lcb[:, 0:8], bA.t[:, 8:16]))
    p.emit("vector", lambda e: e.tensor_copy(g2c[:, 0:8], bA.t[:, 16:24]))
    p.emit("vector", lambda e: e.tensor_copy(b2lcb[:, 0:8], bA.t[:, 24:32]),
           inc=bA.released())
    w0 = bB.produce_waits()
    for kb in range(8):
        rsl = wr_raw[0:8, kb * 128:(kb + 1) * 128]
        p.emit("tensor", lambda e, kb=kb, rsl=rsl: e.matmul(
            bB.t[:, kb * 8:(kb + 1) * 8], rsl, IDF[0:8, 0:8],
            start=True, stop=True),
            waits=(w0 if kb == 0 else ()), inc=(bB.produced() if kb == 7 else None))
    wB = bB.consume_waits()
    p.emit("vector", lambda e: e.tensor_copy(wrT[:, 0:64], bB.t[:, 0:64]),
           waits=wB, inc=bB.released())
    mVA = p.mark("vector")

    w0 = bC.produce_waits() + [mVA]
    for nb in range(8):
        for kb in range(8):
            p.emit("tensor", lambda e, nb=nb, kb=kb: e.matmul(
                bC.t[:, nb:nb + 1],
                wq_sb[:, kb * 1024 + nb * 128:kb * 1024 + (nb + 1) * 128],
                b1lcb[:, kb:kb + 1], start=(kb == 0), stop=(kb == 7)),
                waits=(w0 if nb == 0 and kb == 0 else ()),
                inc=(bC.produced() if (nb == 7 and kb == 7) else None))
    wC = bC.consume_waits()
    p.emit("vector", lambda e: e.tensor_scalar(bqc[:, 0:8], bC.t[:, 0:8], 0.125,
                                               None, ALU.mult), waits=wC,
           inc=bC.released())

    def row_then_bcast(wsb, dstb, width, ncolw):
        nhalf = (width + 511) // 512
        for half in range(nhalf):
            w = min(512, width - half * 512)
            w0 = bD_.produce_waits()
            for kb in range(8):
                p.emit("tensor", lambda e, kb=kb, half=half, w=w, wsb=wsb,
                       ncolw=ncolw: e.matmul(
                    bD_.t[0:1, 0:w], b1lcb[:, kb:kb + 1],
                    wsb[:, kb * ncolw + half * 512:kb * ncolw + half * 512 + w],
                    start=(kb == 0), stop=(kb == 7)),
                    waits=(w0 if kb == 0 else ()),
                    inc=(bD_.produced() if kb == 7 else None))
            wD = bD_.consume_waits()
            p.emit("vector", lambda e, w=w: e.tensor_copy(
                rowst[0:1, 0:w], bD_.t[0:1, 0:w]), waits=wD,
                inc=bD_.released())
            mr = p.mark("vector")
            w0 = bE.produce_waits() + [mr]
            p.emit("tensor", lambda e, w=w: e.matmul(
                bE.t[:, 0:w], ONESROW, rowst[0:1, 0:w],
                start=True, stop=True), waits=w0, inc=bE.produced())
            wE = bE.consume_waits()
            p.emit("vector", lambda e, dstb=dstb, half=half, w=w: e.tensor_copy(
                dstb[:, half * 512:half * 512 + w], bE.t[:, 0:w]), waits=wE,
                inc=bE.released())

    row_then_bcast(wk_sb, bkbc, D, D)
    row_then_bcast(wv_sb, bvbc, D, D)
    row_then_bcast(wb_sb, bbbc, H, H)
    row_then_bcast(wg_sb, bgbc, H, H)
    for (rw, dstb) in ((rows2[0:1, 0:D], bobc), (rows2[0:1, D:2 * D], b2bc)):
        for half in range(2):
            row = rw[0:1, half * 512:(half + 1) * 512]
            w0 = bE.produce_waits()
            p.emit("tensor", lambda e, row=row: e.matmul(
                bE.t[:, 0:512], ONESROW, row, start=True, stop=True),
                waits=w0, inc=bE.produced())
            wE = bE.consume_waits()
            p.emit("vector", lambda e, dstb=dstb, half=half: e.tensor_copy(
                dstb[:, half * 512:(half + 1) * 512], bE.t[:, 0:512]),
                waits=wE, inc=bE.released())
    for wsb, ncol in ((wq_sb, D), (wk_sb, D), (wv_sb, D), (wb_sb, H), (wg_sb, H)):
        for kb in range(8):
            p.emit("vector", lambda e, wsb=wsb, kb=kb, ncol=ncol: e.tensor_scalar(
                wsb[:, kb * ncol:(kb + 1) * ncol],
                wsb[:, kb * ncol:(kb + 1) * ncol],
                g1c[:, kb:kb + 1], None, ALU.mult))
    p.emit("vector", lambda e: e.memset(S0[0:64, :], 0.0))
    p.barrier()

    STAT = 1.0 / D

    def ln_stats_norm(src_f32, dst_bf, wpre):
        p.emit("scalar", lambda e: e.activation(xsq, src_f32, AF.Square),
               waits=wpre)
        mS = p.mark("scalar")
        p.emit("vector", lambda e: e.reduce_sum(
            stat[:, 0:1], src_f32.rearrange("p (a f) -> p a f", a=1), axis=AX.X),
            waits=wpre)
        p.emit("vector", lambda e: e.reduce_sum(
            stat[:, 1:2], xsq.rearrange("p (a f) -> p a f", a=1), axis=AX.X),
            waits=[mS])
        p.emit("vector", lambda e: e.tensor_scalar(stat[:, 2:3], stat[:, 0:1],
                                                   STAT, None, ALU.mult))
        p.emit("vector", lambda e: e.tensor_scalar(stat[:, 3:4], stat[:, 1:2],
                                                   STAT, None, ALU.mult))
        p.emit("vector", lambda e: e.tensor_tensor(stat[:, 4:5], stat[:, 2:3],
                                                   stat[:, 2:3], ALU.mult))
        p.emit("vector", lambda e: e.tensor_tensor(stat[:, 5:6], stat[:, 3:4],
                                                   stat[:, 4:5], ALU.subtract))
        mV = p.mark("vector")
        p.emit("scalar", lambda e: e.activation(stat[:, 6:7], stat[:, 5:6],
                                                AF.Ln, bias=1e-5), waits=[mV])
        p.emit("scalar", lambda e: e.activation(stat[:, 7:8], stat[:, 6:7],
                                                AF.Exp, scale=-0.5))
        mR = p.mark("scalar")
        p.emit("vector", lambda e: e.tensor_scalar(
            dst_bf, src_f32, stat[:, 2:3], stat[:, 7:8], ALU.subtract, ALU.mult),
            waits=[mR])
        return p.mark("vector")

    last_x_free = [None, None]
    x1t_spill_gate = [None]
    prev_b3_P = None
    prev_b1gate = None
    prev_gangP = None
    prev_epiP = None

    def chunk_scan(c):
        nonlocal prev_b3_P, prev_b1gate, prev_gangP, prev_epiP
        pAB, pCD, pEF = BK["AB"], BK["CD"], BK["EF"]
        stm = {}

        def b1_gen(bloc):
            nonlocal prev_b1gate
            gc = bloc * 16 + c
            slot = bloc
            xv = xt[slot]
            wx = [last_x_free[slot]] if last_x_free[slot] else []
            p.emit("sync", lambda e, gc=gc, xv=xv: e.dma_start(
                xv, x_d[gc // 16, (gc % 16) * 128:(gc % 16) * 128 + 128, :]),
                waits=wx, inc=chX[slot])
            nx = (chX[slot], chX[slot].n)
            gates = [nx] + ([prev_b1gate] if prev_b1gate else [])
            mnorm = ln_stats_norm(xv, xnb, gates)
            T0 = BK["T0"]
            w0 = T0.produce_waits() + [mnorm]
            for kb in range(8):
                p.emit("tensor", lambda e, kb=kb: e.transpose(
                    T0.t[:, kb * 128:(kb + 1) * 128],
                    xnb[:, kb * 128:(kb + 1) * 128], IDB),
                    waits=(w0 if kb == 0 else ()),
                    inc=(T0.produced() if kb == 7 else None))
            wT0 = T0.consume_waits()
            p.emit("vector", lambda e: e.tensor_copy(x1T, T0.t[:, :]),
                   waits=wT0 + ([x1t_spill_gate[0]] if x1t_spill_gate[0] else []),
                   inc=T0.released())
            mx1T = p.mark("vector")
            yield
            if c == tap_chunk and bloc == 0:
                tap("x1T", x1T, [mx1T])
            qgate = [mx1T] + ([prev_b3_P] if prev_b3_P else [])
            for nbp in range(4):
                bq_ = [pAB, pCD][nbp % 2]
                w0 = bq_.produce_waits() + (qgate if nbp < 2 else [])
                for half in range(2):
                    nb = 2 * nbp + half
                    for kb in range(8):
                        p.emit("tensor", lambda e, nb=nb, kb=kb, half=half,
                               bq_=bq_: e.matmul(
                            bq_.t[:, half * 512:half * 512 + 128],
                            wq_sb[:, kb * 1024 + nb * 128:kb * 1024 + (nb + 1) * 128],
                            x1T[:, kb * 128:(kb + 1) * 128],
                            start=(kb == 0), stop=(kb == 7)),
                            waits=(w0 if half == 0 and kb == 0 else ()),
                            inc=(bq_.produced() if (half == 1 and kb == 7)
                                 else None))
                wq_ = bq_.consume_waits()
                for half in range(2):
                    nb = 2 * nbp + half
                    p.emit("vector", lambda e, nb=nb, half=half, bq_=bq_,
                           bloc=bloc: e.tensor_scalar(
                        qt_b[bloc][0:64, (2 * nb) * 128:(2 * nb + 1) * 128],
                        bq_.t[0:64, half * 512:half * 512 + 128], 0.125,
                        bqc[0:64, nb:nb + 1], ALU.mult, ALU.add),
                        waits=(wq_ if half == 0 else ()))
                    p.emit("vector", lambda e, nb=nb, half=half, bq_=bq_,
                           bloc=bloc: e.tensor_scalar(
                        qt_b[bloc][0:64, (2 * nb + 1) * 128:(2 * nb + 2) * 128],
                        bq_.t[64:128, half * 512:half * 512 + 128], 0.125,
                        bqc[64:128, nb:nb + 1], ALU.mult, ALU.add),
                        inc=(bq_.released() if half == 1 else None))
            w0 = pAB.produce_waits()
            for half in range(2):
                for kb in range(8):
                    p.emit("tensor", lambda e, kb=kb, half=half: e.matmul(
                        pAB.t[:, half * 512:(half + 1) * 512],
                        x1T[:, kb * 128:(kb + 1) * 128],
                        wk_sb[:, kb * 1024 + half * 512:kb * 1024 + (half + 1) * 512],
                        start=(kb == 0), stop=(kb == 7)),
                        waits=(w0 if half == 0 and kb == 0 else ()),
                        inc=(pAB.produced() if (half == 1 and kb == 7) else None))
            wk_ = pAB.consume_waits()
            p.emit("vector", lambda e: e.tensor_tensor(
                ktm_f[:, 0:1024], pAB.t[:, 0:1024], bkbc[:, 0:1024], ALU.add),
                waits=wk_, inc=pAB.released())
            mkt = p.mark("vector")
            yield
            w0 = pCD.produce_waits()
            for half in range(2):
                for kb in range(8):
                    p.emit("tensor", lambda e, kb=kb, half=half: e.matmul(
                        pCD.t[:, half * 512:(half + 1) * 512],
                        x1T[:, kb * 128:(kb + 1) * 128],
                        wv_sb[:, kb * 1024 + half * 512:kb * 1024 + (half + 1) * 512],
                        start=(kb == 0), stop=(kb == 7)),
                        waits=(w0 if half == 0 and kb == 0 else ()),
                        inc=(pCD.produced() if (half == 1 and kb == 7) else None))
            wv_ = pCD.consume_waits()
            p.emit("vector", lambda e, bloc=bloc: e.tensor_tensor(
                vtm_b[bloc][:, 0:1024], pCD.t[:, 0:1024], bvbc[:, 0:1024],
                ALU.add), waits=wv_, inc=pCD.released())
            w0 = pEF.produce_waits()
            for i, wsb in enumerate((wb_sb, wg_sb)):
                for kb in range(8):
                    p.emit("tensor", lambda e, i=i, kb=kb, wsb=wsb: e.matmul(
                        pEF.t[:, i * 16:(i + 1) * 16],
                        x1T[:, kb * 128:(kb + 1) * 128],
                        wsb[:, kb * 16:(kb + 1) * 16],
                        start=(kb == 0), stop=(kb == 7)),
                        waits=(w0 if i == 0 and kb == 0 else ()),
                        inc=(pEF.produced() if (i == 1 and kb == 7) else None))
            wE = pEF.consume_waits()
            p.emit("vector", lambda e: e.tensor_tensor(zbf[:, 0:16],
                                                       pEF.t[:, 0:16],
                                                       bbbc[:, 0:16], ALU.add),
                   waits=wE)
            p.emit("vector", lambda e: e.tensor_tensor(zgf[:, 0:16],
                                                       pEF.t[:, 16:32],
                                                       bgbc[:, 0:16], ALU.add),
                   inc=pEF.released())
            mz = p.mark("vector")
            yield
            p.emit("scalar", lambda e, bloc=bloc: e.activation(
                beta_s[bloc][:, 0:16], zbf[:, 0:16], AF.Sigmoid), waits=[mz])
            p.emit("vector", lambda e: e.tensor_scalar(zmin[:, 0:16],
                                                       zgf[:, 0:16], 0.0,
                                                       None, ALU.min))
            mzv = p.mark("vector")
            yield
            p.emit("scalar", lambda e: e.activation(zab[:, 0:16], zgf[:, 0:16],
                                                    AF.Abs), waits=[mzv])
            p.emit("scalar", lambda e: e.activation(zab[:, 0:16], zab[:, 0:16],
                                                    AF.Exp, scale=-1.0))
            p.emit("scalar", lambda e: e.activation(zab[:, 0:16], zab[:, 0:16],
                                                    AF.Ln, bias=1.0))
            mzs = p.mark("scalar")
            yield
            p.emit("vector", lambda e: e.tensor_tensor(
                logg_s[:, 0:16], zmin[:, 0:16], zab[:, 0:16], ALU.subtract),
                waits=[mzs])
            mlg = p.mark("vector")
            yield
            p.emit("scalar", lambda e: e.activation(xsq, ktm_f, AF.Square),
                   waits=[mkt])
            msq = p.mark("scalar")
            yield
            p.emit("vector", lambda e: e.reduce_sum(
                knss[:, 0:16], xsq.rearrange("p (h d) -> p h d", d=64),
                axis=AX.X), waits=[msq])
            mks = p.mark("vector")
            yield
            p.emit("scalar", lambda e: e.activation(knsc[:, 0:16],
                                                    knss[:, 0:16], AF.Ln),
                   waits=[mks])
            p.emit("scalar", lambda e: e.activation(knsc[:, 0:16],
                                                    knsc[:, 0:16], AF.Exp,
                                                    scale=0.5))
            mkn = p.mark("scalar")
            yield
            p.emit("vector", lambda e: e.tensor_scalar(knsc[:, 0:16],
                                                       knsc[:, 0:16], 1e-6,
                                                       None, ALU.add),
                   waits=[mkn])
            p.emit("vector", lambda e: e.reciprocal(knsc[:, 0:16],
                                                    knsc[:, 0:16]))
            p.emit("vector", lambda e, bloc=bloc: e.tensor_tensor(
                khat_b[bloc].rearrange("p (h j) -> p h j", j=64),
                ktm_f.rearrange("p (h j) -> p h j", j=64),
                bc(knsc, 128, 16), ALU.mult))
            mkh = p.mark("vector")
            yield
            if c == tap_chunk and bloc == 0:
                tap("ktmf", ktm_f, [mkh])
                tap("khat", khat_b[0], [mkh])
                tap("vtm", vtm_b[0], [mkh])
                tap("beta", beta_s[0][:, 0:16], [mkh])
            w0 = pEF.produce_waits() + [mlg]
            p.emit("tensor", lambda e: e.matmul(pEF.t[:, 0:16], TRIU,
                                                logg_s[:, 0:16], start=True,
                                                stop=True), waits=w0)
            p.emit("tensor", lambda e: e.matmul(pEF.t[0:1, 16:32], MASKCOL,
                                                logg_s[:, 0:16], start=True,
                                                stop=True))
            p.emit("tensor", lambda e: e.matmul(pEF.t[0:1, 32:48], ONESCOL,
                                                logg_s[:, 0:16], start=True,
                                                stop=True), inc=pEF.produced())
            wF = pEF.consume_waits()
            p.emit("vector", lambda e: e.tensor_copy(L_s[:, 0:16],
                                                     pEF.t[:, 0:16]), waits=wF)
            p.emit("vector", lambda e: e.tensor_copy(rowst[0:1, 512:528],
                                                     pEF.t[0:1, 16:32]))
            p.emit("vector", lambda e: e.tensor_copy(rowst[0:1, 528:544],
                                                     pEF.t[0:1, 32:48]),
                   inc=pEF.released())
            p.emit("vector", lambda e: e.tensor_tensor(
                rowst[0:1, 544:560], rowst[0:1, 528:544], rowst[0:1, 512:528],
                ALU.subtract))
            mLr = p.mark("vector")
            yield
            w0 = pEF.produce_waits() + [mLr]
            p.emit("tensor", lambda e: e.matmul(pEF.t[:, 0:16], ONESROW,
                                                rowst[0:1, 512:528],
                                                start=True, stop=True),
                   waits=w0)
            p.emit("tensor", lambda e: e.matmul(pEF.t[0:64, 16:32],
                                                TRIU[0:1, 0:64],
                                                rowst[0:1, 512:528],
                                                start=True, stop=True))
            p.emit("tensor", lambda e: e.matmul(pEF.t[:, 32:48], ONESROW,
                                                rowst[0:1, 544:560],
                                                start=True, stop=True))
            p.emit("tensor", lambda e: e.matmul(pEF.t[0:64, 48:64],
                                                TRIU[0:1, 0:64],
                                                rowst[0:1, 528:544],
                                                start=True, stop=True),
                   inc=pEF.produced())
            wE = pEF.consume_waits()
            p.emit("vector", lambda e: e.tensor_tensor(Lt_s[:, 0:16],
                                                       L_s[:, 0:16],
                                                       pEF.t[:, 0:16],
                                                       ALU.subtract), waits=wE)
            p.emit("scalar", lambda e, bloc=bloc: e.activation(
                eLm[bloc][0:64, 0:16], pEF.t[0:64, 16:32], AF.Exp), waits=wE)
            p.emit("scalar", lambda e, bloc=bloc: e.activation(
                gCh[bloc][:, 0:16], pEF.t[:, 32:48], AF.Exp))
            p.emit("scalar", lambda e, bloc=bloc: e.activation(
                gC[bloc][0:64, 0:16], pEF.t[0:64, 48:64], AF.Exp),
                inc=pEF.released())
            mLt = p.mark("vector")
            yield
            p.emit("scalar", lambda e, bloc=bloc: e.activation(
                gam[bloc][:, 0:16], Lt_s[:, 0:16], AF.Exp), waits=[mLt])
            p.emit("scalar", lambda e: e.activation(knss[:, 0:16],
                                                    Lt_s[:, 0:16], AF.Exp,
                                                    scale=-1.0))
            mex = p.mark("scalar")
            yield
            p.emit("vector", lambda e, bloc=bloc: e.tensor_tensor(
                vbsc[bloc][:, 0:16], knss[:, 0:16], beta_s[bloc][:, 0:16],
                ALU.mult), waits=[mex])
            mvb = p.mark("vector")
            yield
            prev_b1gate = mvb
            if c == tap_chunk and bloc == 0:
                tap("L", L_s[:, 0:16], [mLt])
                tap("gamma", gam[0][:, 0:16], [mvb])

            stm[("mkh", bloc)] = mkh
            stm[("mex", bloc)] = mex
            yield

        def gangs_gen(bloc):
            nonlocal prev_gangP
            for gidx in range(2):
                hs = [8 * gidx + i for i in range(8)]
                g8 = bloc * 2 + gidx
                T0 = BK["T0"]
                w0 = T0.produce_waits() + [stm[("mkh", bloc)]]
                for i, h in enumerate(hs):
                    p.emit("tensor", lambda e, i=i, h=h, bloc=bloc: e.transpose(
                        T0.t[0:64, i * 128:(i + 1) * 128],
                        khat_b[bloc][:, h * 64:(h + 1) * 64], IDB),
                        waits=(w0 if i == 0 else ()),
                        inc=(T0.produced() if i == 7 else None))
                wT0 = T0.consume_waits()
                p.emit("vector", lambda e, g8=g8: e.tensor_copy(
                    khatT[0:64, g8 * 1024:(g8 + 1) * 1024], T0.t[0:64, 0:1024]),
                    waits=wT0 + ([prev_b3_P] if prev_b3_P and g8 == 0 else []),
                    inc=T0.released())
                mkT = p.mark("vector")
                yield
                kTg = khatT[0:64, g8 * 1024:(g8 + 1) * 1024]
                w0 = pAB.produce_waits() + [mkT]
                for i, h in enumerate(hs):
                    p.emit("tensor", lambda e, i=i, kTg=kTg: e.matmul(
                        pAB.t[:, i * 128:(i + 1) * 128],
                        kTg[0:64, i * 128:(i + 1) * 128],
                        kTg[0:64, i * 128:(i + 1) * 128], start=True, stop=True),
                        waits=(w0 if i == 0 else ()),
                        inc=(pAB.produced() if i == 7 else None))
                wA = pAB.consume_waits()
                gateg = [prev_gangP] if prev_gangP else []
                p.emit("vector", lambda e, bloc=bloc, hs=hs: e.tensor_tensor(
                    nsb.rearrange("p (i cc) -> p i cc", cc=128),
                    pAB.t.rearrange("p (i cc) -> p i cc", cc=128),
                    bc(beta_s[bloc], 128, 8, rep=128, h0=hs[0]), ALU.mult),
                    waits=wA + gateg)
                p.emit("vector", lambda e: e.tensor_tensor(
                    nsb[:, 0:512], nsb[:, 0:512], STRIL4[:, 0:512], ALU.mult))
                p.emit("vector", lambda e: e.tensor_tensor(
                    nsb[:, 512:1024], nsb[:, 512:1024], STRIL4[:, 0:512],
                    ALU.mult), inc=pAB.released())
                mN = p.mark("vector")
                yield
                T1 = BK["T1"]
                w0 = T1.produce_waits() + [mN]
                for i in range(8):
                    p.emit("tensor", lambda e, i=i: e.transpose(
                        T1.t[:, i * 128:(i + 1) * 128],
                        nsb[:, i * 128:(i + 1) * 128], IDB),
                        waits=(w0 if i == 0 else ()),
                        inc=(T1.produced() if i == 7 else None))
                wT1 = T1.consume_waits()
                p.emit("scalar", lambda e: e.copy(fTp[0][:, 0:1024],
                                                  T1.t[:, 0:1024]),
                       waits=wT1, inc=T1.released())
                mprev = p.mark("scalar")
                yield
                cur, nxt = nsb, sqa
                for lev in range(1, NLEV + 1):
                    if lev < NLEV:
                        w0 = pCD.produce_waits() + [mprev]
                        for i in range(8):
                            p.emit("tensor", lambda e, i=i, lev=lev,
                                   cur=cur: e.matmul(
                                pCD.t[:, i * 128:(i + 1) * 128],
                                fTp[lev - 1][:, i * 128:(i + 1) * 128],
                                cur[:, i * 128:(i + 1) * 128],
                                start=True, stop=True),
                                waits=(w0 if i == 0 else ()),
                                inc=(pCD.produced() if i == 7 else None))
                    w0 = pEF.produce_waits() + [mprev]
                    for i in range(8):
                        p.emit("tensor", lambda e, i=i, lev=lev,
                               cur=cur: e.matmul(
                            pEF.t[:, i * 128:(i + 1) * 128],
                            cur[:, i * 128:(i + 1) * 128],
                            fTp[lev - 1][:, i * 128:(i + 1) * 128],
                            start=True, stop=True),
                            waits=(w0 if i == 0 else ()),
                            inc=(pEF.produced() if i == 7 else None))
                    if lev < NLEV:
                        wc = pCD.consume_waits()
                        p.emit("scalar", lambda e, nxt=nxt: e.copy(
                            nxt[:, 0:1024], pCD.t[:, 0:1024]), waits=wc,
                            inc=pCD.released())
                    we = pEF.consume_waits()
                    p.emit("scalar", lambda e, lev=lev: e.copy(
                        fTp[lev][:, 0:1024], pEF.t[:, 0:1024]), waits=we,
                        inc=pEF.released())
                    mprev = p.mark("scalar")
                    yield
                    cur, nxt = nxt, cur
                t0, t1 = nsb, sqa
                h0 = hs[0]
                p.emit("vector", lambda e, h0=h0, t0=t0, bloc=bloc: e.tensor_tensor(
                    t0.rearrange("p (i cc) -> p i cc", cc=128)[:, :, 0:64],
                    khat_b[bloc].rearrange("p (h j) -> p h j", j=64)
                    [:, h0:h0 + 8, :],
                    bc(beta_s[bloc], 128, 8, h0=h0), ALU.mult),
                    waits=[mprev])
                p.emit("vector", lambda e, h0=h0, t0=t0, bloc=bloc: e.tensor_tensor(
                    t0.rearrange("p (i cc) -> p i cc", cc=128)[:, :, 64:128],
                    vtm_b[bloc].rearrange("p (h j) -> p h j", j=64)
                    [:, h0:h0 + 8, :],
                    bc(vbsc[bloc], 128, 8, h0=h0), ALU.mult))
                mt = p.mark("vector")
                yield
                for lev in range(NLEV, 0, -1):
                    w0 = pAB.produce_waits() + [mt]
                    for i in range(8):
                        p.emit("tensor", lambda e, i=i, t0=t0: e.matmul(
                            pAB.t[:, i * 128:(i + 1) * 128],
                            IDB, t0[:, i * 128:(i + 1) * 128],
                            start=True, stop=False),
                            waits=(w0 if i == 0 else ()))
                        p.emit("tensor", lambda e, i=i, lev=lev, t0=t0: e.matmul(
                            pAB.t[:, i * 128:(i + 1) * 128],
                            fTp[lev][:, i * 128:(i + 1) * 128],
                            t0[:, i * 128:(i + 1) * 128],
                            start=False, stop=True),
                            inc=(pAB.produced() if i == 7 else None))
                    wa = pAB.consume_waits()
                    p.emit("scalar", lambda e, t1=t1: e.copy(
                        t1[:, 0:1024], pAB.t[:, 0:1024]), waits=wa,
                        inc=pAB.released())
                    mt = p.mark("scalar")
                    yield
                    t0, t1 = t1, t0
                w0 = pCD.produce_waits() + [mt]
                for i in range(8):
                    p.emit("tensor", lambda e, i=i, t0=t0: e.matmul(
                        pCD.t[:, i * 128:(i + 1) * 128],
                        fTp[0][:, i * 128:(i + 1) * 128],
                        t0[:, i * 128:(i + 1) * 128], start=True, stop=True),
                        waits=(w0 if i == 0 else ()),
                        inc=(pCD.produced() if i == 7 else None))
                wc = pCD.consume_waits()
                p.emit("vector", lambda e, g8=g8, t0=t0: e.tensor_tensor(
                    tall[:, g8 * 1024:(g8 + 1) * 1024], t0[:, 0:1024],
                    pCD.t[:, 0:1024], ALU.subtract), waits=wc,
                    inc=pCD.released())
                mtall = p.mark("vector")
                yield
                w0 = pEF.produce_waits() + [mkT]
                for i, h in enumerate(hs):
                    p.emit("tensor", lambda e, i=i, h=h, bloc=bloc,
                           kTg=kTg: e.matmul(
                        pEF.t[:, i * 128:(i + 1) * 128],
                        kTg[0:64, i * 128:(i + 1) * 128],
                        qt_b[bloc][0:64, h * 128:(h + 1) * 128],
                        start=True, stop=True),
                        waits=(w0 if i == 0 else ()),
                        inc=(pEF.produced() if i == 7 else None))
                we = pEF.consume_waits()
                p.emit("vector", lambda e, g8=g8: e.tensor_tensor(
                    attnT[:, g8 * 1024:g8 * 1024 + 512], pEF.t[:, 0:512],
                    TRIU4[:, 0:512], ALU.mult), waits=we)
                p.emit("vector", lambda e, g8=g8: e.tensor_tensor(
                    attnT[:, g8 * 1024 + 512:(g8 + 1) * 1024], pEF.t[:, 512:1024],
                    TRIU4[:, 0:512], ALU.mult), inc=pEF.released())
                T0 = BK["T0"]
                w0 = T0.produce_waits() + [mtall]
                for i in range(8):
                    u = bloc * 16 + 8 * gidx + i
                    p.emit("tensor", lambda e, i=i, u=u: e.transpose(
                        T0.t[0:64, i * 128:(i + 1) * 128],
                        tall[:, u * 128:u * 128 + 64], IDB),
                        waits=(w0 if i == 0 else ()),
                        inc=(T0.produced() if i == 7 else None))
                wT0 = T0.consume_waits()
                p.emit("vector", lambda e, g8=g8: e.tensor_copy(
                    wT[0:64, g8 * 1024:(g8 + 1) * 1024], T0.t[0:64, 0:1024]),
                    waits=wT0, inc=T0.released())
                prev_gangP = p.mark("tensor")
                yield
            yield

        def b34_gen(bloc):
            nonlocal prev_b3_P, prev_epiP, prev_b1gate
            p.emit("vector", lambda e, bloc=bloc: e.tensor_tensor(
                u0b[0:64, bloc * 1024:(bloc + 1) * 1024].rearrange(
                    "p (u j) -> p u j", j=64),
                S0[0:64, bloc * 1024:(bloc + 1) * 1024].rearrange(
                    "p (u j) -> p u j", j=64),
                bc(eLm[bloc], 64, 16), ALU.mult),
                waits=[stm[("mex", bloc)]])
            mu0 = p.mark("vector")
            yield
            w0 = pAB.produce_waits() + [mu0]
            for i in range(16):
                u = bloc * 16 + i
                p.emit("tensor", lambda e, i=i, u=u: e.matmul(
                    pAB.t[:, i * 64:(i + 1) * 64],
                    wT[0:64, u * 128:(u + 1) * 128],
                    u0b[0:64, u * 64:(u + 1) * 64], start=True, stop=True),
                    waits=(w0 if i == 0 else ()),
                    inc=(pAB.produced() if i == 15 else None))
            wg_ = pAB.consume_waits()
            u0_ = bloc * 16
            p.emit("vector", lambda e, u0_=u0_: e.tensor_tensor(
                zp[:, u0_ * 64:(u0_ + 16) * 64],
                tall.rearrange("p (u cc) -> p u cc", cc=128)
                [:, u0_:u0_ + 16, 64:128],
                pAB.t[:, 0:1024], ALU.subtract), waits=wg_, inc=pAB.released())
            mzp = p.mark("vector")
            yield
            w0 = pCD.produce_waits() + [mzp]
            for i in range(16):
                u = bloc * 16 + i
                h = i
                p.emit("tensor", lambda e, i=i, u=u, h=h, bloc=bloc: e.matmul(
                    pCD.t[:, i * 64:(i + 1) * 64],
                    qt_b[bloc][0:64, h * 128:(h + 1) * 128],
                    u0b[0:64, u * 64:(u + 1) * 64], start=True, stop=False),
                    waits=(w0 if i == 0 else ()))
                p.emit("tensor", lambda e, i=i, u=u: e.matmul(
                    pCD.t[:, i * 64:(i + 1) * 64],
                    attnT[:, u * 128:(u + 1) * 128],
                    zp[:, u * 64:(u + 1) * 64], start=False, stop=True),
                    inc=(pCD.produced() if i == 15 else None))
            wo_ = pCD.consume_waits()
            wev = list(wo_)
            if prev_epiP:
                wev.append(prev_epiP)
            p.emit("vector", lambda e, bloc=bloc: e.tensor_tensor(
                abf.rearrange("p (h j) -> p h j", j=64),
                pCD.t.rearrange("p (i j) -> p i j", j=64),
                bc(gam[bloc], 128, 16), ALU.mult),
                waits=wev, inc=pCD.released())
            w0 = pEF.produce_waits() + [mzp]
            for i in range(16):
                u = bloc * 16 + i
                h = i
                p.emit("tensor", lambda e, i=i, u=u, h=h, bloc=bloc: e.matmul(
                    pEF.t[0:64, i * 64:(i + 1) * 64],
                    khat_b[bloc][:, h * 64:(h + 1) * 64],
                    zp[:, u * 64:(u + 1) * 64], start=True, stop=True),
                    waits=(w0 if i == 0 else ()),
                    inc=(pEF.produced() if i == 15 else None))
            ws_ = pEF.consume_waits()
            u0c = bloc * 1024
            p.emit("vector", lambda e, bloc=bloc: e.tensor_tensor(
                stmp[0:64, 0:1024].rearrange("p (i j) -> p i j", j=64)
                if False else
                stmpw[0:64, 0:1024].rearrange("p (i j) -> p i j", j=64),
                pEF.t[0:64, 0:1024].rearrange("p (i j) -> p i j", j=64),
                bc(gCh[bloc], 64, 16), ALU.mult), waits=ws_,
                inc=pEF.released())
            p.emit("vector", lambda e, u0c=u0c, bloc=bloc: e.tensor_tensor(
                S0[0:64, u0c:u0c + 1024].rearrange("p (i j) -> p i j", j=64),
                S0[0:64, u0c:u0c + 1024].rearrange("p (i j) -> p i j", j=64),
                bc(gC[bloc], 64, 16), ALU.mult))
            p.emit("vector", lambda e, u0c=u0c: e.tensor_tensor(
                S0[0:64, u0c:u0c + 1024], S0[0:64, u0c:u0c + 1024],
                stmpw[0:64, 0:1024], ALU.add))
            prev_b3_P = p.mark("tensor")
            yield
            matt = p.mark("vector")
            yield
            if c == tap_chunk and bloc == 0:
                tap("u0b", u0b[0:64, 0:1024], [matt])
                tap("zp", zp[:, 0:1024], [matt])
                tap("attnout", abf, [matt])
                tap("S0", S0[0:64, 0:1024], [matt])

            gc = bloc * 16 + c
            slot = bloc
            xv = xt[slot]
            T1 = BK["T1"]
            w0 = T1.produce_waits() + [matt]
            for kb in range(8):
                p.emit("tensor", lambda e, kb=kb: e.transpose(
                    T1.t[:, kb * 128:(kb + 1) * 128],
                    abf[:, kb * 128:(kb + 1) * 128], IDB),
                    waits=(w0 if kb == 0 else ()),
                    inc=(T1.produced() if kb == 7 else None))
            prev_epiP = p.mark("tensor")
            yield
            wT1 = T1.consume_waits()
            p.emit("vector", lambda e: e.tensor_copy(aT, T1.t[:, :]), waits=wT1,
                   inc=T1.released())
            maT = p.mark("vector")
            yield
            w0 = pAB.produce_waits() + [maT]
            for half in range(2):
                for kb in range(8):
                    p.emit("tensor", lambda e, kb=kb, half=half: e.matmul(
                        pAB.t[:, half * 512:(half + 1) * 512],
                        aT[:, kb * 128:(kb + 1) * 128],
                        wo_sb[:, kb * 1024 + half * 512:kb * 1024 + (half + 1) * 512],
                        start=(kb == 0), stop=(kb == 7)),
                        waits=(w0 if half == 0 and kb == 0 else ()),
                        inc=(pAB.produced() if (half == 1 and kb == 7) else None))
            ww_ = pAB.consume_waits()
            p.emit("vector", lambda e: e.tensor_tensor(
                x2t[:, 0:1024], pAB.t[:, 0:1024], bobc[:, 0:1024], ALU.add),
                waits=ww_)
            p.emit("vector", lambda e, xv=xv: e.tensor_tensor(
                x2t[:, 0:1024], x2t[:, 0:1024], xv[:, 0:1024], ALU.add),
                inc=pAB.released())
            mx2 = p.mark("vector")
            yield
            last_x_free[slot] = mx2
            p.emit("sync", lambda e, gc=gc: e.dma_start(x2_sp[gc], x2t),
                   waits=[mx2], inc=p.chD)
            if c == tap_chunk and bloc == 0:
                tap("x2", x2t, [mx2])
            mn2 = ln_stats_norm(x2t, xnb, [mx2])
            T0 = BK["T0"]
            w0 = T0.produce_waits() + [mn2]
            for kb in range(8):
                p.emit("tensor", lambda e, kb=kb: e.transpose(
                    T0.t[:, kb * 128:(kb + 1) * 128],
                    xnb[:, kb * 128:(kb + 1) * 128], IDB),
                    waits=(w0 if kb == 0 else ()),
                    inc=(T0.produced() if kb == 7 else None))
            wT0 = T0.consume_waits()
            p.emit("vector", lambda e: e.tensor_copy(x1t_sb, T0.t[:, :]),
                   waits=wT0, inc=T0.released())
            mx1m = p.mark("vector")
            yield
            p.emit("sync", lambda e, gc=gc: e.dma_start(
                x1T_sp[gc].rearrange("kb p t -> p kb t"),
                x1t_sb.rearrange("p (kb t) -> p kb t", kb=8)),
                waits=[mx1m], inc=chSp)
            x1t_spill_gate[0] = (chSp, chSp.n)
            prev_b1gate = mx1m
            yield

        def _drain(g):
            for _ in g:
                pass

        def _ilv(a, b):
            act = [iter(a), iter(b)]
            while act:
                for g in list(act):
                    try:
                        next(g)
                    except StopIteration:
                        act.remove(g)

        if len(blocs) == 2:
            _drain(b1_gen(0))
            _ilv(gangs_gen(0), b1_gen(1))
            _ilv(gangs_gen(1), b34_gen(0))
            _drain(b34_gen(1))
        else:
            for bl in blocs:
                _drain(b1_gen(bl))
                _drain(gangs_gen(bl))
                _drain(b34_gen(bl))
        p.barrier3()
    if stop_after != "A" and dbg_spill != "input":
        for c in range(min(n_chunks, 16)):
            chunk_scan(c)

    if stop_after in ("C", "D"):
        phase_C(nc, p, ar, BK, dict(
            x_d=x_d, w_in=w_in, x2_sp=x2_sp, x1T_sp=x1T_sp, y_sp=y_sp,
            yT_sp=yT_sp, IDB=IDB, IDF=IDF, ONESROW=ONESROW, b2bc=b2bc,
            wrT=wrT, cw_all=cw_all, g2c=g2c, b2lcb=b2lcb, tap=tap,
            tap_chunk=tap_chunk, n_chunks=n_chunks, chCx1=chCx1, chCx2=chCx2,
            chCxc=chCxc), stop_after)
    if stop_after == "D":
        phase_D(nc, p, ar, BK, dict(
            w_in=w_in, y_sp=y_sp, yT_sp=yT_sp, out_d=out_d, cw_all=cw_all,
            tap=tap, n_chunks=n_chunks, chYT=chYT, chYs=chYs, chWp=chWp))
    p.barrier()
    with nc.Block() as block:
        p.realize(block)
    stack.close()
    return nc


def phase_C(nc, p, ar, BK, env, stop_after):
    bA, bB, bC, bD_, bE, bF_ = (BK[k] for k in "ABCDEF")
    x_d, w_in = env["x_d"], env["w_in"]
    x2_sp, x1T_sp, y_sp, yT_sp = env["x2_sp"], env["x1T_sp"], env["y_sp"], env["yT_sp"]
    IDB, IDF, ONESROW = env["IDB"], env["IDF"], env["ONESROW"]
    b2bc, wrT, cw_all = env["b2bc"], env["wrT"], env["cw_all"]
    g2c, b2lcb, tap = env["g2c"], env["b2lcb"], env["tap"]
    n_chunks = env["n_chunks"]
    chCx1, chCx2, chCxc = env["chCx1"], env["chCx2"], env["chCxc"]
    p.barrier()
    ar.pop()
    ar.push()
    w1_sb = ar.bf(32768)
    w2_sb = ar.bf(32768)
    x1g = [ar.bf(2048), ar.bf(2048)]
    hTh = ar.bf(8192)
    x2c = [ar.f32(D) for _ in range(2)]
    xc = [ar.f32(D) for _ in range(2)]
    ypre = [ar.f32(D) for _ in range(2)]
    xsq2 = ar.f32(D)
    stat2 = ar.f32(8)
    ybf = ar.bf(D)
    yT_s = ar.bf(D)
    gelub = ar.f32(32)
    b1st = ar.f32(128)
    rsm = ar.f32(16)
    zrow = ar.f32(16)
    assert ar.off <= ar.cols, ar.off

    src = w_in["w1"].rearrange("(kb p) (hb t) -> p kb hb t", p=128, t=128)
    dst = w1_sb.rearrange("p (kb hb t) -> p kb hb t", kb=8, t=128)
    p.emit("gpsimd", lambda e: e.dma_start(dst, src), inc=p.chW)
    src2 = w_in["w2"].rearrange("(hb p) (fh f) -> p hb fh f", p=128, f=512)
    dst2 = w2_sb.rearrange("p (hb fh f) -> p hb fh f", hb=32, f=512)
    p.emit("gpsimd", lambda e: e.dma_start(dst2, src2), inc=p.chW)
    p.emit("sync", lambda e: e.dma_start(
        b1st[0:32, 0:128], w_in["b1"].rearrange("(r t) -> r t", r=32)),
        inc=p.chD)
    nWD = (p.chW, p.chW.n)
    nDD = (p.chD, p.chD.n)
    w0 = bE.produce_waits() + [nWD]
    for hb in range(32):
        for kb in range(8):
            p.emit("tensor", lambda e, hb=hb, kb=kb: e.matmul(
                bE.t[:, hb:hb + 1],
                w1_sb[:, kb * 4096 + hb * 128:kb * 4096 + (hb + 1) * 128],
                b2lcb[:, kb:kb + 1], start=(kb == 0), stop=(kb == 7)),
                waits=(w0 if hb == 0 and kb == 0 else ()),
                inc=(bE.produced() if (hb == 31 and kb == 7) else None))
    w0 = bF_.produce_waits() + [nDD]
    p.emit("tensor", lambda e: e.matmul(bF_.t[:, 0:32], b1st[0:32, 0:128],
                                        IDF[0:32, 0:32], start=True, stop=True),
           waits=w0, inc=bF_.produced())
    wE = bE.consume_waits()
    wF = bF_.consume_waits()
    p.emit("vector", lambda e: e.tensor_copy(b1st[:, 0:32], bF_.t[:, 0:32]),
           waits=wF, inc=bF_.released())
    p.emit("vector", lambda e: e.tensor_tensor(gelub[:, 0:32], bE.t[:, 0:32],
                                               b1st[:, 0:32], ALU.add),
           waits=wE, inc=bE.released())
    tap("gelub", gelub[:, 0:32], [p.mark("vector")])
    for kb in range(8):
        p.emit("vector", lambda e, kb=kb: e.tensor_scalar(
            w1_sb[:, kb * 4096:(kb + 1) * 4096],
            w1_sb[:, kb * 4096:(kb + 1) * 4096],
            g2c[:, kb:kb + 1], None, ALU.mult))
    p.barrier()

    assert n_chunks >= 16
    ngrp = 16
    prev_marks = {}
    for grp in range(ngrp):
        gcs = [2 * grp, 2 * grp + 1]
        buf = grp % 2
        wx = [prev_marks.get(("x1g", buf))] if ("x1g", buf) in prev_marks else []
        for ci, gc in enumerate(gcs):
            p.emit("sync", lambda e, gc=gc, ci=ci, buf=buf: e.dma_start(
                x1g[buf].rearrange("p (kb c t) -> p kb c t", kb=8, c=2)
                [:, :, ci, :],
                x1T_sp[gc].rearrange("kb p t -> p kb t")),
                waits=([w for w in wx if w] if ci == 0 else []), inc=chCx1[buf])
            p.emit("sync", lambda e, gc=gc, ci=ci, buf=buf: e.dma_start(
                x2c[ci], x2_sp[gc]),
                waits=[prev_marks[("x2c", ci)]] if ("x2c", ci) in prev_marks
                else [], inc=chCx2[ci])
            p.emit("sync", lambda e, gc=gc, ci=ci: e.dma_start(
                xc[ci], x_d[gc // 16, (gc % 16) * 128:(gc % 16) * 128 + 128, :]),
                waits=[prev_marks[("xc", ci)]] if ("xc", ci) in prev_marks
                else [], inc=chCxc[ci])
        nload = [(chCx1[buf], chCx1[buf].n), (chCx2[0], chCx2[0].n),
                 (chCx2[1], chCx2[1].n), (chCxc[0], chCxc[0].n),
                 (chCxc[1], chCxc[1].n)]
        for hf in range(2):
            for hb in range(16):
                hbg = hf * 16 + hb
                bm = [bE, bF_][hbg % 2]
                w0 = bm.produce_waits() + nload
                for kb in range(8):
                    p.emit("tensor", lambda e, kb=kb, hbg=hbg, bm=bm, buf=buf: e.matmul(
                        bm.t[:, 0:256],
                        w1_sb[:, kb * 4096 + hbg * 128:kb * 4096 + (hbg + 1) * 128],
                        x1g[buf][:, kb * 256:(kb + 1) * 256],
                        start=(kb == 0), stop=(kb == 7)),
                        waits=(w0 if kb == 0 else ()),
                        inc=(bm.produced() if kb == 7 else None))
                wm = bm.consume_waits()
                p.emit("scalar", lambda e, hbg=hbg, bm=bm: e.activation(
                    hTh[:, hbg * 256:(hbg + 1) * 256], bm.t[:, 0:256], AF.Gelu,
                    bias=gelub[:, hbg:hbg + 1]), waits=wm, inc=bm.released())
        mh = p.mark("scalar")
        ob = {(0, 0): bA, (0, 1): bB, (1, 0): bC, (1, 1): bD_}
        for ci in range(2):
            for fh in range(2):
                bk_ = ob[(ci, fh)]
                w0 = bk_.produce_waits() + [mh]
                for hbg in range(32):
                    p.emit("tensor", lambda e, ci=ci, fh=fh, hbg=hbg,
                           bk_=bk_: e.matmul(
                        bk_.t[:, 0:512],
                        hTh[:, hbg * 256 + ci * 128:hbg * 256 + (ci + 1) * 128],
                        w2_sb[:, hbg * 1024 + fh * 512:hbg * 1024 + (fh + 1) * 512],
                        start=(hbg == 0), stop=(hbg == 31)),
                        waits=(w0 if hbg == 0 else ()),
                        inc=(bk_.produced() if hbg == 31 else None))
        mrow_marks = []
        for ci, gc in enumerate(gcs):
            for fh in range(2):
                bk_ = ob[(ci, fh)]
                wk_ = bk_.consume_waits()
                p.emit("vector", lambda e, ci=ci, fh=fh, bk_=bk_: e.tensor_tensor(
                    ypre[ci][:, fh * 512:(fh + 1) * 512], bk_.t[:, 0:512],
                    b2bc[:, fh * 512:(fh + 1) * 512], ALU.add), waits=wk_,
                    inc=bk_.released())
            if gc == 1:
                tap("ypre0", ypre[1], [p.mark("vector")])
            for fh in range(2):
                p.emit("vector", lambda e, ci=ci, fh=fh: e.tensor_tensor(
                    ypre[ci][:, fh * 512:(fh + 1) * 512],
                    ypre[ci][:, fh * 512:(fh + 1) * 512],
                    x2c[ci][:, fh * 512:(fh + 1) * 512], ALU.add))
                p.emit("vector", lambda e, ci=ci, fh=fh: e.tensor_tensor(
                    ypre[ci][:, fh * 512:(fh + 1) * 512],
                    ypre[ci][:, fh * 512:(fh + 1) * 512],
                    xc[ci][:, fh * 512:(fh + 1) * 512], ALU.add))
            mypre = p.mark("vector")
            if gc == 1:
                tap("ypre", ypre[ci], [mypre])
            prev_marks[("xc", ci)] = mypre
            p.emit("scalar", lambda e, ci=ci: e.activation(xsq2, ypre[ci],
                                                           AF.Square),
                   waits=[mypre])
            mS = p.mark("scalar")
            p.emit("vector", lambda e, ci=ci: e.reduce_sum(
                stat2[:, 0:1], ypre[ci].rearrange("p (a f) -> p a f", a=1),
                axis=AX.X), waits=[mypre])
            p.emit("vector", lambda e: e.reduce_sum(
                stat2[:, 1:2], xsq2.rearrange("p (a f) -> p a f", a=1),
                axis=AX.X), waits=[mS])
            p.emit("vector", lambda e: e.tensor_scalar(stat2[:, 2:3],
                                                       stat2[:, 0:1], 1.0 / D,
                                                       None, ALU.mult))
            p.emit("vector", lambda e: e.tensor_scalar(stat2[:, 3:4],
                                                       stat2[:, 1:2], 1.0 / D,
                                                       None, ALU.mult))
            p.emit("vector", lambda e: e.tensor_tensor(stat2[:, 4:5],
                                                       stat2[:, 2:3],
                                                       stat2[:, 2:3], ALU.mult))
            p.emit("vector", lambda e: e.tensor_tensor(stat2[:, 5:6],
                                                       stat2[:, 3:4],
                                                       stat2[:, 4:5],
                                                       ALU.subtract))
            mV = p.mark("vector")
            p.emit("scalar", lambda e: e.activation(stat2[:, 6:7], stat2[:, 5:6],
                                                    AF.Ln, bias=1e-5),
                   waits=[mV])
            p.emit("scalar", lambda e: e.activation(stat2[:, 7:8], stat2[:, 6:7],
                                                    AF.Exp, scale=-0.5))
            mR = p.mark("scalar")
            p.emit("vector", lambda e, ci=ci: e.tensor_scalar(
                x2c[ci], ypre[ci], stat2[:, 2:3], stat2[:, 7:8],
                ALU.subtract, ALU.mult), waits=[mR])
            my = p.mark("vector")
            p.emit("sync", lambda e, ci=ci, gc=gc: e.dma_start(
                y_sp[gc], x2c[ci]), waits=[my], inc=p.chD)
            p.emit("scalar", lambda e, ci=ci: e.copy(ybf, x2c[ci]), waits=[my])
            myb = p.mark("scalar")
            prev_marks[("x2c", ci)] = myb
            T0 = BK["T0"]
            w0 = T0.produce_waits() + [myb]
            for kb in range(8):
                p.emit("tensor", lambda e, kb=kb: e.transpose(
                    T0.t[:, kb * 128:(kb + 1) * 128],
                    ybf[:, kb * 128:(kb + 1) * 128], IDB),
                    waits=(w0 if kb == 0 else ()),
                    inc=(T0.produced() if kb == 7 else None))
            wT0 = T0.consume_waits()
            p.emit("vector", lambda e: e.tensor_copy(yT_s, T0.t[:, :]),
                   waits=wT0, inc=T0.released())
            myT = p.mark("vector")
            p.emit("sync", lambda e, gc=gc: e.dma_start(
                yT_sp[gc].rearrange("kb p t -> p kb t"),
                yT_s.rearrange("p (kb t) -> p kb t", kb=8)),
                waits=[myT], inc=p.chD)
            w0 = bE.produce_waits() + [myT]
            for kb in range(8):
                p.emit("tensor", lambda e, kb=kb: e.matmul(
                    bE.t[:, 0:8], yT_s[:, kb * 128:(kb + 1) * 128],
                    wrT[:, kb * 8:(kb + 1) * 8], start=(kb == 0),
                    stop=(kb == 7)),
                    waits=(w0 if kb == 0 else ()),
                    inc=(bE.produced() if kb == 7 else None))
            wE = bE.consume_waits()
            p.emit("vector", lambda e: e.reduce_max(rsm[:, 0:1], bE.t[:, 0:8],
                                                    axis=AX.X), waits=wE)
            p.emit("vector", lambda e: e.tensor_scalar(rsm[:, 1:2],
                                                       rsm[:, 0:1], -1.0, None,
                                                       ALU.mult))
            mneg = p.mark("vector")
            p.emit("scalar", lambda e: e.activation(rsm[:, 8:16], bE.t[:, 0:8],
                                                    AF.Exp, bias=rsm[:, 1:2]),
                   waits=[mneg], inc=bE.released())
            msc = p.mark("scalar")
            p.emit("vector", lambda e: e.reduce_max(rsm[:, 2:3], rsm[:, 8:16],
                                                    axis=AX.X), waits=[msc])
            p.emit("vector", lambda e: e.tensor_single_scalar(
                zrow[:, 0:8], rsm[:, 8:16], rsm[:, 2:3], ALU.is_equal))
            p.emit("vector", lambda e: e.tensor_tensor(
                zrow[:, 8:16], zrow[:, 0:8], rsm[:, 8:16], ALU.mult))
            p.emit("vector", lambda e: e.tensor_tensor(
                zrow[:, 8:16], rsm[:, 8:16], zrow[:, 8:16], ALU.subtract))
            p.emit("vector", lambda e: e.reduce_max(rsm[:, 4:5], zrow[:, 8:16],
                                                    axis=AX.X))
            p.emit("vector", lambda e: e.tensor_single_scalar(
                zrow[:, 0:8], rsm[:, 8:16], rsm[:, 4:5], ALU.is_ge))
            p.emit("vector", lambda e: e.tensor_tensor(zrow[:, 0:8],
                                                       zrow[:, 0:8],
                                                       rsm[:, 8:16], ALU.mult))
            p.emit("vector", lambda e: e.reduce_sum(
                rsm[:, 5:6], zrow[:, 0:8].rearrange("p (a f) -> p a f", a=1),
                axis=AX.X))
            p.emit("vector", lambda e: e.reciprocal(rsm[:, 6:7], rsm[:, 5:6]))
            p.emit("vector", lambda e, gc=gc: e.tensor_scalar(
                cw_all[:, gc * 8:(gc + 1) * 8], zrow[:, 0:8], rsm[:, 6:7],
                None, ALU.mult))
            if gc == NCH - 1:
                tap("cwt", cw_all[:, 0:8 * NCH], [p.mark("vector")])
            if gc == 0:
                mrt = p.mark("vector")
                tap("rsm", rsm[:, 0:16], [mrt])
                tap("zrow", zrow[:, 0:16], [mrt])
        prev_marks[("x1g", buf)] = p.mark("tensor")
        p.barrier3()


def phase_D(nc, p, ar, BK, env):
    bA, bB, bC, bD_, bE, bF_ = (BK[k] for k in "ABCDEF")
    w_in, y_sp, yT_sp, out_d = env["w_in"], env["y_sp"], env["yT_sp"], env["out_d"]
    cw_all, tap = env["cw_all"], env["tap"]
    chYT, chYs, chWp = env["chYT"], env["chYs"], env["chWp"]
    p.barrier()
    ar.pop()
    ar.push()
    w1p = [ar.bf(8192), ar.bf(8192)]
    w2p = [ar.bf(8192), ar.bf(8192)]
    yTg = [ar.bf(2048) for _ in range(4)]
    hTg = [ar.bf(2048), ar.bf(2048)]
    accq = ar.f32(8192)
    yst = [ar.f32(D), ar.f32(D)]
    outt = ar.f32(D)
    xsq3 = ar.f32(D)
    stat3 = ar.f32(8)
    assert ar.off <= ar.cols, ar.off
    p.barrier()

    wprev = {}
    for q in range(4):
        p.emit("vector", lambda e: e.memset(accq[:, 0:4096], 0.0),
               waits=[wprev["acc"]] if "acc" in wprev else [])
        p.emit("vector", lambda e: e.memset(accq[:, 4096:8192], 0.0))
        macc0 = p.mark("vector")
        for ee in range(NE):
            for pan in range(4):
                pb = (ee * 4 + pan) % 2
                wwait = [wprev[("w", pb)]] if ("w", pb) in wprev else []
                src = w_in["W1e"][ee].rearrange(
                    "(kb p) (pan hh t) -> pan p kb hh t", p=128, pan=4, t=128)[pan]
                dst = w1p[pb].rearrange("p (kb hh t) -> p kb hh t", kb=8, t=128)
                p.emit("gpsimd", lambda e, src=src, dst=dst: e.dma_start(dst, src),
                       waits=wwait, inc=chWp[pb])
                src2 = w_in["W2e"][ee].rearrange(
                    "(pan hh p) f -> pan p hh f", p=128, pan=4)[pan]
                dst2 = w2p[pb].rearrange("p (hh f) -> p hh f", hh=8)
                p.emit("gpsimd", lambda e, src2=src2, dst2=dst2: e.dma_start(
                    dst2, src2), inc=chWp[pb])
                nw = (chWp[pb], chWp[pb].n)
                for grp in range(4):
                    gc = q * 8 + grp * 2
                    gb = grp
                    if ee == 0 and pan == 0:
                        ywait = [wprev[("y", gb)]] if ("y", gb) in wprev else []
                        for ci in range(2):
                            p.emit("sync", lambda e, gc=gc, ci=ci, gb=gb: e.dma_start(
                                yTg[gb].rearrange("p (kb c t) -> p kb c t",
                                                  kb=8, c=2)[:, :, ci, :],
                                yT_sp[gc + ci].rearrange("kb p t -> p kb t")),
                                waits=(ywait if ci == 0 else []), inc=chYT[gb])
                    ny = (chYT[gb], chYT[gb].n)
                    hb_ = hTg[grp % 2]
                    hwait = [wprev[("h", grp % 2)]] if ("h", grp % 2) in wprev else []
                    mgel = []
                    for hh in range(8):
                        bm = [bE, bF_][hh % 2]
                        w0 = bm.produce_waits() + [nw, ny] + (hwait if hh == 0 else [])
                        for kb in range(8):
                            p.emit("tensor", lambda e, kb=kb, hh=hh, bm=bm,
                                   pb=pb, gb=grp: e.matmul(
                                bm.t[:, 0:256],
                                w1p[pb][:, kb * 1024 + hh * 128:kb * 1024 + (hh + 1) * 128],
                                yTg[gb][:, kb * 256:(kb + 1) * 256],
                                start=(kb == 0), stop=(kb == 7)),
                                waits=(w0 if kb == 0 else ()),
                                inc=(bm.produced() if kb == 7 else None))
                        wm = bm.consume_waits()
                        p.emit("scalar", lambda e, hh=hh, bm=bm, hb_=hb_: e.activation(
                            hb_[:, hh * 256:(hh + 1) * 256], bm.t[:, 0:256],
                            AF.Gelu), waits=wm, inc=bm.released())
                        mgel.append(p.mark("scalar"))
                    for ci in range(2):
                        gcc = gc + ci
                        for fh in range(2):
                            bk_ = {(0, 0): bA, (0, 1): bB, (1, 0): bC,
                                   (1, 1): bD_}[(ci, fh)]
                            w0 = bk_.produce_waits() + [mgel[0]]
                            for hh in range(8):
                                p.emit("tensor", lambda e, ci=ci, fh=fh, hh=hh,
                                       bk_=bk_, pb=pb, hb_=hb_: e.matmul(
                                    bk_.t[:, 0:512],
                                    hb_[:, hh * 256 + ci * 128:hh * 256 + (ci + 1) * 128],
                                    w2p[pb][:, hh * 1024 + fh * 512:hh * 1024 + (fh + 1) * 512],
                                    start=(hh == 0), stop=(hh == 7)),
                                    waits=(w0 if hh == 0 else [mgel[hh]]),
                                    inc=(bk_.produced() if hh == 7 else None))
                            wk_ = bk_.consume_waits()
                            p.emit("vector", lambda e, ci=ci, fh=fh, gcc=gcc,
                                   ee=ee, grp=grp, bk_=bk_: e.scalar_tensor_tensor(
                                accq[:, (grp * 2 + ci) * 1024 + fh * 512:
                                     (grp * 2 + ci) * 1024 + (fh + 1) * 512],
                                bk_.t[:, 0:512],
                                cw_all[:, gcc * 8 + ee:gcc * 8 + ee + 1],
                                accq[:, (grp * 2 + ci) * 1024 + fh * 512:
                                     (grp * 2 + ci) * 1024 + (fh + 1) * 512],
                                ALU.mult, ALU.add),
                                waits=wk_ + ([macc0] if ee == 0 and pan == 0
                                             else []), inc=bk_.released())
                    wprev[("h", grp % 2)] = p.mark("tensor")
                wprev[("w", pb)] = p.mark("tensor")
        mq = p.mark("vector")
        if q == 0:
            tap("accq", accq[:, 0:8192], [mq])
        mqP = p.mark("tensor")
        for gb in range(4):
            wprev[("y", gb)] = mqP
        for ci8 in range(8):
            gc = q * 8 + ci8
            sb = ci8 % 2
            ywait = [wprev[("ys", sb)]] if ("ys", sb) in wprev else []
            p.emit("sync", lambda e, gc=gc, sb=sb: e.dma_start(yst[sb], y_sp[gc]),
                   waits=ywait + ([mq] if ci8 < 2 else []), inc=chYs[sb])
            nyl = (chYs[sb], chYs[sb].n)
            p.emit("vector", lambda e, ci8=ci8, sb=sb: e.tensor_tensor(
                outt, yst[sb], accq[:, ci8 * 1024:(ci8 + 1) * 1024], ALU.add),
                waits=[nyl, mq])
            mo = p.mark("vector")
            p.emit("scalar", lambda e: e.activation(xsq3, outt, AF.Square),
                   waits=[mo])
            mS = p.mark("scalar")
            p.emit("vector", lambda e: e.reduce_sum(
                stat3[:, 0:1], outt.rearrange("p (a f) -> p a f", a=1),
                axis=AX.X))
            p.emit("vector", lambda e: e.reduce_sum(
                stat3[:, 1:2], xsq3.rearrange("p (a f) -> p a f", a=1),
                axis=AX.X), waits=[mS])
            p.emit("vector", lambda e: e.tensor_scalar(
                stat3[:, 2:3], stat3[:, 0:1], 1.0 / D, None, ALU.mult))
            p.emit("vector", lambda e: e.tensor_scalar(
                stat3[:, 3:4], stat3[:, 1:2], 1.0 / D, None, ALU.mult))
            p.emit("vector", lambda e: e.tensor_tensor(
                stat3[:, 4:5], stat3[:, 2:3], stat3[:, 2:3], ALU.mult))
            p.emit("vector", lambda e: e.tensor_tensor(
                stat3[:, 5:6], stat3[:, 3:4], stat3[:, 4:5], ALU.subtract))
            mV = p.mark("vector")
            p.emit("scalar", lambda e: e.activation(stat3[:, 6:7], stat3[:, 5:6],
                                                    AF.Ln, bias=1e-5),
                   waits=[mV])
            p.emit("scalar", lambda e: e.activation(stat3[:, 7:8], stat3[:, 6:7],
                                                    AF.Exp, scale=-0.5))
            mR = p.mark("scalar")
            p.emit("vector", lambda e, sb=sb: e.tensor_scalar(
                yst[sb], outt, stat3[:, 2:3], stat3[:, 7:8], ALU.subtract,
                ALU.mult), waits=[mR])
            mfin = p.mark("vector")
            wprev[("ys", sb)] = mfin
            p.emit("sync", lambda e, gc=gc, sb=sb: e.dma_start(
                out_d[gc // 16, (gc % 16) * 128:(gc % 16) * 128 + 128, :],
                yst[sb]), waits=[mfin], inc=p.chD)
        wprev["acc"] = p.mark("vector")


_PROG_CACHE = {}


def kernel(**inputs):
    cstf_np, cstb_np = make_consts()
    x = np.ascontiguousarray(np.asarray(inputs["L0_attn_x"], np.float32))
    names = dict(
        Wq="L0_attn_Wq", Wk="L0_attn_Wk", Wv="L0_attn_Wv", Wb="L0_attn_Wb",
        Wg="L0_attn_Wg", Wo="L0_attn_Wo", bo="L0_attn_bo", ln1g="L0_attn_ln1_g",
        ln1b="L0_attn_ln1_b", ln2g="L0_attn_ln2_g", ln2b="L0_attn_ln2_b",
        w1="L0_attn_w1", b1="L0_attn_b1", w2="L0_attn_w2", b2="L0_attn_b2",
        Wr="L0_ffn_Wr", W1e="L0_ffn_W1", W2e="L0_ffn_W2")
    wmap = {k: np.ascontiguousarray(np.asarray(inputs[v], np.float32))
            for k, v in names.items()}
    key = "full"
    if key not in _PROG_CACHE:
        _PROG_CACHE[key] = build_program()
    nc = _PROG_CACHE[key]
    in_maps = []
    for core in range(8):
        m = {"x": x[2 * core:2 * core + 2], "cstf": cstf_np, "cstb": cstb_np}
        m.update(wmap)
        in_maps.append(m)
    res = run_bass_kernel_spmd(nc, in_maps, core_ids=list(range(8)))
    out = np.concatenate([r["out"] for r in res.results], axis=0)
    return out.reshape(16, 2048, D).astype(np.float32)
